# version 05
# baseline (speedup 1.0000x reference)
import numpy as np
import ml_dtypes
from contextlib import ExitStack
import concourse.bass as bass
import concourse.mybir as mybir
from concourse.bass_utils import run_bass_kernel_spmd

F32 = mybir.dt.float32
BF16 = mybir.dt.bfloat16
I32 = mybir.dt.int32
AF = mybir.ActivationFunctionType
ALU = mybir.AluOpType
AX = mybir.AxisListType.X

D = 2048
NST = 16
RDT = 64
HID = 5632
DC = D // 128
HM = HID // 128
NCORES = 8
EPS = 1e-5
T1 = 1024
T2 = 512
WCH = 4096


class Walk:

    def __init__(self, nc, sems):
        self.nc = nc
        self.sems = sems
        self.cnt = {k: 0 for k in sems}
        self.ops = {k: [] for k in ("sync", "tensor", "scalar", "vector", "gpsimd")}

    def _emit(self, stream, own, waits, fn, inc_amt=1, self_serialize=False):
        best = {}
        for item in waits:
            if item is None:
                continue
            s, v = item
            if v and v > 0 and v > best.get(s, 0):
                best[s] = v
        if self_serialize and own and self.cnt[own] > 0:
            if best.get(own, 0) < self.cnt[own]:
                best[own] = self.cnt[own]
        clean = sorted(best.items())
        sems = self.sems
        own_sem = sems[own] if own else None

        def run(e, waits=tuple(clean), fn=fn, own_sem=own_sem, inc_amt=inc_amt):
            for s, v in waits:
                e.wait_ge(sems[s], v)
            ins = fn(e)
            if own_sem is not None:
                ins.then_inc(own_sem, inc_amt)

        self.ops[stream].append(run)
        if own:
            self.cnt[own] += inc_amt
            return self.cnt[own]
        return None

    def dma(self, q, out, in_, waits=()):
        return self._emit("sync", q, waits,
                          lambda e: e.dma_start(out=out, in_=in_), 16)

    def pe(self, fn, waits=()):
        return self._emit("tensor", "pe", waits, fn)

    def act(self, fn, waits=()):
        return self._emit("scalar", "act", waits, fn, self_serialize=True)

    def dve(self, fn, waits=()):
        return self._emit("vector", "dve", waits, fn, self_serialize=True)

    def pool(self, fn, waits=()):
        return self._emit("gpsimd", "pool", waits, fn, self_serialize=True)

    def cc(self, fn, waits=()):
        return self._emit("gpsimd", "cc", waits, fn, 1)

    def barrier(self, stream, waits):
        clean = tuple((s, v) for (s, v) in waits if v and v > 0)
        sems = self.sems

        def run(e, waits=clean):
            for s, v in waits:
                e.wait_ge(sems[s], v)

        self.ops[stream].append(run)


def ep_wait(eng, val):
    return (eng, val)


def build_nc(l_core):
    nt1 = l_core // T1
    nt2 = l_core // T2
    nc = bass.Bass("TRN2", target_bir_lowering=False)

    x_T = nc.dram_tensor("x_T", [D, l_core], F32, kind="ExternalInput").ap()
    w_inT = nc.dram_tensor("w_inT", [D, 96], F32, kind="ExternalInput").ap()
    w_dtT = nc.dram_tensor("w_dtT", [RDT, D], F32, kind="ExternalInput").ap()
    b_dt_t = nc.dram_tensor("b_dt_t", [128, DC], F32, kind="ExternalInput").ap()
    a_t = nc.dram_tensor("a_t", [128, DC * NST], F32, kind="ExternalInput").ap()
    d_t = nc.dram_tensor("d_t", [128, DC], F32, kind="ExternalInput").ap()
    s_g = nc.dram_tensor("s_g", [128, HM], F32, kind="ExternalInput").ap()
    s_u = nc.dram_tensor("s_u", [128, HM], F32, kind="ExternalInput").ap()
    s_d = nc.dram_tensor("s_d", [128, DC], F32, kind="ExternalInput").ap()
    wq_g = nc.dram_tensor("wq_g", [128, HM * DC * 128], I32, kind="ExternalInput").ap()
    wq_u = nc.dram_tensor("wq_u", [128, HM * DC * 128], I32, kind="ExternalInput").ap()
    wq_d = nc.dram_tensor("wq_d", [128, DC * HM * 128], I32, kind="ExternalInput").ap()
    msk_in = nc.dram_tensor("msk", [128, NCORES], F32, kind="ExternalInput").ap()
    imsk_in = nc.dram_tensor("imsk", [128, NCORES], F32, kind="ExternalInput").ap()
    sel_in = nc.dram_tensor("sel", [32, 2 * NST * 128], BF16, kind="ExternalInput").ap()
    onescol_in = nc.dram_tensor("onescol", [128, 1], BF16, kind="ExternalInput").ap()
    onesrow_in = nc.dram_tensor("onesrow", [1, 128], F32, kind="ExternalInput").ap()

    out_T = nc.dram_tensor("out_T", [D, l_core], F32, kind="ExternalOutput").ap()

    dstash = nc.dram_tensor("dstash", [D, l_core], F32)
    bcstash = nc.dram_tensor("bcstash", [32, l_core], BF16)
    wbf_g = nc.dram_tensor("wbf_g", [128, HM * DC * 128], BF16)
    wbf_u = nc.dram_tensor("wbf_u", [128, HM * DC * 128], BF16)
    wbf_d = nc.dram_tensor("wbf_d", [128, DC * HM * 128], BF16)
    gp_src = nc.dram_tensor("gp_src", [256, DC * NST], F32)
    gp_dst = nc.dram_tensor("gp_dst", [256 * NCORES, DC * NST], F32)

    stack = ExitStack()
    sems = {}
    dma_sem_names = (["dS", "dT", "dV", "dP"]
                     + [f"d{k}{i}" for k in "CIOXDABEJ" for i in range(2)]
                     + ["dZ0", "dZ1"])
    for name in dma_sem_names + ["pe", "act", "dve", "pool", "cc"]:
        sems[name] = stack.enter_context(nc.semaphore(name))
    w = Walk(nc, sems)

    E = stack.enter_context
    sb_bdt = E(nc.sbuf_tensor("sb_bdt", [128, DC], F32))
    sb_Apos = E(nc.sbuf_tensor("sb_Apos", [128, DC * NST], F32))
    sb_Aneg = E(nc.sbuf_tensor("sb_Aneg", [128, DC * NST], F32))
    sb_Dt = E(nc.sbuf_tensor("sb_Dt", [128, DC], F32))
    sb_D1 = E(nc.sbuf_tensor("sb_D1", [128, DC], F32))
    sb_sg = E(nc.sbuf_tensor("sb_sg", [128, HM], F32))
    sb_su = E(nc.sbuf_tensor("sb_su", [128, HM], F32))
    sb_sd = E(nc.sbuf_tensor("sb_sd", [128, DC], F32))
    sb_msk = E(nc.sbuf_tensor("sb_msk", [128, NCORES], F32))
    sb_imsk = E(nc.sbuf_tensor("sb_imsk", [128, NCORES], F32))
    sb_sel = E(nc.sbuf_tensor("sb_sel", [32, 2 * NST * 128], BF16))
    sb_onescol = E(nc.sbuf_tensor("sb_onescol", [128, 1], BF16))
    sb_onesrow = E(nc.sbuf_tensor("sb_onesrow", [1, 128], F32))
    sb_state = E(nc.sbuf_tensor("sb_state", [128, DC * NST], F32))

    ps8 = E(nc.psum_tensor("ps8", [128, 4096], F32))

    def bank(i, pn=128, c0=0, cn=512):
        return ps8[0:pn, i * 512 + c0:i * 512 + c0 + cn]

    def sdma(out, in_):
        return w.dma("dS", out, in_, waits=[("dS", w.cnt["dS"])])

    def dma_all():
        return [(n, w.cnt[n]) for n in dma_sem_names]

    ld = {}
    ld["bdt"] = sdma(sb_bdt[:], b_dt_t)
    ld["alog"] = sdma(sb_Apos[:], a_t)
    ld["dt"] = sdma(sb_Dt[:], d_t)
    ld["sg"] = sdma(sb_sg[:], s_g)
    ld["su"] = sdma(sb_su[:], s_u)
    ld["sd"] = sdma(sb_sd[:], s_d)
    ld["msk"] = sdma(sb_msk[:], msk_in)
    ld["imsk"] = sdma(sb_imsk[:], imsk_in)
    ld["sel"] = sdma(sb_sel[:], sel_in)
    ld["onescol"] = sdma(sb_onescol[:], onescol_in)
    ld["onesrow"] = sdma(sb_onesrow[:], onesrow_in)

    v_apos = w.act(lambda e: e.activation(out=sb_Apos[:], in_=sb_Apos[:], func=AF.Exp),
                   waits=[("dS", ld["alog"])])
    v_aneg = w.pool(lambda e: e.tensor_scalar(out=sb_Aneg[:], in0=sb_Apos[:],
                                              scalar1=-1.0, scalar2=None, op0=ALU.mult),
                    waits=[("act", v_apos)])
    v_d1 = w.pool(lambda e: e.tensor_scalar(out=sb_D1[:], in0=sb_Dt[:],
                                            scalar1=1.0, scalar2=None, op0=ALU.add),
                  waits=[("dS", ld["dt"])])
    v_state0 = w.dve(lambda e: e.memset(sb_state[:], 0.0))

    conv_stack = ExitStack()
    cv_i = [conv_stack.enter_context(nc.sbuf_tensor(f"cv_i{i}", [128, WCH], I32)) for i in range(2)]
    cv_b = [conv_stack.enter_context(nc.sbuf_tensor(f"cv_b{i}", [128, WCH], BF16)) for i in range(2)]
    nch = HM * DC * 128 // WCH
    cv_in_done = [0, 0]
    cv_ts_done = [0, 0]
    cv_out_done = [0, 0]
    ci = 0
    for src, dst in ((wq_g, wbf_g), (wq_u, wbf_u), (wq_d, wbf_d)):
        for ch in range(nch):
            s = ci % 2
            cv_in_done[s] = w.dma(f"dC{s}", cv_i[s][:], src[:, ch * WCH:(ch + 1) * WCH],
                                  waits=[("dve", cv_ts_done[s])])
            cv_ts_done[s] = w.dve(
                lambda e, s=s: e.tensor_scalar(out=cv_b[s][:], in0=cv_i[s][:],
                                               scalar1=128.0, scalar2=None,
                                               op0=ALU.subtract),
                waits=[(f"dC{s}", cv_in_done[s]), (f"dI{s}", cv_out_done[s])])
            cv_out_done[s] = w.dma(f"dI{s}", dst[:, ch * WCH:(ch + 1) * WCH], cv_b[s][:],
                                   waits=[("dve", cv_ts_done[s])])
            ci += 1
    conv_done_dve = w.cnt["dve"]
    conv_stack.close()

    w.barrier("sync", [("dve", conv_done_dve)] + dma_all())

    r1_stack = ExitStack()
    R = r1_stack.enter_context
    sb_winT = R(nc.sbuf_tensor("sb_winT", [128, DC * 96], F32))
    sb_wdtT = R(nc.sbuf_tensor("sb_wdtT", [RDT, D], F32))
    sb_G = R(nc.sbuf_tensor("sb_G", [128, DC * NST], F32))
    sb_P = R(nc.sbuf_tensor("sb_P", [128, DC * NST], F32))
    sb_hin = R(nc.sbuf_tensor("sb_hin", [128, DC * NST], F32))
    sb_ht = R(nc.sbuf_tensor("sb_ht", [128, DC * NST], F32))
    sb_gm = R(nc.sbuf_tensor("sb_gm", [128, DC * NST], F32))
    sb_pm = R(nc.sbuf_tensor("sb_pm", [128, DC * NST], F32))
    sb_Gj = [R(nc.sbuf_tensor(f"sb_Gj{i}", [128, DC * NST], F32)) for i in range(2)]
    sb_Pj = [R(nc.sbuf_tensor(f"sb_Pj{i}", [128, DC * NST], F32)) for i in range(2)]
    sb_sumdall = R(nc.sbuf_tensor("sb_sumdall", [128, DC * 16], F32))
    sb_sumdtot = R(nc.sbuf_tensor("sb_sumdtot", [128, DC], F32))
    sb_d1l = R(nc.sbuf_tensor("sb_d1l", [128, DC * T1], F32))
    sb_u1l = R(nc.sbuf_tensor("sb_u1l", [128, DC * T1], F32))
    sb_xs = [R(nc.sbuf_tensor(f"sb_xs{i}", [128, T1], F32)) for i in range(2)]
    sb_dlr = R(nc.sbuf_tensor("sb_dlr", [RDT, T1], F32))
    sb_bcb1 = R(nc.sbuf_tensor("sb_bcb1", [32, T1], BF16))
    sb_rep1 = [R(nc.sbuf_tensor(f"sb_rep1_{i}", [128, T1], BF16)) for i in range(2)]
    sb_sa = [R(nc.sbuf_tensor(f"sb_sa{i}", [128, T1], BF16)) for i in range(2)]
    sb_sb = [R(nc.sbuf_tensor(f"sb_sb{i}", [128, T1], BF16)) for i in range(2)]
    sb_sh = R(nc.sbuf_tensor("sb_sh", [128, T1], F32))
    sb_S = R(nc.sbuf_tensor("sb_S", [128, T1], F32))
    sb_ones1 = R(nc.sbuf_tensor("sb_ones1", [128, T1], BF16))
    sb_Gparts = R(nc.sbuf_tensor("sb_Gparts", [128, 4 * DC * NST], F32))
    sb_scar = R(nc.sbuf_tensor("sb_scar", [128, DC], F32))
    sb_suf = R(nc.sbuf_tensor("sb_suf", [128, DC], F32))
    sb_wt = R(nc.sbuf_tensor("sb_wt", [128, DC * NST], F32))

    for dc in range(DC):
        ld["winT"] = sdma(sb_winT[:, dc * 96:(dc + 1) * 96],
                          w_inT[dc * 128:(dc + 1) * 128, :])
    ld["wdtT"] = sdma(sb_wdtT[:], w_dtT)

    xs_free = [[], []]
    a_free = [0, 0]
    b_free = [0, 0]
    h_free = [0, 0]
    rep1_free = [0, 0]
    bank_free = [[] for _ in range(8)]
    v_softadd = 0
    ev_bc_last = 0
    dlr_free_pe = 0
    bcb1_free_pe = 0
    stash_last = 0

    for t in range(nt1):
        c0 = t * T1
        lr_mm = [0, 0]
        bc_mm = [0, 0]
        for dc in range(DC):
            s = dc % 2
            xr = w.dma(f"dX{s}", sb_xs[s][:], x_T[dc * 128:(dc + 1) * 128, c0:c0 + T1],
                       waits=xs_free[s])
            for cc in range(2):
                first = dc == 0
                last = dc == DC - 1
                wl = [(f"dX{s}", xr)]
                if first:
                    wl += [("dS", ld["winT"])] + bank_free[cc] + bank_free[2 + cc]
                lr_mm[cc] = w.pe(
                    lambda e, s=s, cc=cc, dc=dc, first=first, last=last: e.matmul(
                        bank(cc, RDT), sb_winT[:, dc * 96:dc * 96 + RDT],
                        sb_xs[s][:, cc * 512:(cc + 1) * 512],
                        start=first, stop=last, skip_group_check=True),
                    waits=wl)
                bc_mm[cc] = w.pe(
                    lambda e, s=s, cc=cc, dc=dc, first=first, last=last: e.matmul(
                        bank(2 + cc, 32), sb_winT[:, dc * 96 + RDT:(dc + 1) * 96],
                        sb_xs[s][:, cc * 512:(cc + 1) * 512],
                        start=first, stop=last, skip_group_check=True),
                    waits=[])
            xs_free[s] = [("pe", bc_mm[1])]
        ev_lr = [0, 0]
        ev_bc = [0, 0]
        for cc in range(2):
            ev_lr[cc] = w.act(
                lambda e, cc=cc: e.copy(out=sb_dlr[:, cc * 512:(cc + 1) * 512],
                                        in_=bank(cc, RDT)),
                waits=[("pe", lr_mm[cc]), ("pe", dlr_free_pe)])
            ev_bc[cc] = w.act(
                lambda e, cc=cc: e.copy(out=sb_bcb1[:, cc * 512:(cc + 1) * 512],
                                        in_=bank(2 + cc, 32)),
                waits=[("pe", bc_mm[cc]), ("pe", bcb1_free_pe)])
        bank_free[0] = [("act", ev_lr[0])]
        bank_free[1] = [("act", ev_lr[1])]
        bank_free[2] = [("act", ev_bc[0])]
        bank_free[3] = [("act", ev_bc[1])]
        ev_bc_last = max(ev_bc)
        w.dma("dT", bcstash[:, c0:c0 + T1], sb_bcb1[:],
              waits=[("act", ev_bc_last), ("dT", w.cnt["dT"])])

        for dc in range(DC):
            for cc in range(2):
                bki = 4 + cc
                pcol = dc * T1 + cc * 512
                mmv = w.pe(
                    lambda e, dc=dc, cc=cc, bki=bki: e.matmul(
                        bank(bki), sb_wdtT[:, dc * 128:(dc + 1) * 128],
                        sb_dlr[:, cc * 512:(cc + 1) * 512],
                        start=True, stop=True),
                    waits=bank_free[bki] + [("act", max(ev_lr)),
                                            ("dS", ld["wdtT"] if t == 0 and dc == 0 else 0)])
                dlr_free_pe = mmv
                v1 = w.act(lambda e, dc=dc, bki=bki: e.activation(
                    out=sb_S[:, 0:512], in_=bank(bki), func=AF.Abs,
                    bias=sb_bdt[:, dc:dc + 1]),
                    waits=[("pe", mmv), ("dve", max(a_free[0], a_free[1])),
                           ("dS", ld["bdt"] if t == 0 and dc == 0 else 0),
                           ("dT", stash_last if dc == 0 and cc == 0 else 0)])
                v2 = w.act(lambda e: e.activation(
                    out=sb_sh[:, 0:512], in_=sb_S[:, 0:512], func=AF.Exp,
                    scale=-1.0),
                    waits=[("act", v1)])
                v3 = w.act(lambda e: e.activation(
                    out=sb_S[:, 512:1024], in_=sb_sh[:, 0:512], func=AF.Ln,
                    bias=1.0),
                    waits=[("act", v2), ("pool", v_softadd)])
                v4 = w.act(lambda e, dc=dc, bki=bki: e.activation(
                    out=sb_sh[:, 512:1024], in_=bank(bki), func=AF.Relu,
                    bias=sb_bdt[:, dc:dc + 1]),
                    waits=[("act", v3)])
                bank_free[bki] = [("act", v4)]
                v_softadd = w.pool(lambda e, pcol=pcol: e.tensor_tensor(
                    out=sb_d1l[:, pcol:pcol + 512], in0=sb_S[:, 512:1024],
                    in1=sb_sh[:, 512:1024], op=ALU.add),
                    waits=[("act", v4)])
        delta_done = v_softadd

        for dc in range(DC):
            w.dve(lambda e, dc=dc, t=t: e.reduce_sum(
                out=sb_sumdall[:, dc * 16 + t:dc * 16 + t + 1],
                in_=sb_d1l[:, dc * T1:(dc + 1) * T1], axis=AX),
                waits=[("pool", delta_done)])

        u_done = 0
        for dc in range(DC):
            s = dc % 2
            stash_last = w.dma("dT", dstash[dc * 128:(dc + 1) * 128, c0:c0 + T1],
                               sb_d1l[:, dc * T1:(dc + 1) * T1],
                               waits=[("pool", delta_done), ("dT", w.cnt["dT"])])
            xr = w.dma(f"dX{s}", sb_xs[s][:], x_T[dc * 128:(dc + 1) * 128, c0:c0 + T1],
                       waits=xs_free[s])
            u_done = w.pool(lambda e, dc=dc, s=s: e.tensor_tensor(
                out=sb_u1l[:, dc * T1:(dc + 1) * T1], in0=sb_d1l[:, dc * T1:(dc + 1) * T1],
                in1=sb_xs[s][:], op=ALU.mult),
                waits=[(f"dX{s}", xr), ("pool", delta_done)])
            xs_free[s] = [("pool", u_done)]

        if t == 0:
            v_ones = w.dve(lambda e: e.memset(sb_ones1[:], 1.0), waits=[])
            v_scar = w.dve(lambda e: e.memset(sb_scar[:], 0.0), waits=[])
        r_done = 0
        for dc in range(DC):
            vS = w.dve(lambda e, dc=dc: e.tensor_tensor_scan(
                out=sb_S[:], data0=sb_ones1[:], data1=sb_d1l[:, dc * T1:(dc + 1) * T1],
                initial=sb_scar[:, dc:dc + 1], op0=ALU.mult, op1=ALU.add),
                waits=[("pool", u_done), (f"dT", stash_last)])
            vC = w.dve(lambda e, dc=dc: e.tensor_copy(
                out=sb_scar[:, dc:dc + 1], in_=sb_S[:, T1 - 1:T1]),
                waits=[("dve", vS)])
            r_done = w.dve(lambda e, dc=dc: e.tensor_scalar(
                out=sb_d1l[:, dc * T1:(dc + 1) * T1], in0=sb_S[:],
                scalar1=-1.0, scalar2=sb_scar[:, dc:dc + 1],
                op0=ALU.mult, op1=ALU.add),
                waits=[("dve", vC)])

        for n in range(NST):
            sn = n % 2
            bks = 0 if sn == 0 else 2
            mm1 = w.pe(lambda e, n=n, bks=bks: e.matmul(
                bank(bks), sb_sel[:, n * 128:(n + 1) * 128], sb_bcb1[:, 0:512],
                start=True, stop=True),
                waits=bank_free[bks] + bank_free[bks + 1] +
                      [("act", ev_bc_last), ("dS", ld["sel"] if t == 0 and n == 0 else 0)])
            mm2 = w.pe(lambda e, n=n, bks=bks: e.matmul(
                bank(bks + 1), sb_sel[:, n * 128:(n + 1) * 128], sb_bcb1[:, 512:1024],
                start=True, stop=True),
                waits=[])
            bcb1_free_pe = mm2
            evr = w.act(lambda e, sn=sn, bks=bks: e.copy(
                out=sb_rep1[sn][:], in_=bank(bks, 128, 0, 1024)),
                waits=[("pe", mm2), ("pool", rep1_free[sn])])
            bank_free[bks] = [("act", evr)]
            bank_free[bks + 1] = [("act", evr)]
            for dc in range(DC):
                col = dc * NST + n
                sa = dc % 2
                va = w.act(lambda e, dc=dc, col=col, sa=sa: e.activation(
                    out=sb_sa[sa][:], in_=sb_d1l[:, dc * T1:(dc + 1) * T1], func=AF.Exp,
                    scale=sb_Aneg[:, col:col + 1]),
                    waits=[("dve", max(r_done, a_free[sa])),
                           ("pool", v_aneg if t == 0 and n == 0 and dc == 0 else 0)])
                vb = w.pool(lambda e, dc=dc, sa=sa, sn=sn: e.tensor_tensor(
                    out=sb_sb[sa][:], in0=sb_u1l[:, dc * T1:(dc + 1) * T1],
                    in1=sb_rep1[sn][:], op=ALU.mult),
                    waits=[("pool", u_done), ("dve", b_free[sa]), ("act", evr)])
                vg = w.dve(lambda e, sa=sa, col=col, t=t: e.scalar_tensor_tensor(
                    out=sb_sh[:], in0=sb_sa[sa][:], scalar=1.0, in1=sb_sb[sa][:],
                    op0=ALU.bypass, op1=ALU.mult,
                    accum_out=sb_Gparts[:, t * DC * NST + col:t * DC * NST + col + 1]),
                    waits=[("act", va), ("pool", vb)])
                a_free[sa] = vg
                b_free[sa] = vg
            rep1_free[sn] = w.cnt["pool"]

    r1_dve = w.cnt["dve"]
    r1_pool = w.cnt["pool"]
    r1_act = w.cnt["act"]
    r1_pe = w.cnt["pe"]

    vG = w.dve(lambda e: e.memset(sb_G[:], 0.0), waits=[("dve", r1_dve)])
    for t in range(nt1):
        for dc in range(DC):
            if t == nt1 - 1:
                vsf = w.dve(lambda e, dc=dc: e.memset(sb_suf[:, dc:dc + 1], 0.0),
                            waits=[])
            else:
                vsf = w.dve(lambda e, dc=dc, t=t: e.reduce_sum(
                    out=sb_suf[:, dc:dc + 1],
                    in_=sb_sumdall[:, dc * 16 + t + 1:dc * 16 + nt1], axis=AX),
                    waits=[("dve", r1_dve)])
            w.act(lambda e, dc=dc: e.activation(
                out=sb_wt[:, dc * NST:(dc + 1) * NST],
                in_=sb_Aneg[:, dc * NST:(dc + 1) * NST], func=AF.Exp,
                scale=sb_suf[:, dc:dc + 1]),
                waits=[("dve", vsf)])
        va_wt = w.cnt["act"]
        vmul = w.dve(lambda e, t=t: e.tensor_tensor(
            out=sb_ht[:], in0=sb_Gparts[:, t * DC * NST:(t + 1) * DC * NST],
            in1=sb_wt[:], op=ALU.mult),
            waits=[("act", va_wt), ("dve", vG)])
        vG = w.dve(lambda e: e.tensor_tensor(
            out=sb_gm[:], in0=sb_G[:], in1=sb_ht[:], op=ALU.add),
            waits=[("dve", vmul)])
        vG = w.dve(lambda e: e.tensor_copy(out=sb_G[:], in_=sb_gm[:]),
                   waits=[("dve", vG)])
    vP = 0
    for dc in range(DC):
        vsd = w.dve(lambda e, dc=dc: e.reduce_sum(
            out=sb_sumdtot[:, dc:dc + 1],
            in_=sb_sumdall[:, dc * 16:dc * 16 + nt1], axis=AX),
            waits=[("dve", r1_dve)])
        vP = w.act(lambda e, dc=dc: e.activation(
            out=sb_P[:, dc * NST:(dc + 1) * NST],
            in_=sb_Aneg[:, dc * NST:(dc + 1) * NST], func=AF.Exp,
            scale=sb_sumdtot[:, dc:dc + 1]),
            waits=[("dve", vsd)])

    d_g = w.dma("dP", gp_src[0:128, :], sb_G[:], waits=[("dve", vG)])
    d_p = w.dma("dP", gp_src[128:256, :], sb_P[:],
                waits=[("act", vP), ("dP", d_g)])
    cc_done = w.cc(lambda e: e.collective_compute(
        "AllGather", ALU.bypass, replica_groups=[list(range(NCORES))],
        ins=[gp_src.ap().opt()], outs=[gp_dst.ap().opt()]),
        waits=[("dP", d_p), ("pool", r1_pool)])
    vH = w.dve(lambda e: e.memset(sb_hin[:], 0.0), waits=[("dve", vG)])
    gp_slot_free = [0, 0]
    for j in range(NCORES):
        s = j % 2
        gj = w.dma(f"dJ{s}", sb_Gj[s][:], gp_dst[j * 256:j * 256 + 128, :],
                   waits=[("cc", cc_done), ("dve", gp_slot_free[s]),
                          (f"dJ{s}", w.cnt[f"dJ{s}"])])
        pj = w.dma(f"dJ{s}", sb_Pj[s][:], gp_dst[j * 256 + 128:(j + 1) * 256, :],
                   waits=[(f"dJ{s}", gj)])
        v1 = w.dve(lambda e, s=s, j=j: e.tensor_scalar(
            out=sb_gm[:], in0=sb_Gj[s][:], scalar1=sb_msk[:, j:j + 1], scalar2=None,
            op0=ALU.mult),
            waits=[(f"dJ{s}", pj), ("dve", vH)])
        v2 = w.dve(lambda e, s=s, j=j: e.tensor_scalar(
            out=sb_pm[:], in0=sb_Pj[s][:], scalar1=sb_msk[:, j:j + 1],
            scalar2=sb_imsk[:, j:j + 1], op0=ALU.mult, op1=ALU.add),
            waits=[("dve", v1)])
        gp_slot_free[s] = v2
        v3 = w.dve(lambda e: e.tensor_tensor(
            out=sb_ht[:], in0=sb_hin[:], in1=sb_pm[:], op=ALU.mult),
            waits=[("dve", v2)])
        vH = w.dve(lambda e: e.tensor_tensor(
            out=sb_hin[:], in0=sb_ht[:], in1=sb_gm[:], op=ALU.add),
            waits=[("dve", v3)])
    v_state_r2 = w.dve(lambda e: e.tensor_copy(out=sb_state[:], in_=sb_hin[:]),
                       waits=[("dve", vH)])

    r1_stack.close()

    r2_stack = ExitStack()
    R = r2_stack.enter_context
    NG, GSZ = 8, 2
    sb_repB = R(nc.sbuf_tensor("sb_repB", [128, GSZ * T2], BF16))
    sb_repC = R(nc.sbuf_tensor("sb_repC", [128, GSZ * T2], BF16))
    sb_zz = [R(nc.sbuf_tensor(f"sb_zz{i}", [128, DC * T2], F32)) for i in range(2)]
    sb_x1bf = R(nc.sbuf_tensor("sb_x1bf", [128, DC * T2], BF16))
    sb_gu = R(nc.sbuf_tensor("sb_gu", [128, HM * T2], BF16))
    sb_bcb2 = R(nc.sbuf_tensor("sb_bcb2", [32, T2], BF16))
    sb_dl2 = [R(nc.sbuf_tensor(f"sb_dl2_{i}", [128, T2], F32)) for i in range(2)]
    sb_x2 = [R(nc.sbuf_tensor(f"sb_x2_{i}", [128, T2], F32)) for i in range(2)]
    sb_u2 = R(nc.sbuf_tensor("sb_u2", [128, 2 * T2], F32))
    sb_a2 = [R(nc.sbuf_tensor(f"sb_a2_{i}", [128, T2], F32)) for i in range(2)]
    sb_b2 = [R(nc.sbuf_tensor(f"sb_b2_{i}", [128, T2], BF16)) for i in range(2)]
    sb_h2 = [R(nc.sbuf_tensor(f"sb_h2_{i}", [128, T2], BF16)) for i in range(2)]
    sb_p2 = [R(nc.sbuf_tensor(f"sb_p2_{i}", [128, T2], BF16)) for i in range(2)]
    sb_zacc = [R(nc.sbuf_tensor(f"sb_zacc{i}", [128, T2], F32)) for i in range(4)]
    sb_zb = [R(nc.sbuf_tensor(f"sb_zb{i}", [128, T2], BF16)) for i in range(1)]
    sb_z2b = [R(nc.sbuf_tensor(f"sb_z2b{i}", [128, T2], BF16)) for i in range(1)]
    sb_rows = R(nc.sbuf_tensor("sb_rows", [128, 3 * T2], F32))
    sb_wg = [R(nc.sbuf_tensor(f"sb_wg{i}", [128, 8 * 128], BF16)) for i in range(2)]
    sb_wu = [R(nc.sbuf_tensor(f"sb_wu{i}", [128, 8 * 128], BF16)) for i in range(2)]
    sb_wd = [R(nc.sbuf_tensor(f"sb_wd{i}", [128, 22 * 128], BF16)) for i in range(2)]
    sb_gt = [R(nc.sbuf_tensor(f"sb_gt{i}", [128, T2], BF16)) for i in range(1)]
    sb_ut = [R(nc.sbuf_tensor(f"sb_ut{i}", [128, T2], BF16)) for i in range(1)]
    sb_xo = [R(nc.sbuf_tensor(f"sb_xo{i}", [128, T2], F32)) for i in range(3)]

    def rr(base, slot):
        return sb_rows[base:base + 1, slot * T2:(slot + 1) * T2]

    r1_cut = [("dve", v_state_r2), ("pool", r1_pool), ("act", max(r1_act, vP)),
              ("pe", r1_pe)]
    for st_name in ("vector", "scalar", "gpsimd", "tensor"):
        w.barrier(st_name, r1_cut)
    w.barrier("sync", r1_cut + dma_all())

    bank_free = [[] for _ in range(8)]
    dl2_free = [[], []]
    x2_free = [[], []]
    a2_free = [0, 0]
    b2_free = [0, 0]
    h2_free = [("dve", 0), ("dve", 0)]
    p2_free = [("dve", 0), ("dve", 0)]
    zb_free = [0, 0]
    gt_free = [0, 0]
    wg_free = [0, 0]
    wu_free = [0, 0]
    wd_free = [0, 0]
    xo_free = [[], [], []]
    zz_free = [[], []]
    repB_free = 0
    repC_free = []
    rep_mm_last = 0
    rows_free = []
    x1bf_free = []
    gu_free = []
    H = {}

    def emit_scanA(t):
        nonlocal repB_free, repC_free
        c0 = t * T2
        zb_ = t % 2
        sbz = sb_zz[zb_]
        global_state = {}
        bc_ready = w.dma("dV", sb_bcb2[:], bcstash[:, c0:c0 + T2],
                         waits=[("pe", rep_mm_last), ("dV", w.cnt["dV"])])
        vz_carry = [None] * DC
        for g in range(NG):
            n0 = g * GSZ
            repB_ready = 0
            repC_ready = 0
            for ni in range(GSZ):
                n = n0 + ni
                bki = ni % 2
                mmB = w.pe(lambda e, n=n, bki=bki: e.matmul(
                    bank(bki), sb_sel[:, n * 128:(n + 1) * 128], sb_bcb2[:],
                    start=True, stop=True),
                    waits=bank_free[bki] + [("dV", bc_ready)])
                repB_ready = w.act(lambda e, ni=ni, bki=bki: e.copy(
                    out=sb_repB[:, ni * T2:(ni + 1) * T2], in_=bank(bki)),
                    waits=[("pe", mmB), ("pool", repB_free)])
                bank_free[bki] = [("act", repB_ready)]
                mmC = w.pe(lambda e, n=n, bki=bki: e.matmul(
                    bank(bki), sb_sel[:, (NST + n) * 128:(NST + n + 1) * 128], sb_bcb2[:],
                    start=True, stop=True),
                    waits=bank_free[bki])
                mmlast = mmC
                repC_ready = w.act(lambda e, ni=ni, bki=bki: e.copy(
                    out=sb_repC[:, ni * T2:(ni + 1) * T2], in_=bank(bki)),
                    waits=[("pe", mmC)] + repC_free)
                bank_free[bki] = [("act", repC_ready)]
            global_state["rep_mm_last"] = mmlast
            for dcp in range(0, DC, 2):
                lane_state = []
                for lane in range(2):
                    dc = dcp + lane
                    s = lane
                    ez = "pool" if lane == 0 else "dve"
                    eop = w.pool if lane == 0 else w.dve
                    d_rdy = w.dma(f"dD{s}", sb_dl2[s][:],
                                  dstash[dc * 128:(dc + 1) * 128, c0:c0 + T2],
                                  waits=dl2_free[s])
                    x_rdy = w.dma(f"dX{s}", sb_x2[s][:],
                                  x_T[dc * 128:(dc + 1) * 128, c0:c0 + T2],
                                  waits=x2_free[s])
                    vu = w.pool(lambda e, s=s: e.tensor_tensor(
                        out=sb_u2[:, s * T2:(s + 1) * T2], in0=sb_dl2[s][:],
                        in1=sb_x2[s][:], op=ALU.mult),
                        waits=[(f"dD{s}", d_rdy), (f"dX{s}", x_rdy),
                               ("pool", v_d1 if t == 0 and dc == 0 else 0)])
                    if g == 0:
                        zsl = 0 if lane == 0 else 2
                        vy = eop(lambda e, s=s, dc=dc, zsl=zsl: e.tensor_scalar(
                            out=sb_zacc[zsl][:], in0=sb_x2[s][:],
                            scalar1=sb_D1[:, dc:dc + 1],
                            scalar2=None, op0=ALU.mult),
                            waits=[("pool", vu)])
                        vz = (ez, vy)
                    else:
                        vz = vz_carry[dc]
                    x2_free[s] = [("pool", vu)] + ([vz] if g == 0 else [])
                    lane_state.append({"dc": dc, "s": s, "d_rdy": d_rdy, "vu": vu,
                                       "vz": vz, "act_last": 0})
                for ni in range(GSZ):
                    n = n0 + ni
                    for lane in range(2):
                        L = lane_state[lane]
                        dc, s = L["dc"], L["s"]
                        col = dc * NST + n
                        va = w.act(lambda e, s=s, col=col: e.activation(
                            out=sb_a2[s][:], in_=sb_dl2[s][:], func=AF.Exp,
                            scale=sb_Aneg[:, col:col + 1]),
                            waits=[(f"dD{s}", L["d_rdy"]), ("dve", a2_free[s])])
                        L["act_last"] = va
                        L["va"] = va
                    for lane in range(2):
                        L = lane_state[lane]
                        s = L["s"]
                        L["vb"] = w.pool(lambda e, s=s, ni=ni: e.tensor_tensor(
                            out=sb_b2[s][:], in0=sb_u2[:, s * T2:(s + 1) * T2],
                            in1=sb_repB[:, ni * T2:(ni + 1) * T2], op=ALU.mult),
                            waits=[("pool", L["vu"]), ("act", repB_ready),
                                   ("dve", b2_free[s])])
                    for lane in range(2):
                        L = lane_state[lane]
                        dc, s = L["dc"], L["s"]
                        col = dc * NST + n
                        vh = w.dve(lambda e, s=s, col=col: e.tensor_tensor_scan(
                            out=sb_h2[s][:], data0=sb_a2[s][:], data1=sb_b2[s][:],
                            initial=sb_state[:, col:col + 1], op0=ALU.mult, op1=ALU.add),
                            waits=[("act", L["va"]), ("pool", L["vb"]),
                                   ("dve", max(v_state_r2, h2_free[s][1] if h2_free[s][0] == "dve" else 0)),
                                   h2_free[s] if h2_free[s][0] != "dve" else None])
                        vst = w.dve(lambda e, s=s, col=col: e.tensor_copy(
                            out=sb_state[:, col:col + 1], in_=sb_h2[s][:, T2 - 1:T2]),
                            waits=[("dve", vh)])
                        a2_free[s] = vst
                        b2_free[s] = vst
                        L["vh"] = vh
                    for lane in range(2):
                        L = lane_state[lane]
                        s = L["s"]
                        ep = "dve" if lane == 0 else "pool"
                        pop = w.dve if lane == 0 else w.pool
                        vp = pop(lambda e, s=s, ni=ni: e.tensor_tensor(
                            out=sb_p2[s][:], in0=sb_h2[s][:],
                            in1=sb_repC[:, ni * T2:(ni + 1) * T2], op=ALU.mult),
                            waits=[("dve", L["vh"]), ("act", repC_ready), p2_free[s]])
                        h2_free[s] = (ep, vp)
                        L["vp"] = (ep, vp)
                    for lane in range(2):
                        L = lane_state[lane]
                        dc, s = L["dc"], L["s"]
                        ez = "pool" if lane == 0 else "dve"
                        eop = w.pool if lane == 0 else w.dve
                        zsl = 0 if lane == 0 else 2
                        into_z = ni == GSZ - 1
                        from_z = g > 0 and ni == 0
                        wl = [L["vp"], L["vz"]]
                        if g == 0 and ni == GSZ - 1:
                            wl += zz_free[zb_]
                        vz = (ez, eop(lambda e, s=s, ni=ni, dc=dc, into_z=into_z, zsl=zsl,
                                      from_z=from_z, sbz=sbz: e.tensor_tensor(
                            out=(sbz[:, dc * T2:(dc + 1) * T2] if into_z
                                 else sb_zacc[zsl + (ni + 1) % 2][:]),
                            in0=(sbz[:, dc * T2:(dc + 1) * T2] if from_z
                                 else sb_zacc[zsl + ni % 2][:]),
                            in1=sb_p2[s][:], op=ALU.add),
                            waits=wl))
                        p2_free[s] = vz
                        L["vz"] = vz
                for lane in range(2):
                    L = lane_state[lane]
                    dl2_free[L["s"]] = [("pool", L["vu"]), ("act", L["act_last"])]
                    vz_carry[L["dc"]] = L["vz"]
            repB_free = w.cnt["pool"]
            repC_free = [("dve", w.cnt["dve"]), ("pool", w.cnt["pool"])]
        global_state["vz_fin"] = list(vz_carry)
        return global_state

    def emit_statsrows(t, gs):
        zb_ = t % 2
        sbz = sb_zz[zb_]
        st2 = 0
        for dc in range(DC):
            s = dc % 2
            vzb = w.act(lambda e, dc=dc, sbz=sbz: e.copy(
                out=sb_zb[0][:], in_=sbz[:, dc * T2:(dc + 1) * T2]),
                waits=[gs["vz_fin"][dc], ("pe", zb_free[0])])
            vz2b = w.act(lambda e, dc=dc, sbz=sbz: e.square(
                out=sb_z2b[0][:], in_=sbz[:, dc * T2:(dc + 1) * T2]),
                waits=[("act", vzb), ("pe", zb_free[1])])
            wl = [("act", vz2b)]
            if dc == 0:
                wl += bank_free[2] + bank_free[3] + [("dS", ld["onescol"] if t == 0 else 0)]
            stz = w.pe(lambda e, dc=dc: e.matmul(
                bank(2, 1), sb_onescol[:], sb_zb[0][:],
                start=(dc == 0), stop=(dc == DC - 1), skip_group_check=True),
                waits=wl)
            st2 = w.pe(lambda e, dc=dc: e.matmul(
                bank(3, 1), sb_onescol[:], sb_z2b[0][:],
                start=(dc == 0), stop=(dc == DC - 1), skip_group_check=True),
                waits=[])
            zb_free[0] = stz
            zb_free[1] = st2
        vm = w.dve(lambda e: e.tensor_scalar(
            out=rr(0, 0), in0=bank(2, 1), scalar1=1.0 / D, scalar2=None, op0=ALU.mult),
            waits=[("pe", st2)] + rows_free)
        vq = w.dve(lambda e: e.tensor_scalar(
            out=rr(0, 1), in0=bank(3, 1), scalar1=1.0 / D, scalar2=None, op0=ALU.mult),
            waits=[("dve", vm)])
        vt = w.dve(lambda e: e.tensor_tensor(out=rr(0, 2), in0=rr(0, 0), in1=rr(0, 0), op=ALU.mult),
                   waits=[("dve", vq)])
        vv = w.dve(lambda e: e.tensor_tensor(out=rr(32, 0), in0=rr(0, 1), in1=rr(0, 2), op=ALU.subtract),
                   waits=[("dve", vt)])
        vve = w.dve(lambda e: e.tensor_scalar(out=rr(32, 1), in0=rr(32, 0), scalar1=EPS,
                                              scalar2=None, op0=ALU.add),
                    waits=[("dve", vv)])
        vsq = w.act(lambda e: e.activation(out=rr(32, 2), in_=rr(32, 1), func=AF.Sqrt),
                    waits=[("dve", vve)])
        vr = w.dve(lambda e: e.reciprocal(out=rr(0, 1), in_=rr(32, 2)), waits=[("act", vsq)])
        vmr = w.dve(lambda e: e.tensor_tensor(out=rr(0, 2), in0=rr(0, 0), in1=rr(0, 1), op=ALU.mult),
                    waits=[("dve", vr)])
        bank_free[2] = [("dve", vm)]
        bank_free[3] = [("dve", vq)]
        gs["vr"] = vr
        gs["vmr"] = vmr

    def emit_ffnA(t, gs):
        nonlocal_ = None
        zb_ = t % 2
        sbz = sb_zz[zb_]
        vr, vmr = gs["vr"], gs["vmr"]
        mmr = w.pe(lambda e: e.matmul(bank(4), sb_onesrow[:], rr(0, 1), start=True, stop=True),
                   waits=bank_free[4] + [("dve", vr),
                                         ("dS", ld["onesrow"] if t == 0 else 0)])
        mmm = w.pe(lambda e: e.matmul(bank(5), sb_onesrow[:], rr(0, 2), start=True, stop=True),
                   waits=bank_free[5] + [("dve", vmr)])
        gs["rows_pe"] = mmm
        x1_last = 0
        for dc in range(DC):
            s = dc % 2
            e1 = w.dve(lambda e, dc=dc, s=s, sbz=sbz: e.tensor_tensor(
                out=sb_xo[s][:], in0=sbz[:, dc * T2:(dc + 1) * T2], in1=bank(4),
                op=ALU.mult),
                waits=[("pe", mmr)] + xo_free[s] + (x1bf_free if dc < 2 else []))
            x1_last = w.dve(lambda e, dc=dc, s=s: e.tensor_tensor(
                out=sb_x1bf[:, dc * T2:(dc + 1) * T2], in0=sb_xo[s][:], in1=bank(5),
                op=ALU.subtract),
                waits=[("pe", mmm), ("dve", e1)])
            xo_free[s] = [("dve", x1_last)]
        bank_free[4] = [("dve", x1_last)]
        bank_free[5] = [("dve", x1_last)]
        gs["x1_last"] = x1_last
        gu_last = 0
        for m in range(HM):
            g_mm = 0
            u_mm = 0
            for hf in range(2):
                wgr = w.dma(f"dA{hf}", sb_wg[hf][:],
                            wbf_g[:, (m * DC + hf * 8) * 128:(m * DC + hf * 8 + 8) * 128],
                            waits=[("pe", wg_free[hf])])
                wur = w.dma(f"dB{hf}", sb_wu[hf][:],
                            wbf_u[:, (m * DC + hf * 8) * 128:(m * DC + hf * 8 + 8) * 128],
                            waits=[("pe", wu_free[hf])])
                for kc8 in range(8):
                    kc = hf * 8 + kc8
                    wl = []
                    if kc8 == 0:
                        wl = [(f"dA{hf}", wgr), ("dve", x1_last)]
                        if kc == 0:
                            wl += bank_free[4]
                    g_mm = w.pe(lambda e, m=m, kc=kc, hf=hf, kc8=kc8: e.matmul(
                        bank(4), sb_wg[hf][:, kc8 * 128:(kc8 + 1) * 128],
                        sb_x1bf[:, kc * T2:(kc + 1) * T2],
                        start=(kc == 0), stop=(kc == DC - 1)),
                        waits=wl)
                wg_free[hf] = g_mm
            for hf in range(2):
                wur2 = 0
                for kc8 in range(8):
                    kc = hf * 8 + kc8
                    wl = []
                    if kc8 == 0:
                        wl = [(f"dB{hf}", w.cnt[f"dB{hf}"])]
                        if kc == 0:
                            wl += bank_free[5]
                    u_mm = w.pe(lambda e, m=m, kc=kc, hf=hf, kc8=kc8: e.matmul(
                        bank(5), sb_wu[hf][:, kc8 * 128:(kc8 + 1) * 128],
                        sb_x1bf[:, kc * T2:(kc + 1) * T2],
                        start=(kc == 0), stop=(kc == DC - 1)),
                        waits=wl)
                wu_free[hf] = u_mm
            vg = w.act(lambda e, m=m: e.activation(
                out=sb_gt[0][:], in_=bank(4), func=AF.Silu, scale=sb_sg[:, m:m + 1]),
                waits=[("pe", g_mm), ("pool", gt_free[0]),
                       ("dS", ld["sg"] if t == 0 and m == 0 else 0)])
            bank_free[4] = [("act", vg)]
            vup = w.act(lambda e, m=m: e.activation(
                out=sb_ut[0][:], in_=bank(5), func=AF.Identity, scale=sb_su[:, m:m + 1]),
                waits=[("pe", u_mm),
                       ("dS", ld["su"] if t == 0 and m == 0 else 0)])
            bank_free[5] = [("act", vup)]
            wl = [("act", vup)]
            if m < 2:
                wl += gu_free
            gu_last = w.pool(lambda e, m=m: e.tensor_tensor(
                out=sb_gu[:, m * T2:(m + 1) * T2], in0=sb_gt[0][:], in1=sb_ut[0][:],
                op=ALU.mult),
                waits=wl)
            gt_free[0] = gu_last
        gs["gu_last"] = gu_last

    def emit_ffnB(t, gs):
        zb_ = t % 2
        sbz = sb_zz[zb_]
        gu_last = gs["gu_last"]
        st4 = 0
        z2_fin = [None] * DC
        for dc in range(DC):
            s = dc % 2
            d_mm = 0
            for hf in range(2):
                wdr = w.dma(f"dE{hf}", sb_wd[hf][:],
                            wbf_d[:, (dc * HM + hf * 22) * 128:(dc * HM + hf * 22 + 22) * 128],
                            waits=[("pe", wd_free[hf])])
                for kh22 in range(22):
                    kh = hf * 22 + kh22
                    wl = []
                    if kh22 == 0:
                        wl = [(f"dE{hf}", wdr), ("pool", gu_last)]
                        if kh == 0:
                            wl += bank_free[6]
                    d_mm = w.pe(lambda e, dc=dc, kh=kh, hf=hf, kh22=kh22: e.matmul(
                        bank(6), sb_wd[hf][:, kh22 * 128:(kh22 + 1) * 128],
                        sb_gu[:, kh * T2:(kh + 1) * T2],
                        start=(kh == 0), stop=(kh == HM - 1)),
                        waits=wl)
                wd_free[hf] = d_mm
            z2 = w.dve(lambda e, dc=dc, sbz=sbz: e.scalar_tensor_tensor(
                out=sbz[:, dc * T2:(dc + 1) * T2], in0=bank(6), scalar=sb_sd[:, dc:dc + 1],
                in1=sb_x1bf[:, dc * T2:(dc + 1) * T2], op0=ALU.mult, op1=ALU.add),
                waits=[("pe", d_mm), ("dS", ld["sd"] if t == 0 and dc == 0 else 0),
                       ("dve", gs["x1_last"])])
            z2_fin[dc] = z2
            gs["x1bf_consumed"] = z2
            bank_free[6] = [("dve", z2)]
            vzb = w.act(lambda e, dc=dc, sbz=sbz: e.copy(
                out=sb_zb[0][:], in_=sbz[:, dc * T2:(dc + 1) * T2]),
                waits=[("dve", z2), ("pe", zb_free[0])])
            vz2b = w.act(lambda e, dc=dc, sbz=sbz: e.square(
                out=sb_z2b[0][:], in_=sbz[:, dc * T2:(dc + 1) * T2]),
                waits=[("act", vzb), ("pe", zb_free[1])])
            wl = [("act", vz2b)]
            if dc == 0:
                wl += bank_free[4] + bank_free[5]
            stz = w.pe(lambda e, dc=dc: e.matmul(
                bank(4, 1), sb_onescol[:], sb_zb[0][:],
                start=(dc == 0), stop=(dc == DC - 1), skip_group_check=True),
                waits=wl)
            st4 = w.pe(lambda e, dc=dc: e.matmul(
                bank(5, 1), sb_onescol[:], sb_z2b[0][:],
                start=(dc == 0), stop=(dc == DC - 1), skip_group_check=True),
                waits=[])
            zb_free[0] = stz
            zb_free[1] = st4
        vm2 = w.dve(lambda e: e.tensor_scalar(
            out=rr(0, 0), in0=bank(4, 1), scalar1=1.0 / D, scalar2=None, op0=ALU.mult),
            waits=[("pe", st4), ("pe", gs["rows_pe"])])
        vq2 = w.dve(lambda e: e.tensor_scalar(
            out=rr(0, 1), in0=bank(5, 1), scalar1=1.0 / D, scalar2=None, op0=ALU.mult),
            waits=[("dve", vm2)])
        vt2 = w.dve(lambda e: e.tensor_tensor(out=rr(0, 2), in0=rr(0, 0), in1=rr(0, 0), op=ALU.mult),
                    waits=[("dve", vq2)])
        vv2 = w.dve(lambda e: e.tensor_tensor(out=rr(32, 0), in0=rr(0, 1), in1=rr(0, 2), op=ALU.subtract),
                    waits=[("dve", vt2)])
        vve2 = w.dve(lambda e: e.tensor_scalar(out=rr(32, 1), in0=rr(32, 0), scalar1=EPS,
                                               scalar2=None, op0=ALU.add),
                     waits=[("dve", vv2)])
        vsq2 = w.act(lambda e: e.activation(out=rr(32, 2), in_=rr(32, 1), func=AF.Sqrt),
                     waits=[("dve", vve2)])
        vr2 = w.dve(lambda e: e.reciprocal(out=rr(0, 1), in_=rr(32, 2)), waits=[("act", vsq2)])
        vmr2 = w.dve(lambda e: e.tensor_tensor(out=rr(0, 2), in0=rr(0, 0), in1=rr(0, 1), op=ALU.mult),
                     waits=[("dve", vr2)])
        bank_free[4] = [("dve", vm2)]
        bank_free[5] = [("dve", vq2)]
        mmr3 = w.pe(lambda e: e.matmul(bank(6), sb_onesrow[:], rr(0, 1), start=True, stop=True),
                    waits=bank_free[6] + [("dve", vr2)])
        mmm3 = w.pe(lambda e: e.matmul(bank(7), sb_onesrow[:], rr(0, 2), start=True, stop=True),
                    waits=bank_free[7] + [("dve", vmr2)])
        o_last = [0, 0]
        c0 = t * T2
        for dc in range(DC):
            s = dc % 2
            e1 = w.dve(lambda e, dc=dc, s=s, sbz=sbz: e.tensor_tensor(
                out=sb_xo[s][:], in0=sbz[:, dc * T2:(dc + 1) * T2], in1=bank(6),
                op=ALU.mult),
                waits=[("pe", mmr3)] + xo_free[s])
            e2 = w.dve(lambda e, dc=dc, s=s: e.tensor_tensor(
                out=sb_xo[2][:], in0=sb_xo[s][:], in1=bank(7),
                op=ALU.subtract),
                waits=[("pe", mmm3), ("dve", e1)] + xo_free[2])
            od = w.dma(f"dO{s}", out_T[dc * 128:(dc + 1) * 128, c0:c0 + T2], sb_xo[2][:],
                       waits=[("dve", e2)])
            xo_free[s] = [("dve", e2)]
            xo_free[2] = [(f"dO{s}", od)]
            o_last[s] = e2
        bank_free[6] = [("dve", max(o_last))]
        bank_free[7] = [("dve", max(o_last))]
        zz_free[zb_] = [("dve", max(o_last)),
                        ("act", w.cnt["act"])]
        nonlocal_rows = [("pe", mmm3)]
        return nonlocal_rows

    for t in range(nt2):
        gs = emit_scanA(t)
        rep_mm_last = gs["rep_mm_last"]
        if t > 0:
            rows_free = emit_ffnB(t - 1, H[t - 1])
            x1bf_free = [("dve", H[t - 1].get("x1bf_consumed", 0))]
            gu_free = [("pe", w.cnt["pe"])]
        emit_statsrows(t, gs)
        emit_ffnA(t, gs)
        H[t] = gs
    rows_free = emit_ffnB(nt2 - 1, H[nt2 - 1])

    r2_stack.close()

    w.barrier("sync", dma_all())

    with nc.Block() as block:
        @block.sync
        def _(e):
            for f in w.ops["sync"]:
                f(e)

        @block.tensor
        def _(e):
            for f in w.ops["tensor"]:
                f(e)

        @block.scalar
        def _(e):
            for f in w.ops["scalar"]:
                f(e)

        @block.vector
        def _(e):
            for f in w.ops["vector"]:
                f(e)

        @block.gpsimd
        def _(e):
            for f in w.ops["gpsimd"]:
                f(e)

    stack.close()
    return nc


def _host_inputs(L0_attn_x, L0_attn_W_in, L0_attn_W_dt, L0_attn_b_dt, L0_attn_A_log,
                 L0_attn_D, L0_ffn_wq_gate, L0_ffn_s_gate, L0_ffn_wq_up, L0_ffn_s_up,
                 L0_ffn_wq_down, L0_ffn_s_down, l_core):
    bf16 = ml_dtypes.bfloat16
    x = np.asarray(L0_attn_x, dtype=np.float32)
    w_inT = np.ascontiguousarray(np.asarray(L0_attn_W_in, np.float32).T)
    w_dtT = np.ascontiguousarray(np.asarray(L0_attn_W_dt, np.float32).T)
    b_dt_t = np.ascontiguousarray(np.asarray(L0_attn_b_dt, np.float32).reshape(DC, 128).T)
    a_t = np.ascontiguousarray(
        np.asarray(L0_attn_A_log, np.float32).reshape(DC, 128, NST)
        .transpose(1, 0, 2).reshape(128, DC * NST))
    d_t = np.ascontiguousarray(np.asarray(L0_attn_D, np.float32).reshape(DC, 128).T)
    s_g = np.ascontiguousarray(np.asarray(L0_ffn_s_gate, np.float32).reshape(HM, 128).T)
    s_u = np.ascontiguousarray(np.asarray(L0_ffn_s_up, np.float32).reshape(HM, 128).T)
    s_d = np.ascontiguousarray(np.asarray(L0_ffn_s_down, np.float32).reshape(DC, 128).T)

    wq_gate = np.asarray(L0_ffn_wq_gate, np.int32).reshape(HM, 128, DC, 128)
    wq_g = np.ascontiguousarray(wq_gate.transpose(3, 0, 2, 1).reshape(128, HM * DC * 128))
    wq_up = np.asarray(L0_ffn_wq_up, np.int32).reshape(HM, 128, DC, 128)
    wq_u = np.ascontiguousarray(wq_up.transpose(3, 0, 2, 1).reshape(128, HM * DC * 128))
    wq_down = np.asarray(L0_ffn_wq_down, np.int32).reshape(DC, 128, HM, 128)
    wq_d = np.ascontiguousarray(wq_down.transpose(3, 0, 2, 1).reshape(128, DC * HM * 128))

    sel = np.zeros((32, 2 * NST * 128), dtype=bf16)
    for r in range(32):
        sel[r, r * 128:(r + 1) * 128] = bf16(1.0)
    onescol = np.ones((128, 1), dtype=bf16)
    onesrow = np.ones((1, 128), dtype=np.float32)

    in_maps = []
    for c in range(NCORES):
        b, q = divmod(c, 4)
        xs = np.ascontiguousarray(x[b, q * l_core:(q + 1) * l_core, :].T)
        msk = np.zeros((128, NCORES), dtype=np.float32)
        for j in range(4 * b, c):
            msk[:, j] = 1.0
        imsk = np.ascontiguousarray(1.0 - msk)
        in_maps.append({
            "x_T": xs, "w_inT": w_inT, "w_dtT": w_dtT, "b_dt_t": b_dt_t,
            "a_t": a_t, "d_t": d_t, "s_g": s_g, "s_u": s_u, "s_d": s_d,
            "wq_g": wq_g, "wq_u": wq_u, "wq_d": wq_d, "msk": msk, "imsk": imsk,
            "sel": sel, "onescol": onescol, "onesrow": onesrow,
        })
    return in_maps


def kernel(**inputs):
    l_core = 4096
    in_maps = _host_inputs(l_core=l_core, **inputs)
    nc = build_nc(l_core)
    res = run_bass_kernel_spmd(nc, in_maps, core_ids=list(range(NCORES)))
    out = np.empty((2, 16384, D), dtype=np.float32)
    for c in range(NCORES):
        b, q = divmod(c, 4)
        out[b, q * l_core:(q + 1) * l_core, :] = res.results[c]["out_T"].T
    return out


# version 06
# speedup vs baseline: 1.2547x; 1.2547x over previous
import numpy as np
import ml_dtypes
from contextlib import ExitStack
import concourse.bass as bass
import concourse.mybir as mybir
from concourse.bass_utils import run_bass_kernel_spmd

F32 = mybir.dt.float32
BF16 = mybir.dt.bfloat16
I32 = mybir.dt.int32
AF = mybir.ActivationFunctionType
ALU = mybir.AluOpType
AX = mybir.AxisListType.X

D = 2048
NST = 16
RDT = 64
HID = 5632
DC = D // 128
HM = HID // 128
NCORES = 8
EPS = 1e-5
T1 = 1024
T2 = 512
WCH = 4096


class Walk:

    def __init__(self, nc, sems):
        self.nc = nc
        self.sems = sems
        self.cnt = {k: 0 for k in sems}
        self.ops = {k: [] for k in ("sync", "tensor", "scalar", "vector", "gpsimd")}

    def _emit(self, stream, own, waits, fn, inc_amt=1, self_serialize=False):
        best = {}
        for item in waits:
            if item is None:
                continue
            s, v = item
            if v and v > 0 and v > best.get(s, 0):
                best[s] = v
        if self_serialize and own and self.cnt[own] > 0:
            if best.get(own, 0) < self.cnt[own]:
                best[own] = self.cnt[own]
        clean = sorted(best.items())
        sems = self.sems
        own_sem = sems[own] if own else None

        def run(e, waits=tuple(clean), fn=fn, own_sem=own_sem, inc_amt=inc_amt):
            for s, v in waits:
                e.wait_ge(sems[s], v)
            ins = fn(e)
            if own_sem is not None:
                ins.then_inc(own_sem, inc_amt)

        self.ops[stream].append(run)
        if own:
            self.cnt[own] += inc_amt
            return self.cnt[own]
        return None

    def dma(self, q, out, in_, waits=()):
        return self._emit("sync", q, waits,
                          lambda e: e.dma_start(out=out, in_=in_), 16)

    def pe(self, fn, waits=()):
        return self._emit("tensor", "pe", waits, fn)

    def act(self, fn, waits=()):
        return self._emit("scalar", "act", waits, fn, self_serialize=True)

    def dve(self, fn, waits=()):
        return self._emit("vector", "dve", waits, fn, self_serialize=True)

    def pool(self, fn, waits=()):
        return self._emit("gpsimd", "pool", waits, fn, self_serialize=True)

    def cc(self, fn, waits=()):
        return self._emit("gpsimd", "cc", waits, fn, 1)

    def barrier(self, stream, waits):
        clean = tuple((s, v) for (s, v) in waits if v and v > 0)
        sems = self.sems

        def run(e, waits=clean):
            for s, v in waits:
                e.wait_ge(sems[s], v)

        self.ops[stream].append(run)


def ep_wait(eng, val):
    return (eng, val)


def build_nc(l_core):
    nt1 = l_core // T1
    nt2 = l_core // T2
    nc = bass.Bass("TRN2", target_bir_lowering=False)

    x_T = nc.dram_tensor("x_T", [D, l_core], F32, kind="ExternalInput").ap()
    w_inT = nc.dram_tensor("w_inT", [D, 96], F32, kind="ExternalInput").ap()
    w_dtT = nc.dram_tensor("w_dtT", [RDT, D], F32, kind="ExternalInput").ap()
    b_dt_t = nc.dram_tensor("b_dt_t", [128, DC], F32, kind="ExternalInput").ap()
    a_t = nc.dram_tensor("a_t", [128, DC * NST], F32, kind="ExternalInput").ap()
    d_t = nc.dram_tensor("d_t", [128, DC], F32, kind="ExternalInput").ap()
    s_g = nc.dram_tensor("s_g", [128, HM], F32, kind="ExternalInput").ap()
    s_u = nc.dram_tensor("s_u", [128, HM], F32, kind="ExternalInput").ap()
    s_d = nc.dram_tensor("s_d", [128, DC], F32, kind="ExternalInput").ap()
    wq_g = nc.dram_tensor("wq_g", [128, HM * DC * 128], I32, kind="ExternalInput").ap()
    wq_u = nc.dram_tensor("wq_u", [128, HM * DC * 128], I32, kind="ExternalInput").ap()
    wq_d = nc.dram_tensor("wq_d", [128, DC * HM * 128], I32, kind="ExternalInput").ap()
    msk_in = nc.dram_tensor("msk", [128, NCORES], F32, kind="ExternalInput").ap()
    imsk_in = nc.dram_tensor("imsk", [128, NCORES], F32, kind="ExternalInput").ap()
    sel_in = nc.dram_tensor("sel", [32, 2 * NST * 128], BF16, kind="ExternalInput").ap()
    onescol_in = nc.dram_tensor("onescol", [128, 1], BF16, kind="ExternalInput").ap()
    onesrow_in = nc.dram_tensor("onesrow", [1, 128], F32, kind="ExternalInput").ap()

    out_T = nc.dram_tensor("out_T", [D, l_core], F32, kind="ExternalOutput").ap()

    dstash = nc.dram_tensor("dstash", [D, l_core], F32)
    bcstash = nc.dram_tensor("bcstash", [32, l_core], BF16)
    bstash = nc.dram_tensor("bstash", [128, DC * NST * l_core], BF16)
    wbf_g = nc.dram_tensor("wbf_g", [128, HM * DC * 128], BF16)
    wbf_u = nc.dram_tensor("wbf_u", [128, HM * DC * 128], BF16)
    wbf_d = nc.dram_tensor("wbf_d", [128, DC * HM * 128], BF16)
    gp_src = nc.dram_tensor("gp_src", [256, DC * NST], F32)
    gp_dst = nc.dram_tensor("gp_dst", [256 * NCORES, DC * NST], F32)

    stack = ExitStack()
    sems = {}
    dma_sem_names = (["dS", "dT", "dV", "dP"]
                     + [f"d{k}{i}" for k in "CIOXDABEJ" for i in range(2)]
                     + [f"dK{i}{j}" for i in range(2) for j in range(3)]
                     + ["dZ0", "dZ1"])
    for name in dma_sem_names + ["pe", "act", "dve", "pool", "cc"]:
        sems[name] = stack.enter_context(nc.semaphore(name))
    w = Walk(nc, sems)

    E = stack.enter_context
    sb_bdt = E(nc.sbuf_tensor("sb_bdt", [128, DC], F32))
    sb_Apos = E(nc.sbuf_tensor("sb_Apos", [128, DC * NST], F32))
    sb_Aneg = E(nc.sbuf_tensor("sb_Aneg", [128, DC * NST], F32))
    sb_Dt = E(nc.sbuf_tensor("sb_Dt", [128, DC], F32))
    sb_D1 = E(nc.sbuf_tensor("sb_D1", [128, DC], F32))
    sb_sg = E(nc.sbuf_tensor("sb_sg", [128, HM], F32))
    sb_su = E(nc.sbuf_tensor("sb_su", [128, HM], F32))
    sb_sd = E(nc.sbuf_tensor("sb_sd", [128, DC], F32))
    sb_msk = E(nc.sbuf_tensor("sb_msk", [128, NCORES], F32))
    sb_imsk = E(nc.sbuf_tensor("sb_imsk", [128, NCORES], F32))
    sb_sel = E(nc.sbuf_tensor("sb_sel", [32, 2 * NST * 128], BF16))
    sb_onescol = E(nc.sbuf_tensor("sb_onescol", [128, 1], BF16))
    sb_onesrow = E(nc.sbuf_tensor("sb_onesrow", [1, 128], F32))
    sb_state = E(nc.sbuf_tensor("sb_state", [128, DC * NST], F32))

    ps8 = E(nc.psum_tensor("ps8", [128, 4096], F32))

    def bank(i, pn=128, c0=0, cn=512):
        return ps8[0:pn, i * 512 + c0:i * 512 + c0 + cn]

    def sdma(out, in_):
        return w.dma("dS", out, in_, waits=[("dS", w.cnt["dS"])])

    def dma_all():
        return [(n, w.cnt[n]) for n in dma_sem_names]

    ld = {}
    ld["bdt"] = sdma(sb_bdt[:], b_dt_t)
    ld["alog"] = sdma(sb_Apos[:], a_t)
    ld["dt"] = sdma(sb_Dt[:], d_t)
    ld["sg"] = sdma(sb_sg[:], s_g)
    ld["su"] = sdma(sb_su[:], s_u)
    ld["sd"] = sdma(sb_sd[:], s_d)
    ld["msk"] = sdma(sb_msk[:], msk_in)
    ld["imsk"] = sdma(sb_imsk[:], imsk_in)
    ld["sel"] = sdma(sb_sel[:], sel_in)
    ld["onescol"] = sdma(sb_onescol[:], onescol_in)
    ld["onesrow"] = sdma(sb_onesrow[:], onesrow_in)

    v_apos = w.act(lambda e: e.activation(out=sb_Apos[:], in_=sb_Apos[:], func=AF.Exp),
                   waits=[("dS", ld["alog"])])
    v_aneg = w.pool(lambda e: e.tensor_scalar(out=sb_Aneg[:], in0=sb_Apos[:],
                                              scalar1=-1.0, scalar2=None, op0=ALU.mult),
                    waits=[("act", v_apos)])
    v_d1 = w.pool(lambda e: e.tensor_scalar(out=sb_D1[:], in0=sb_Dt[:],
                                            scalar1=1.0, scalar2=None, op0=ALU.add),
                  waits=[("dS", ld["dt"])])
    v_state0 = w.dve(lambda e: e.memset(sb_state[:], 0.0))

    conv_stack = ExitStack()
    cv_i = [conv_stack.enter_context(nc.sbuf_tensor(f"cv_i{i}", [128, WCH], I32)) for i in range(2)]
    cv_b = [conv_stack.enter_context(nc.sbuf_tensor(f"cv_b{i}", [128, WCH], BF16)) for i in range(2)]
    nch = HM * DC * 128 // WCH
    cv_in_done = [0, 0]
    cv_ts_done = [0, 0]
    cv_out_done = [0, 0]
    ci = 0
    for src, dst in ((wq_g, wbf_g), (wq_u, wbf_u), (wq_d, wbf_d)):
        for ch in range(nch):
            s = ci % 2
            cv_in_done[s] = w.dma(f"dC{s}", cv_i[s][:], src[:, ch * WCH:(ch + 1) * WCH],
                                  waits=[("dve", cv_ts_done[s])])
            cv_ts_done[s] = w.dve(
                lambda e, s=s: e.tensor_scalar(out=cv_b[s][:], in0=cv_i[s][:],
                                               scalar1=128.0, scalar2=None,
                                               op0=ALU.subtract),
                waits=[(f"dC{s}", cv_in_done[s]), (f"dI{s}", cv_out_done[s])])
            cv_out_done[s] = w.dma(f"dI{s}", dst[:, ch * WCH:(ch + 1) * WCH], cv_b[s][:],
                                   waits=[("dve", cv_ts_done[s])])
            ci += 1
    conv_done_dve = w.cnt["dve"]
    conv_stack.close()

    w.barrier("sync", [("dve", conv_done_dve)] + dma_all())

    r1_stack = ExitStack()
    R = r1_stack.enter_context
    sb_winT = R(nc.sbuf_tensor("sb_winT", [128, DC * 96], F32))
    sb_wdtT = R(nc.sbuf_tensor("sb_wdtT", [RDT, D], F32))
    sb_G = R(nc.sbuf_tensor("sb_G", [128, DC * NST], F32))
    sb_P = R(nc.sbuf_tensor("sb_P", [128, DC * NST], F32))
    sb_hin = R(nc.sbuf_tensor("sb_hin", [128, DC * NST], F32))
    sb_ht = R(nc.sbuf_tensor("sb_ht", [128, DC * NST], F32))
    sb_gm = R(nc.sbuf_tensor("sb_gm", [128, DC * NST], F32))
    sb_pm = R(nc.sbuf_tensor("sb_pm", [128, DC * NST], F32))
    sb_Gj = [R(nc.sbuf_tensor(f"sb_Gj{i}", [128, DC * NST], F32)) for i in range(2)]
    sb_Pj = [R(nc.sbuf_tensor(f"sb_Pj{i}", [128, DC * NST], F32)) for i in range(2)]
    sb_sumdall = R(nc.sbuf_tensor("sb_sumdall", [128, DC * 16], F32))
    sb_sumdtot = R(nc.sbuf_tensor("sb_sumdtot", [128, DC], F32))
    sb_d1l = R(nc.sbuf_tensor("sb_d1l", [128, DC * T1], F32))
    sb_u1l = R(nc.sbuf_tensor("sb_u1l", [128, DC * T1], F32))
    sb_xs = [R(nc.sbuf_tensor(f"sb_xs{i}", [128, T1], F32)) for i in range(2)]
    sb_dlr = R(nc.sbuf_tensor("sb_dlr", [RDT, T1], F32))
    sb_bcb1 = R(nc.sbuf_tensor("sb_bcb1", [32, T1], BF16))
    sb_rep1 = [R(nc.sbuf_tensor(f"sb_rep1_{i}", [128, T1], BF16)) for i in range(2)]
    sb_sa = [R(nc.sbuf_tensor(f"sb_sa{i}", [128, T1], BF16)) for i in range(2)]
    sb_sb = [R(nc.sbuf_tensor(f"sb_sb{i}", [128, T1], BF16)) for i in range(2)]
    sb_sh = R(nc.sbuf_tensor("sb_sh", [128, T1], F32))
    sb_S = R(nc.sbuf_tensor("sb_S", [128, T1], F32))
    sb_ones1 = R(nc.sbuf_tensor("sb_ones1", [128, T1], BF16))
    sb_Gparts = R(nc.sbuf_tensor("sb_Gparts", [128, 4 * DC * NST], F32))
    sb_scar = R(nc.sbuf_tensor("sb_scar", [128, DC], F32))
    sb_suf = R(nc.sbuf_tensor("sb_suf", [128, DC], F32))
    sb_wt = R(nc.sbuf_tensor("sb_wt", [128, DC * NST], F32))

    for dc in range(DC):
        ld["winT"] = sdma(sb_winT[:, dc * 96:(dc + 1) * 96],
                          w_inT[dc * 128:(dc + 1) * 128, :])
    ld["wdtT"] = sdma(sb_wdtT[:], w_dtT)

    xs_free = [[], []]
    a_free = [0, 0]
    b_free = [0, 0]
    h_free = [0, 0]
    rep1_free = [0, 0]
    bank_free = [[] for _ in range(8)]
    v_softadd = 0
    bst_done = [0, 0]
    ev_bc_last = 0
    dlr_free_pe = 0
    bcb1_free_pe = 0
    stash_last = 0

    for t in range(nt1):
        c0 = t * T1
        lr_mm = [0, 0]
        bc_mm = [0, 0]
        for dc in range(DC):
            s = dc % 2
            xr = w.dma(f"dX{s}", sb_xs[s][:], x_T[dc * 128:(dc + 1) * 128, c0:c0 + T1],
                       waits=xs_free[s])
            for cc in range(2):
                first = dc == 0
                last = dc == DC - 1
                wl = [(f"dX{s}", xr)]
                if first:
                    wl += [("dS", ld["winT"])] + bank_free[cc] + bank_free[2 + cc]
                lr_mm[cc] = w.pe(
                    lambda e, s=s, cc=cc, dc=dc, first=first, last=last: e.matmul(
                        bank(cc, RDT), sb_winT[:, dc * 96:dc * 96 + RDT],
                        sb_xs[s][:, cc * 512:(cc + 1) * 512],
                        start=first, stop=last, skip_group_check=True),
                    waits=wl)
                bc_mm[cc] = w.pe(
                    lambda e, s=s, cc=cc, dc=dc, first=first, last=last: e.matmul(
                        bank(2 + cc, 32), sb_winT[:, dc * 96 + RDT:(dc + 1) * 96],
                        sb_xs[s][:, cc * 512:(cc + 1) * 512],
                        start=first, stop=last, skip_group_check=True),
                    waits=[])
            xs_free[s] = [("pe", bc_mm[1])]
        ev_lr = [0, 0]
        ev_bc = [0, 0]
        for cc in range(2):
            ev_lr[cc] = w.act(
                lambda e, cc=cc: e.copy(out=sb_dlr[:, cc * 512:(cc + 1) * 512],
                                        in_=bank(cc, RDT)),
                waits=[("pe", lr_mm[cc]), ("pe", dlr_free_pe)])
            ev_bc[cc] = w.act(
                lambda e, cc=cc: e.copy(out=sb_bcb1[:, cc * 512:(cc + 1) * 512],
                                        in_=bank(2 + cc, 32)),
                waits=[("pe", bc_mm[cc]), ("pe", bcb1_free_pe)])
        bank_free[0] = [("act", ev_lr[0])]
        bank_free[1] = [("act", ev_lr[1])]
        bank_free[2] = [("act", ev_bc[0])]
        bank_free[3] = [("act", ev_bc[1])]
        ev_bc_last = max(ev_bc)
        w.dma("dT", bcstash[:, c0:c0 + T1], sb_bcb1[:],
              waits=[("act", ev_bc_last), ("dT", w.cnt["dT"])])

        for dc in range(DC):
            for cc in range(2):
                bki = 4 + cc
                pcol = dc * T1 + cc * 512
                mmv = w.pe(
                    lambda e, dc=dc, cc=cc, bki=bki: e.matmul(
                        bank(bki), sb_wdtT[:, dc * 128:(dc + 1) * 128],
                        sb_dlr[:, cc * 512:(cc + 1) * 512],
                        start=True, stop=True),
                    waits=bank_free[bki] + [("act", max(ev_lr)),
                                            ("dS", ld["wdtT"] if t == 0 and dc == 0 else 0)])
                dlr_free_pe = mmv
                v1 = w.act(lambda e, dc=dc, bki=bki: e.activation(
                    out=sb_S[:, 0:512], in_=bank(bki), func=AF.Abs,
                    bias=sb_bdt[:, dc:dc + 1]),
                    waits=[("pe", mmv), ("dve", max(a_free[0], a_free[1])),
                           ("dS", ld["bdt"] if t == 0 and dc == 0 else 0),
                           ("dT", stash_last if dc == 0 and cc == 0 else 0)])
                v2 = w.act(lambda e: e.activation(
                    out=sb_sh[:, 0:512], in_=sb_S[:, 0:512], func=AF.Exp,
                    scale=-1.0),
                    waits=[("act", v1)])
                v3 = w.act(lambda e: e.activation(
                    out=sb_S[:, 512:1024], in_=sb_sh[:, 0:512], func=AF.Ln,
                    bias=1.0),
                    waits=[("act", v2), ("pool", v_softadd)])
                v4 = w.act(lambda e, dc=dc, bki=bki: e.activation(
                    out=sb_sh[:, 512:1024], in_=bank(bki), func=AF.Relu,
                    bias=sb_bdt[:, dc:dc + 1]),
                    waits=[("act", v3)])
                bank_free[bki] = [("act", v4)]
                v_softadd = w.pool(lambda e, pcol=pcol: e.tensor_tensor(
                    out=sb_d1l[:, pcol:pcol + 512], in0=sb_S[:, 512:1024],
                    in1=sb_sh[:, 512:1024], op=ALU.add),
                    waits=[("act", v4)])
        delta_done = v_softadd

        for dc in range(DC):
            w.dve(lambda e, dc=dc, t=t: e.reduce_sum(
                out=sb_sumdall[:, dc * 16 + t:dc * 16 + t + 1],
                in_=sb_d1l[:, dc * T1:(dc + 1) * T1], axis=AX),
                waits=[("pool", delta_done)])

        u_done = 0
        for dc in range(DC):
            s = dc % 2
            stash_last = w.dma("dT", dstash[dc * 128:(dc + 1) * 128, c0:c0 + T1],
                               sb_d1l[:, dc * T1:(dc + 1) * T1],
                               waits=[("pool", delta_done), ("dT", w.cnt["dT"])])
            xr = w.dma(f"dX{s}", sb_xs[s][:], x_T[dc * 128:(dc + 1) * 128, c0:c0 + T1],
                       waits=xs_free[s])
            u_done = w.pool(lambda e, dc=dc, s=s: e.tensor_tensor(
                out=sb_u1l[:, dc * T1:(dc + 1) * T1], in0=sb_d1l[:, dc * T1:(dc + 1) * T1],
                in1=sb_xs[s][:], op=ALU.mult),
                waits=[(f"dX{s}", xr), ("pool", delta_done)])
            xs_free[s] = [("pool", u_done)]

        if t == 0:
            v_ones = w.dve(lambda e: e.memset(sb_ones1[:], 1.0), waits=[])
            v_scar = w.dve(lambda e: e.memset(sb_scar[:], 0.0), waits=[])
        r_done = 0
        for dc in range(DC):
            vS = w.dve(lambda e, dc=dc: e.tensor_tensor_scan(
                out=sb_S[:], data0=sb_ones1[:], data1=sb_d1l[:, dc * T1:(dc + 1) * T1],
                initial=sb_scar[:, dc:dc + 1], op0=ALU.mult, op1=ALU.add),
                waits=[("pool", u_done), (f"dT", stash_last)])
            vC = w.dve(lambda e, dc=dc: e.tensor_copy(
                out=sb_scar[:, dc:dc + 1], in_=sb_S[:, T1 - 1:T1]),
                waits=[("dve", vS)])
            r_done = w.dve(lambda e, dc=dc: e.tensor_scalar(
                out=sb_d1l[:, dc * T1:(dc + 1) * T1], in0=sb_S[:],
                scalar1=-1.0, scalar2=sb_scar[:, dc:dc + 1],
                op0=ALU.mult, op1=ALU.add),
                waits=[("dve", vC)])

        for n in range(NST):
            sn = n % 2
            bks = 0 if sn == 0 else 2
            mm1 = w.pe(lambda e, n=n, bks=bks: e.matmul(
                bank(bks), sb_sel[:, n * 128:(n + 1) * 128], sb_bcb1[:, 0:512],
                start=True, stop=True),
                waits=bank_free[bks] + bank_free[bks + 1] +
                      [("act", ev_bc_last), ("dS", ld["sel"] if t == 0 and n == 0 else 0)])
            mm2 = w.pe(lambda e, n=n, bks=bks: e.matmul(
                bank(bks + 1), sb_sel[:, n * 128:(n + 1) * 128], sb_bcb1[:, 512:1024],
                start=True, stop=True),
                waits=[])
            bcb1_free_pe = mm2
            evr = w.act(lambda e, sn=sn, bks=bks: e.copy(
                out=sb_rep1[sn][:], in_=bank(bks, 128, 0, 1024)),
                waits=[("pe", mm2), ("pool", rep1_free[sn])])
            bank_free[bks] = [("act", evr)]
            bank_free[bks + 1] = [("act", evr)]
            for dc in range(DC):
                col = dc * NST + n
                sa = dc % 2
                va = w.act(lambda e, dc=dc, col=col, sa=sa: e.activation(
                    out=sb_sa[sa][:], in_=sb_d1l[:, dc * T1:(dc + 1) * T1], func=AF.Exp,
                    scale=sb_Aneg[:, col:col + 1]),
                    waits=[("dve", max(r_done, a_free[sa])),
                           ("pool", v_aneg if t == 0 and n == 0 and dc == 0 else 0)])
                vb = w.pool(lambda e, dc=dc, sa=sa, sn=sn: e.tensor_tensor(
                    out=sb_sb[sa][:], in0=sb_u1l[:, dc * T1:(dc + 1) * T1],
                    in1=sb_rep1[sn][:], op=ALU.mult),
                    waits=[("pool", u_done), ("dve", b_free[sa]), ("act", evr),
                           (f"dZ{sa}", bst_done[sa])])
                bst_done[sa] = w.dma(f"dZ{sa}",
                                     bstash[:, col * l_core + c0:col * l_core + c0 + T1],
                                     sb_sb[sa][:], waits=[("pool", vb)])
                vg = w.dve(lambda e, sa=sa, col=col, t=t: e.scalar_tensor_tensor(
                    out=sb_sh[:], in0=sb_sa[sa][:], scalar=1.0, in1=sb_sb[sa][:],
                    op0=ALU.bypass, op1=ALU.mult,
                    accum_out=sb_Gparts[:, t * DC * NST + col:t * DC * NST + col + 1]),
                    waits=[("act", va), ("pool", vb)])
                a_free[sa] = vg
                b_free[sa] = vg
            rep1_free[sn] = w.cnt["pool"]

    r1_dve = w.cnt["dve"]
    r1_pool = w.cnt["pool"]
    r1_act = w.cnt["act"]
    r1_pe = w.cnt["pe"]

    vG = w.dve(lambda e: e.memset(sb_G[:], 0.0), waits=[("dve", r1_dve)])
    for t in range(nt1):
        for dc in range(DC):
            if t == nt1 - 1:
                vsf = w.dve(lambda e, dc=dc: e.memset(sb_suf[:, dc:dc + 1], 0.0),
                            waits=[])
            else:
                vsf = w.dve(lambda e, dc=dc, t=t: e.reduce_sum(
                    out=sb_suf[:, dc:dc + 1],
                    in_=sb_sumdall[:, dc * 16 + t + 1:dc * 16 + nt1], axis=AX),
                    waits=[("dve", r1_dve)])
            w.act(lambda e, dc=dc: e.activation(
                out=sb_wt[:, dc * NST:(dc + 1) * NST],
                in_=sb_Aneg[:, dc * NST:(dc + 1) * NST], func=AF.Exp,
                scale=sb_suf[:, dc:dc + 1]),
                waits=[("dve", vsf)])
        va_wt = w.cnt["act"]
        vmul = w.dve(lambda e, t=t: e.tensor_tensor(
            out=sb_ht[:], in0=sb_Gparts[:, t * DC * NST:(t + 1) * DC * NST],
            in1=sb_wt[:], op=ALU.mult),
            waits=[("act", va_wt), ("dve", vG)])
        vG = w.dve(lambda e: e.tensor_tensor(
            out=sb_gm[:], in0=sb_G[:], in1=sb_ht[:], op=ALU.add),
            waits=[("dve", vmul)])
        vG = w.dve(lambda e: e.tensor_copy(out=sb_G[:], in_=sb_gm[:]),
                   waits=[("dve", vG)])
    vP = 0
    for dc in range(DC):
        vsd = w.dve(lambda e, dc=dc: e.reduce_sum(
            out=sb_sumdtot[:, dc:dc + 1],
            in_=sb_sumdall[:, dc * 16:dc * 16 + nt1], axis=AX),
            waits=[("dve", r1_dve)])
        vP = w.act(lambda e, dc=dc: e.activation(
            out=sb_P[:, dc * NST:(dc + 1) * NST],
            in_=sb_Aneg[:, dc * NST:(dc + 1) * NST], func=AF.Exp,
            scale=sb_sumdtot[:, dc:dc + 1]),
            waits=[("dve", vsd)])

    d_g = w.dma("dP", gp_src[0:128, :], sb_G[:], waits=[("dve", vG)])
    d_p = w.dma("dP", gp_src[128:256, :], sb_P[:],
                waits=[("act", vP), ("dP", d_g)])
    cc_done = w.cc(lambda e: e.collective_compute(
        "AllGather", ALU.bypass, replica_groups=[list(range(NCORES))],
        ins=[gp_src.ap().opt()], outs=[gp_dst.ap().opt()]),
        waits=[("dP", d_p), ("pool", r1_pool)])
    vH = w.dve(lambda e: e.memset(sb_hin[:], 0.0), waits=[("dve", vG)])
    gp_slot_free = [0, 0]
    for j in range(NCORES):
        s = j % 2
        gj = w.dma(f"dJ{s}", sb_Gj[s][:], gp_dst[j * 256:j * 256 + 128, :],
                   waits=[("cc", cc_done), ("dve", gp_slot_free[s]),
                          (f"dJ{s}", w.cnt[f"dJ{s}"])])
        pj = w.dma(f"dJ{s}", sb_Pj[s][:], gp_dst[j * 256 + 128:(j + 1) * 256, :],
                   waits=[(f"dJ{s}", gj)])
        v1 = w.dve(lambda e, s=s, j=j: e.tensor_scalar(
            out=sb_gm[:], in0=sb_Gj[s][:], scalar1=sb_msk[:, j:j + 1], scalar2=None,
            op0=ALU.mult),
            waits=[(f"dJ{s}", pj), ("dve", vH)])
        v2 = w.dve(lambda e, s=s, j=j: e.tensor_scalar(
            out=sb_pm[:], in0=sb_Pj[s][:], scalar1=sb_msk[:, j:j + 1],
            scalar2=sb_imsk[:, j:j + 1], op0=ALU.mult, op1=ALU.add),
            waits=[("dve", v1)])
        gp_slot_free[s] = v2
        v3 = w.dve(lambda e: e.tensor_tensor(
            out=sb_ht[:], in0=sb_hin[:], in1=sb_pm[:], op=ALU.mult),
            waits=[("dve", v2)])
        vH = w.dve(lambda e: e.tensor_tensor(
            out=sb_hin[:], in0=sb_ht[:], in1=sb_gm[:], op=ALU.add),
            waits=[("dve", v3)])
    v_state_r2 = w.dve(lambda e: e.tensor_copy(out=sb_state[:], in_=sb_hin[:]),
                       waits=[("dve", vH)])

    r1_stack.close()

    r2_stack = ExitStack()
    R = r2_stack.enter_context
    NG, GSZ = 8, 2
    sb_repC = R(nc.sbuf_tensor("sb_repC", [128, GSZ * T2], BF16))
    sb_zz = [R(nc.sbuf_tensor(f"sb_zz{i}", [128, DC * T2], F32)) for i in range(2)]
    sb_x1bf = R(nc.sbuf_tensor("sb_x1bf", [128, DC * T2], BF16))
    sb_gu = R(nc.sbuf_tensor("sb_gu", [128, HM * T2], BF16))
    sb_bcb2 = R(nc.sbuf_tensor("sb_bcb2", [32, T2], BF16))
    sb_dl2 = [R(nc.sbuf_tensor(f"sb_dl2_{i}", [128, T2], F32)) for i in range(2)]
    sb_x2 = [R(nc.sbuf_tensor(f"sb_x2_{i}", [128, T2], F32)) for i in range(2)]
    sb_a2 = [R(nc.sbuf_tensor(f"sb_a2_{i}", [128, T2], F32)) for i in range(2)]
    sb_b2 = [R(nc.sbuf_tensor(f"sb_b2_{i}", [128, 3 * T2], BF16)) for i in range(2)]
    sb_h2 = [R(nc.sbuf_tensor(f"sb_h2_{i}", [128, T2], BF16)) for i in range(2)]
    sb_p2 = [R(nc.sbuf_tensor(f"sb_p2_{i}", [128, T2], BF16)) for i in range(2)]
    sb_zacc = [R(nc.sbuf_tensor(f"sb_zacc{i}", [128, T2], F32)) for i in range(4)]
    sb_zb = [R(nc.sbuf_tensor(f"sb_zb{i}", [128, T2], BF16)) for i in range(1)]
    sb_z2b = [R(nc.sbuf_tensor(f"sb_z2b{i}", [128, T2], BF16)) for i in range(1)]
    sb_rows = R(nc.sbuf_tensor("sb_rows", [128, 3 * T2], F32))
    sb_wg = [R(nc.sbuf_tensor(f"sb_wg{i}", [128, 8 * 128], BF16)) for i in range(2)]
    sb_wu = [R(nc.sbuf_tensor(f"sb_wu{i}", [128, 8 * 128], BF16)) for i in range(2)]
    sb_wd = [R(nc.sbuf_tensor(f"sb_wd{i}", [128, 22 * 128], BF16)) for i in range(2)]
    sb_gt = [R(nc.sbuf_tensor(f"sb_gt{i}", [128, T2], BF16)) for i in range(1)]
    sb_ut = [R(nc.sbuf_tensor(f"sb_ut{i}", [128, T2], BF16)) for i in range(1)]
    sb_xo = [R(nc.sbuf_tensor(f"sb_xo{i}", [128, T2], F32)) for i in range(3)]

    def rr(base, slot):
        return sb_rows[base:base + 1, slot * T2:(slot + 1) * T2]

    r1_cut = [("dve", v_state_r2), ("pool", r1_pool), ("act", max(r1_act, vP)),
              ("pe", r1_pe)]
    for st_name in ("vector", "scalar", "gpsimd", "tensor"):
        w.barrier(st_name, r1_cut)
    w.barrier("sync", r1_cut + dma_all())

    bank_free = [[] for _ in range(8)]
    dl2_free = [[], []]
    x2_free = [[], []]
    a2_free = [0, 0]
    bfree3 = [[0, 0, 0], [0, 0, 0]]
    h2_free = [("dve", 0), ("dve", 0)]
    p2_free = [("dve", 0), ("dve", 0)]
    zb_free = [0, 0]
    gt_free = [0, 0]
    wg_free = [0, 0]
    wu_free = [0, 0]
    wd_free = [0, 0]
    xo_free = [[], [], []]
    zz_free = [[], []]
    repC_free = []
    rep_mm_last = 0
    rows_free = []
    x1bf_free = []
    gu_free = []
    H = {}

    def emit_scanA(t):
        nonlocal repC_free
        c0 = t * T2
        zb_ = t % 2
        sbz = sb_zz[zb_]
        global_state = {}
        bc_ready = w.dma("dV", sb_bcb2[:], bcstash[:, c0:c0 + T2],
                         waits=[("pe", rep_mm_last), ("dV", w.cnt["dV"])])
        vz_carry = [None] * DC
        for g in range(NG):
            n0 = g * GSZ
            repC_ready = 0
            for ni in range(GSZ):
                n = n0 + ni
                bki = ni % 2
                mmC = w.pe(lambda e, n=n, bki=bki: e.matmul(
                    bank(bki), sb_sel[:, (NST + n) * 128:(NST + n + 1) * 128], sb_bcb2[:],
                    start=True, stop=True),
                    waits=bank_free[bki] + [("dV", bc_ready)])
                mmlast = mmC
                repC_ready = w.act(lambda e, ni=ni, bki=bki: e.copy(
                    out=sb_repC[:, ni * T2:(ni + 1) * T2], in_=bank(bki)),
                    waits=[("pe", mmC)] + repC_free)
                bank_free[bki] = [("act", repC_ready)]
            global_state["rep_mm_last"] = mmlast
            for dcp in range(0, DC, 2):
                lane_state = []
                for lane in range(2):
                    dc = dcp + lane
                    s = lane
                    ez = "pool" if lane == 0 else "dve"
                    eop = w.pool if lane == 0 else w.dve
                    d_rdy = w.dma(f"dD{s}", sb_dl2[s][:],
                                  dstash[dc * 128:(dc + 1) * 128, c0:c0 + T2],
                                  waits=dl2_free[s])
                    if g == 0:
                        x_rdy = w.dma(f"dX{s}", sb_x2[s][:],
                                      x_T[dc * 128:(dc + 1) * 128, c0:c0 + T2],
                                      waits=x2_free[s])
                        zsl = 0 if lane == 0 else 2
                        vy = eop(lambda e, s=s, dc=dc, zsl=zsl: e.tensor_scalar(
                            out=sb_zacc[zsl][:], in0=sb_x2[s][:],
                            scalar1=sb_D1[:, dc:dc + 1],
                            scalar2=None, op0=ALU.mult),
                            waits=[(f"dX{s}", x_rdy),
                                   ("pool", v_d1 if t == 0 and dc == 0 else 0)])
                        vz = (ez, vy)
                        x2_free[s] = [vz]
                    else:
                        vz = vz_carry[dc]
                    lane_state.append({"dc": dc, "s": s, "d_rdy": d_rdy,
                                       "vz": vz, "act_last": 0})
                for ni in range(GSZ):
                    n = n0 + ni
                    for lane in range(2):
                        L = lane_state[lane]
                        dc, s = L["dc"], L["s"]
                        col = dc * NST + n
                        va = w.act(lambda e, s=s, col=col: e.activation(
                            out=sb_a2[s][:], in_=sb_dl2[s][:], func=AF.Exp,
                            scale=sb_Aneg[:, col:col + 1]),
                            waits=[(f"dD{s}", L["d_rdy"]), ("dve", a2_free[s])])
                        L["act_last"] = va
                        L["va"] = va
                    for lane in range(2):
                        L = lane_state[lane]
                        dc, s = L["dc"], L["s"]
                        col = dc * NST + n
                        bslot = (g * GSZ + ni) % 3
                        L["vb"] = w.dma(f"dK{s}{bslot}",
                                        sb_b2[s][:, bslot * T2:(bslot + 1) * T2],
                                        bstash[:, col * l_core + c0:col * l_core + c0 + T2],
                                        waits=[("dve", bfree3[s][bslot])])
                        L["bslot"] = bslot
                    for lane in range(2):
                        L = lane_state[lane]
                        dc, s = L["dc"], L["s"]
                        col = dc * NST + n
                        bslot = L["bslot"]
                        vh = w.dve(lambda e, s=s, col=col, bslot=bslot: e.tensor_tensor_scan(
                            out=sb_h2[s][:], data0=sb_a2[s][:],
                            data1=sb_b2[s][:, bslot * T2:(bslot + 1) * T2],
                            initial=sb_state[:, col:col + 1], op0=ALU.mult, op1=ALU.add),
                            waits=[("act", L["va"]), (f"dK{s}{bslot}", L["vb"]),
                                   ("dve", max(v_state_r2, h2_free[s][1] if h2_free[s][0] == "dve" else 0)),
                                   h2_free[s] if h2_free[s][0] != "dve" else None])
                        vst = w.dve(lambda e, s=s, col=col: e.tensor_copy(
                            out=sb_state[:, col:col + 1], in_=sb_h2[s][:, T2 - 1:T2]),
                            waits=[("dve", vh)])
                        a2_free[s] = vst
                        bfree3[s][bslot] = vh
                        L["vh"] = vh
                    for lane in range(2):
                        L = lane_state[lane]
                        s = L["s"]
                        ep = "dve" if lane == 0 else "pool"
                        pop = w.dve if lane == 0 else w.pool
                        vp = pop(lambda e, s=s, ni=ni: e.tensor_tensor(
                            out=sb_p2[s][:], in0=sb_h2[s][:],
                            in1=sb_repC[:, ni * T2:(ni + 1) * T2], op=ALU.mult),
                            waits=[("dve", L["vh"]), ("act", repC_ready), p2_free[s]])
                        h2_free[s] = (ep, vp)
                        L["vp"] = (ep, vp)
                    for lane in range(2):
                        L = lane_state[lane]
                        dc, s = L["dc"], L["s"]
                        ez = "pool" if lane == 0 else "dve"
                        eop = w.pool if lane == 0 else w.dve
                        zsl = 0 if lane == 0 else 2
                        into_z = ni == GSZ - 1
                        from_z = g > 0 and ni == 0
                        wl = [L["vp"], L["vz"]]
                        if g == 0 and ni == GSZ - 1:
                            wl += zz_free[zb_]
                        vz = (ez, eop(lambda e, s=s, ni=ni, dc=dc, into_z=into_z, zsl=zsl,
                                      from_z=from_z, sbz=sbz: e.tensor_tensor(
                            out=(sbz[:, dc * T2:(dc + 1) * T2] if into_z
                                 else sb_zacc[zsl + (ni + 1) % 2][:]),
                            in0=(sbz[:, dc * T2:(dc + 1) * T2] if from_z
                                 else sb_zacc[zsl + ni % 2][:]),
                            in1=sb_p2[s][:], op=ALU.add),
                            waits=wl))
                        p2_free[s] = vz
                        L["vz"] = vz
                for lane in range(2):
                    L = lane_state[lane]
                    dl2_free[L["s"]] = [("act", L["act_last"])]
                    vz_carry[L["dc"]] = L["vz"]
            repC_free = [("dve", w.cnt["dve"]), ("pool", w.cnt["pool"])]
        global_state["vz_fin"] = list(vz_carry)
        return global_state

    def emit_statsrows(t, gs):
        zb_ = t % 2
        sbz = sb_zz[zb_]
        st2 = 0
        for dc in range(DC):
            s = dc % 2
            vzb = w.act(lambda e, dc=dc, sbz=sbz: e.copy(
                out=sb_zb[0][:], in_=sbz[:, dc * T2:(dc + 1) * T2]),
                waits=[gs["vz_fin"][dc], ("pe", zb_free[0])])
            vz2b = w.act(lambda e, dc=dc, sbz=sbz: e.square(
                out=sb_z2b[0][:], in_=sbz[:, dc * T2:(dc + 1) * T2]),
                waits=[("act", vzb), ("pe", zb_free[1])])
            wl = [("act", vz2b)]
            if dc == 0:
                wl += bank_free[2] + bank_free[3] + [("dS", ld["onescol"] if t == 0 else 0)]
            stz = w.pe(lambda e, dc=dc: e.matmul(
                bank(2, 1), sb_onescol[:], sb_zb[0][:],
                start=(dc == 0), stop=(dc == DC - 1), skip_group_check=True),
                waits=wl)
            st2 = w.pe(lambda e, dc=dc: e.matmul(
                bank(3, 1), sb_onescol[:], sb_z2b[0][:],
                start=(dc == 0), stop=(dc == DC - 1), skip_group_check=True),
                waits=[])
            zb_free[0] = stz
            zb_free[1] = st2
        vm = w.dve(lambda e: e.tensor_scalar(
            out=rr(0, 0), in0=bank(2, 1), scalar1=1.0 / D, scalar2=None, op0=ALU.mult),
            waits=[("pe", st2)] + rows_free)
        vq = w.dve(lambda e: e.tensor_scalar(
            out=rr(0, 1), in0=bank(3, 1), scalar1=1.0 / D, scalar2=None, op0=ALU.mult),
            waits=[("dve", vm)])
        vt = w.dve(lambda e: e.tensor_tensor(out=rr(0, 2), in0=rr(0, 0), in1=rr(0, 0), op=ALU.mult),
                   waits=[("dve", vq)])
        vv = w.dve(lambda e: e.tensor_tensor(out=rr(32, 0), in0=rr(0, 1), in1=rr(0, 2), op=ALU.subtract),
                   waits=[("dve", vt)])
        vve = w.dve(lambda e: e.tensor_scalar(out=rr(32, 1), in0=rr(32, 0), scalar1=EPS,
                                              scalar2=None, op0=ALU.add),
                    waits=[("dve", vv)])
        vsq = w.act(lambda e: e.activation(out=rr(32, 2), in_=rr(32, 1), func=AF.Sqrt),
                    waits=[("dve", vve)])
        vr = w.dve(lambda e: e.reciprocal(out=rr(0, 1), in_=rr(32, 2)), waits=[("act", vsq)])
        vmr = w.dve(lambda e: e.tensor_tensor(out=rr(0, 2), in0=rr(0, 0), in1=rr(0, 1), op=ALU.mult),
                    waits=[("dve", vr)])
        bank_free[2] = [("dve", vm)]
        bank_free[3] = [("dve", vq)]
        gs["vr"] = vr
        gs["vmr"] = vmr

    def emit_ffnA(t, gs):
        nonlocal_ = None
        zb_ = t % 2
        sbz = sb_zz[zb_]
        vr, vmr = gs["vr"], gs["vmr"]
        mmr = w.pe(lambda e: e.matmul(bank(4), sb_onesrow[:], rr(0, 1), start=True, stop=True),
                   waits=bank_free[4] + [("dve", vr),
                                         ("dS", ld["onesrow"] if t == 0 else 0)])
        mmm = w.pe(lambda e: e.matmul(bank(5), sb_onesrow[:], rr(0, 2), start=True, stop=True),
                   waits=bank_free[5] + [("dve", vmr)])
        gs["rows_pe"] = mmm
        x1_last = 0
        for dc in range(DC):
            s = dc % 2
            e1 = w.dve(lambda e, dc=dc, s=s, sbz=sbz: e.tensor_tensor(
                out=sb_xo[s][:], in0=sbz[:, dc * T2:(dc + 1) * T2], in1=bank(4),
                op=ALU.mult),
                waits=[("pe", mmr)] + xo_free[s] + (x1bf_free if dc < 2 else []))
            x1_last = w.dve(lambda e, dc=dc, s=s: e.tensor_tensor(
                out=sb_x1bf[:, dc * T2:(dc + 1) * T2], in0=sb_xo[s][:], in1=bank(5),
                op=ALU.subtract),
                waits=[("pe", mmm), ("dve", e1)])
            xo_free[s] = [("dve", x1_last)]
        bank_free[4] = [("dve", x1_last)]
        bank_free[5] = [("dve", x1_last)]
        gs["x1_last"] = x1_last
        gu_last = 0
        for m in range(HM):
            g_mm = 0
            u_mm = 0
            for hf in range(2):
                wgr = w.dma(f"dA{hf}", sb_wg[hf][:],
                            wbf_g[:, (m * DC + hf * 8) * 128:(m * DC + hf * 8 + 8) * 128],
                            waits=[("pe", wg_free[hf])])
                wur = w.dma(f"dB{hf}", sb_wu[hf][:],
                            wbf_u[:, (m * DC + hf * 8) * 128:(m * DC + hf * 8 + 8) * 128],
                            waits=[("pe", wu_free[hf])])
                for kc8 in range(8):
                    kc = hf * 8 + kc8
                    wl = []
                    if kc8 == 0:
                        wl = [(f"dA{hf}", wgr), ("dve", x1_last)]
                        if kc == 0:
                            wl += bank_free[4]
                    g_mm = w.pe(lambda e, m=m, kc=kc, hf=hf, kc8=kc8: e.matmul(
                        bank(4), sb_wg[hf][:, kc8 * 128:(kc8 + 1) * 128],
                        sb_x1bf[:, kc * T2:(kc + 1) * T2],
                        start=(kc == 0), stop=(kc == DC - 1)),
                        waits=wl)
                wg_free[hf] = g_mm
            for hf in range(2):
                wur2 = 0
                for kc8 in range(8):
                    kc = hf * 8 + kc8
                    wl = []
                    if kc8 == 0:
                        wl = [(f"dB{hf}", w.cnt[f"dB{hf}"])]
                        if kc == 0:
                            wl += bank_free[5]
                    u_mm = w.pe(lambda e, m=m, kc=kc, hf=hf, kc8=kc8: e.matmul(
                        bank(5), sb_wu[hf][:, kc8 * 128:(kc8 + 1) * 128],
                        sb_x1bf[:, kc * T2:(kc + 1) * T2],
                        start=(kc == 0), stop=(kc == DC - 1)),
                        waits=wl)
                wu_free[hf] = u_mm
            vg = w.act(lambda e, m=m: e.activation(
                out=sb_gt[0][:], in_=bank(4), func=AF.Silu, scale=sb_sg[:, m:m + 1]),
                waits=[("pe", g_mm), ("pool", gt_free[0]),
                       ("dS", ld["sg"] if t == 0 and m == 0 else 0)])
            bank_free[4] = [("act", vg)]
            vup = w.act(lambda e, m=m: e.activation(
                out=sb_ut[0][:], in_=bank(5), func=AF.Identity, scale=sb_su[:, m:m + 1]),
                waits=[("pe", u_mm),
                       ("dS", ld["su"] if t == 0 and m == 0 else 0)])
            bank_free[5] = [("act", vup)]
            wl = [("act", vup)]
            if m < 2:
                wl += gu_free
            gu_last = w.pool(lambda e, m=m: e.tensor_tensor(
                out=sb_gu[:, m * T2:(m + 1) * T2], in0=sb_gt[0][:], in1=sb_ut[0][:],
                op=ALU.mult),
                waits=wl)
            gt_free[0] = gu_last
        gs["gu_last"] = gu_last

    def emit_ffnB(t, gs):
        zb_ = t % 2
        sbz = sb_zz[zb_]
        gu_last = gs["gu_last"]
        st4 = 0
        z2_fin = [None] * DC
        for dc in range(DC):
            s = dc % 2
            d_mm = 0
            for hf in range(2):
                wdr = w.dma(f"dE{hf}", sb_wd[hf][:],
                            wbf_d[:, (dc * HM + hf * 22) * 128:(dc * HM + hf * 22 + 22) * 128],
                            waits=[("pe", wd_free[hf])])
                for kh22 in range(22):
                    kh = hf * 22 + kh22
                    wl = []
                    if kh22 == 0:
                        wl = [(f"dE{hf}", wdr), ("pool", gu_last)]
                        if kh == 0:
                            wl += bank_free[6]
                    d_mm = w.pe(lambda e, dc=dc, kh=kh, hf=hf, kh22=kh22: e.matmul(
                        bank(6), sb_wd[hf][:, kh22 * 128:(kh22 + 1) * 128],
                        sb_gu[:, kh * T2:(kh + 1) * T2],
                        start=(kh == 0), stop=(kh == HM - 1)),
                        waits=wl)
                wd_free[hf] = d_mm
            z2 = w.dve(lambda e, dc=dc, sbz=sbz: e.scalar_tensor_tensor(
                out=sbz[:, dc * T2:(dc + 1) * T2], in0=bank(6), scalar=sb_sd[:, dc:dc + 1],
                in1=sb_x1bf[:, dc * T2:(dc + 1) * T2], op0=ALU.mult, op1=ALU.add),
                waits=[("pe", d_mm), ("dS", ld["sd"] if t == 0 and dc == 0 else 0),
                       ("dve", gs["x1_last"])])
            z2_fin[dc] = z2
            gs["x1bf_consumed"] = z2
            bank_free[6] = [("dve", z2)]
            vzb = w.act(lambda e, dc=dc, sbz=sbz: e.copy(
                out=sb_zb[0][:], in_=sbz[:, dc * T2:(dc + 1) * T2]),
                waits=[("dve", z2), ("pe", zb_free[0])])
            vz2b = w.act(lambda e, dc=dc, sbz=sbz: e.square(
                out=sb_z2b[0][:], in_=sbz[:, dc * T2:(dc + 1) * T2]),
                waits=[("act", vzb), ("pe", zb_free[1])])
            wl = [("act", vz2b)]
            if dc == 0:
                wl += bank_free[4] + bank_free[5]
            stz = w.pe(lambda e, dc=dc: e.matmul(
                bank(4, 1), sb_onescol[:], sb_zb[0][:],
                start=(dc == 0), stop=(dc == DC - 1), skip_group_check=True),
                waits=wl)
            st4 = w.pe(lambda e, dc=dc: e.matmul(
                bank(5, 1), sb_onescol[:], sb_z2b[0][:],
                start=(dc == 0), stop=(dc == DC - 1), skip_group_check=True),
                waits=[])
            zb_free[0] = stz
            zb_free[1] = st4
        vm2 = w.dve(lambda e: e.tensor_scalar(
            out=rr(0, 0), in0=bank(4, 1), scalar1=1.0 / D, scalar2=None, op0=ALU.mult),
            waits=[("pe", st4), ("pe", gs["rows_pe"])])
        vq2 = w.dve(lambda e: e.tensor_scalar(
            out=rr(0, 1), in0=bank(5, 1), scalar1=1.0 / D, scalar2=None, op0=ALU.mult),
            waits=[("dve", vm2)])
        vt2 = w.dve(lambda e: e.tensor_tensor(out=rr(0, 2), in0=rr(0, 0), in1=rr(0, 0), op=ALU.mult),
                    waits=[("dve", vq2)])
        vv2 = w.dve(lambda e: e.tensor_tensor(out=rr(32, 0), in0=rr(0, 1), in1=rr(0, 2), op=ALU.subtract),
                    waits=[("dve", vt2)])
        vve2 = w.dve(lambda e: e.tensor_scalar(out=rr(32, 1), in0=rr(32, 0), scalar1=EPS,
                                               scalar2=None, op0=ALU.add),
                     waits=[("dve", vv2)])
        vsq2 = w.act(lambda e: e.activation(out=rr(32, 2), in_=rr(32, 1), func=AF.Sqrt),
                     waits=[("dve", vve2)])
        vr2 = w.dve(lambda e: e.reciprocal(out=rr(0, 1), in_=rr(32, 2)), waits=[("act", vsq2)])
        vmr2 = w.dve(lambda e: e.tensor_tensor(out=rr(0, 2), in0=rr(0, 0), in1=rr(0, 1), op=ALU.mult),
                     waits=[("dve", vr2)])
        bank_free[4] = [("dve", vm2)]
        bank_free[5] = [("dve", vq2)]
        mmr3 = w.pe(lambda e: e.matmul(bank(6), sb_onesrow[:], rr(0, 1), start=True, stop=True),
                    waits=bank_free[6] + [("dve", vr2)])
        mmm3 = w.pe(lambda e: e.matmul(bank(7), sb_onesrow[:], rr(0, 2), start=True, stop=True),
                    waits=bank_free[7] + [("dve", vmr2)])
        o_last = [0, 0]
        c0 = t * T2
        for dc in range(DC):
            s = dc % 2
            e1 = w.dve(lambda e, dc=dc, s=s, sbz=sbz: e.tensor_tensor(
                out=sb_xo[s][:], in0=sbz[:, dc * T2:(dc + 1) * T2], in1=bank(6),
                op=ALU.mult),
                waits=[("pe", mmr3)] + xo_free[s])
            e2 = w.dve(lambda e, dc=dc, s=s: e.tensor_tensor(
                out=sb_xo[2][:], in0=sb_xo[s][:], in1=bank(7),
                op=ALU.subtract),
                waits=[("pe", mmm3), ("dve", e1)] + xo_free[2])
            od = w.dma(f"dO{s}", out_T[dc * 128:(dc + 1) * 128, c0:c0 + T2], sb_xo[2][:],
                       waits=[("dve", e2)])
            xo_free[s] = [("dve", e2)]
            xo_free[2] = [(f"dO{s}", od)]
            o_last[s] = e2
        bank_free[6] = [("dve", max(o_last))]
        bank_free[7] = [("dve", max(o_last))]
        zz_free[zb_] = [("dve", max(o_last)),
                        ("act", w.cnt["act"])]
        nonlocal_rows = [("pe", mmm3)]
        return nonlocal_rows

    for t in range(nt2):
        gs = emit_scanA(t)
        rep_mm_last = gs["rep_mm_last"]
        if t > 0:
            rows_free = emit_ffnB(t - 1, H[t - 1])
            x1bf_free = [("dve", H[t - 1].get("x1bf_consumed", 0))]
            gu_free = [("pe", w.cnt["pe"])]
        emit_statsrows(t, gs)
        emit_ffnA(t, gs)
        H[t] = gs
    rows_free = emit_ffnB(nt2 - 1, H[nt2 - 1])

    r2_stack.close()

    w.barrier("sync", dma_all())

    with nc.Block() as block:
        @block.sync
        def _(e):
            for f in w.ops["sync"]:
                f(e)

        @block.tensor
        def _(e):
            for f in w.ops["tensor"]:
                f(e)

        @block.scalar
        def _(e):
            for f in w.ops["scalar"]:
                f(e)

        @block.vector
        def _(e):
            for f in w.ops["vector"]:
                f(e)

        @block.gpsimd
        def _(e):
            for f in w.ops["gpsimd"]:
                f(e)

    stack.close()
    return nc


def _host_inputs(L0_attn_x, L0_attn_W_in, L0_attn_W_dt, L0_attn_b_dt, L0_attn_A_log,
                 L0_attn_D, L0_ffn_wq_gate, L0_ffn_s_gate, L0_ffn_wq_up, L0_ffn_s_up,
                 L0_ffn_wq_down, L0_ffn_s_down, l_core):
    bf16 = ml_dtypes.bfloat16
    x = np.asarray(L0_attn_x, dtype=np.float32)
    w_inT = np.ascontiguousarray(np.asarray(L0_attn_W_in, np.float32).T)
    w_dtT = np.ascontiguousarray(np.asarray(L0_attn_W_dt, np.float32).T)
    b_dt_t = np.ascontiguousarray(np.asarray(L0_attn_b_dt, np.float32).reshape(DC, 128).T)
    a_t = np.ascontiguousarray(
        np.asarray(L0_attn_A_log, np.float32).reshape(DC, 128, NST)
        .transpose(1, 0, 2).reshape(128, DC * NST))
    d_t = np.ascontiguousarray(np.asarray(L0_attn_D, np.float32).reshape(DC, 128).T)
    s_g = np.ascontiguousarray(np.asarray(L0_ffn_s_gate, np.float32).reshape(HM, 128).T)
    s_u = np.ascontiguousarray(np.asarray(L0_ffn_s_up, np.float32).reshape(HM, 128).T)
    s_d = np.ascontiguousarray(np.asarray(L0_ffn_s_down, np.float32).reshape(DC, 128).T)

    wq_gate = np.asarray(L0_ffn_wq_gate, np.int32).reshape(HM, 128, DC, 128)
    wq_g = np.ascontiguousarray(wq_gate.transpose(3, 0, 2, 1).reshape(128, HM * DC * 128))
    wq_up = np.asarray(L0_ffn_wq_up, np.int32).reshape(HM, 128, DC, 128)
    wq_u = np.ascontiguousarray(wq_up.transpose(3, 0, 2, 1).reshape(128, HM * DC * 128))
    wq_down = np.asarray(L0_ffn_wq_down, np.int32).reshape(DC, 128, HM, 128)
    wq_d = np.ascontiguousarray(wq_down.transpose(3, 0, 2, 1).reshape(128, DC * HM * 128))

    sel = np.zeros((32, 2 * NST * 128), dtype=bf16)
    for r in range(32):
        sel[r, r * 128:(r + 1) * 128] = bf16(1.0)
    onescol = np.ones((128, 1), dtype=bf16)
    onesrow = np.ones((1, 128), dtype=np.float32)

    in_maps = []
    for c in range(NCORES):
        b, q = divmod(c, 4)
        xs = np.ascontiguousarray(x[b, q * l_core:(q + 1) * l_core, :].T)
        msk = np.zeros((128, NCORES), dtype=np.float32)
        for j in range(4 * b, c):
            msk[:, j] = 1.0
        imsk = np.ascontiguousarray(1.0 - msk)
        in_maps.append({
            "x_T": xs, "w_inT": w_inT, "w_dtT": w_dtT, "b_dt_t": b_dt_t,
            "a_t": a_t, "d_t": d_t, "s_g": s_g, "s_u": s_u, "s_d": s_d,
            "wq_g": wq_g, "wq_u": wq_u, "wq_d": wq_d, "msk": msk, "imsk": imsk,
            "sel": sel, "onescol": onescol, "onesrow": onesrow,
        })
    return in_maps


def kernel(**inputs):
    l_core = 4096
    in_maps = _host_inputs(l_core=l_core, **inputs)
    nc = build_nc(l_core)
    res = run_bass_kernel_spmd(nc, in_maps, core_ids=list(range(NCORES)))
    out = np.empty((2, 16384, D), dtype=np.float32)
    for c in range(NCORES):
        b, q = divmod(c, 4)
        out[b, q * l_core:(q + 1) * l_core, :] = res.results[c]["out_T"].T
    return out


# version 07 (final)
# speedup vs baseline: 1.2572x; 1.0020x over previous
import numpy as np
import ml_dtypes
from contextlib import ExitStack
import concourse.bass as bass
import concourse.mybir as mybir
from concourse.bass_utils import run_bass_kernel_spmd

F32 = mybir.dt.float32
BF16 = mybir.dt.bfloat16
I32 = mybir.dt.int32
AF = mybir.ActivationFunctionType
ALU = mybir.AluOpType
AX = mybir.AxisListType.X

D = 2048
NST = 16
RDT = 64
HID = 5632
DC = D // 128
HM = HID // 128
NCORES = 8
EPS = 1e-5
T1 = 1024
T2 = 512
WCH = 4096


class Walk:

    def __init__(self, nc, sems):
        self.nc = nc
        self.sems = sems
        self.cnt = {k: 0 for k in sems}
        self.ops = {k: [] for k in ("sync", "tensor", "scalar", "vector", "gpsimd")}

    def _emit(self, stream, own, waits, fn, inc_amt=1, self_serialize=False):
        best = {}
        for item in waits:
            if item is None:
                continue
            s, v = item
            if v and v > 0 and v > best.get(s, 0):
                best[s] = v
        if self_serialize and own and self.cnt[own] > 0:
            if best.get(own, 0) < self.cnt[own]:
                best[own] = self.cnt[own]
        clean = sorted(best.items())
        sems = self.sems
        own_sem = sems[own] if own else None

        def run(e, waits=tuple(clean), fn=fn, own_sem=own_sem, inc_amt=inc_amt):
            for s, v in waits:
                e.wait_ge(sems[s], v)
            ins = fn(e)
            if own_sem is not None:
                ins.then_inc(own_sem, inc_amt)

        self.ops[stream].append(run)
        if own:
            self.cnt[own] += inc_amt
            return self.cnt[own]
        return None

    def dma(self, q, out, in_, waits=()):
        return self._emit("sync", q, waits,
                          lambda e: e.dma_start(out=out, in_=in_), 16)

    def pe(self, fn, waits=()):
        return self._emit("tensor", "pe", waits, fn)

    def act(self, fn, waits=()):
        return self._emit("scalar", "act", waits, fn, self_serialize=True)

    def dve(self, fn, waits=()):
        return self._emit("vector", "dve", waits, fn, self_serialize=True)

    def pool(self, fn, waits=()):
        return self._emit("gpsimd", "pool", waits, fn, self_serialize=True)

    def cc(self, fn, waits=()):
        return self._emit("gpsimd", "cc", waits, fn, 1)

    def barrier(self, stream, waits):
        clean = tuple((s, v) for (s, v) in waits if v and v > 0)
        sems = self.sems

        def run(e, waits=clean):
            for s, v in waits:
                e.wait_ge(sems[s], v)

        self.ops[stream].append(run)


def ep_wait(eng, val):
    return (eng, val)


def build_nc(l_core):
    nt1 = l_core // T1
    nt2 = l_core // T2
    nc = bass.Bass("TRN2", target_bir_lowering=False)

    x_T = nc.dram_tensor("x_T", [D, l_core], F32, kind="ExternalInput").ap()
    w_inT = nc.dram_tensor("w_inT", [D, 96], F32, kind="ExternalInput").ap()
    w_dtT = nc.dram_tensor("w_dtT", [RDT, D], F32, kind="ExternalInput").ap()
    b_dt_t = nc.dram_tensor("b_dt_t", [128, DC], F32, kind="ExternalInput").ap()
    a_t = nc.dram_tensor("a_t", [128, DC * NST], F32, kind="ExternalInput").ap()
    d_t = nc.dram_tensor("d_t", [128, DC], F32, kind="ExternalInput").ap()
    s_g = nc.dram_tensor("s_g", [128, HM], F32, kind="ExternalInput").ap()
    s_u = nc.dram_tensor("s_u", [128, HM], F32, kind="ExternalInput").ap()
    s_d = nc.dram_tensor("s_d", [128, DC], F32, kind="ExternalInput").ap()
    wq_g = nc.dram_tensor("wq_g", [128, HM * DC * 128], I32, kind="ExternalInput").ap()
    wq_u = nc.dram_tensor("wq_u", [128, HM * DC * 128], I32, kind="ExternalInput").ap()
    wq_d = nc.dram_tensor("wq_d", [128, DC * HM * 128], I32, kind="ExternalInput").ap()
    msk_in = nc.dram_tensor("msk", [128, NCORES], F32, kind="ExternalInput").ap()
    imsk_in = nc.dram_tensor("imsk", [128, NCORES], F32, kind="ExternalInput").ap()
    sel_in = nc.dram_tensor("sel", [32, 2 * NST * 128], BF16, kind="ExternalInput").ap()
    onescol_in = nc.dram_tensor("onescol", [128, 1], BF16, kind="ExternalInput").ap()
    onesrow_in = nc.dram_tensor("onesrow", [1, 128], F32, kind="ExternalInput").ap()

    out_T = nc.dram_tensor("out_T", [D, l_core], F32, kind="ExternalOutput").ap()

    dstash = nc.dram_tensor("dstash", [D, l_core], F32)
    bcstash = nc.dram_tensor("bcstash", [32, l_core], BF16)
    bstash = nc.dram_tensor("bstash", [128, DC * NST * l_core], BF16)
    wbf_g = nc.dram_tensor("wbf_g", [128, HM * DC * 128], BF16)
    wbf_u = nc.dram_tensor("wbf_u", [128, HM * DC * 128], BF16)
    wbf_d = nc.dram_tensor("wbf_d", [128, DC * HM * 128], BF16)
    gp_src = nc.dram_tensor("gp_src", [256, DC * NST], F32)
    gp_dst = nc.dram_tensor("gp_dst", [256 * NCORES, DC * NST], F32)

    stack = ExitStack()
    sems = {}
    dma_sem_names = (["dS", "dT", "dV", "dP"]
                     + [f"d{k}{i}" for k in "CIOXDABEJ" for i in range(2)]
                     + [f"dK{i}{j}" for i in range(2) for j in range(2)]
                     + ["dZ0", "dZ1"])
    for name in dma_sem_names + ["pe", "act", "dve", "pool", "cc"]:
        sems[name] = stack.enter_context(nc.semaphore(name))
    w = Walk(nc, sems)

    E = stack.enter_context
    sb_bdt = E(nc.sbuf_tensor("sb_bdt", [128, DC], F32))
    sb_Apos = E(nc.sbuf_tensor("sb_Apos", [128, DC * NST], F32))
    sb_Aneg = E(nc.sbuf_tensor("sb_Aneg", [128, DC * NST], F32))
    sb_Dt = E(nc.sbuf_tensor("sb_Dt", [128, DC], F32))
    sb_D1 = E(nc.sbuf_tensor("sb_D1", [128, DC], F32))
    sb_sg = E(nc.sbuf_tensor("sb_sg", [128, HM], F32))
    sb_su = E(nc.sbuf_tensor("sb_su", [128, HM], F32))
    sb_sd = E(nc.sbuf_tensor("sb_sd", [128, DC], F32))
    sb_msk = E(nc.sbuf_tensor("sb_msk", [128, NCORES], F32))
    sb_imsk = E(nc.sbuf_tensor("sb_imsk", [128, NCORES], F32))
    sb_sel = E(nc.sbuf_tensor("sb_sel", [32, 2 * NST * 128], BF16))
    sb_onescol = E(nc.sbuf_tensor("sb_onescol", [128, 1], BF16))
    sb_onesrow = E(nc.sbuf_tensor("sb_onesrow", [1, 128], F32))
    sb_state = E(nc.sbuf_tensor("sb_state", [128, DC * NST], F32))

    ps8 = E(nc.psum_tensor("ps8", [128, 4096], F32))

    def bank(i, pn=128, c0=0, cn=512):
        return ps8[0:pn, i * 512 + c0:i * 512 + c0 + cn]

    def sdma(out, in_):
        return w.dma("dS", out, in_, waits=[("dS", w.cnt["dS"])])

    def dma_all():
        return [(n, w.cnt[n]) for n in dma_sem_names]

    ld = {}
    ld["bdt"] = sdma(sb_bdt[:], b_dt_t)
    ld["alog"] = sdma(sb_Apos[:], a_t)
    ld["dt"] = sdma(sb_Dt[:], d_t)
    ld["sg"] = sdma(sb_sg[:], s_g)
    ld["su"] = sdma(sb_su[:], s_u)
    ld["sd"] = sdma(sb_sd[:], s_d)
    ld["msk"] = sdma(sb_msk[:], msk_in)
    ld["imsk"] = sdma(sb_imsk[:], imsk_in)
    ld["sel"] = sdma(sb_sel[:], sel_in)
    ld["onescol"] = sdma(sb_onescol[:], onescol_in)
    ld["onesrow"] = sdma(sb_onesrow[:], onesrow_in)

    v_apos = w.act(lambda e: e.activation(out=sb_Apos[:], in_=sb_Apos[:], func=AF.Exp),
                   waits=[("dS", ld["alog"])])
    v_aneg = w.pool(lambda e: e.tensor_scalar(out=sb_Aneg[:], in0=sb_Apos[:],
                                              scalar1=-1.0, scalar2=None, op0=ALU.mult),
                    waits=[("act", v_apos)])
    v_d1 = w.pool(lambda e: e.tensor_scalar(out=sb_D1[:], in0=sb_Dt[:],
                                            scalar1=1.0, scalar2=None, op0=ALU.add),
                  waits=[("dS", ld["dt"])])
    v_state0 = w.dve(lambda e: e.memset(sb_state[:], 0.0))

    conv_stack = ExitStack()
    cv_i = [conv_stack.enter_context(nc.sbuf_tensor(f"cv_i{i}", [128, WCH], I32)) for i in range(2)]
    cv_b = [conv_stack.enter_context(nc.sbuf_tensor(f"cv_b{i}", [128, WCH], BF16)) for i in range(2)]
    nch = HM * DC * 128 // WCH
    cv_in_done = [0, 0]
    cv_ts_done = [0, 0]
    cv_out_done = [0, 0]
    ci = 0
    for src, dst in ((wq_g, wbf_g), (wq_u, wbf_u), (wq_d, wbf_d)):
        for ch in range(nch):
            s = ci % 2
            cv_in_done[s] = w.dma(f"dC{s}", cv_i[s][:], src[:, ch * WCH:(ch + 1) * WCH],
                                  waits=[("dve", cv_ts_done[s])])
            cv_ts_done[s] = w.dve(
                lambda e, s=s: e.tensor_scalar(out=cv_b[s][:], in0=cv_i[s][:],
                                               scalar1=128.0, scalar2=None,
                                               op0=ALU.subtract),
                waits=[(f"dC{s}", cv_in_done[s]), (f"dI{s}", cv_out_done[s])])
            cv_out_done[s] = w.dma(f"dI{s}", dst[:, ch * WCH:(ch + 1) * WCH], cv_b[s][:],
                                   waits=[("dve", cv_ts_done[s])])
            ci += 1
    conv_done_dve = w.cnt["dve"]
    conv_stack.close()

    w.barrier("sync", [("dve", conv_done_dve)] + dma_all())

    r1_stack = ExitStack()
    R = r1_stack.enter_context
    sb_winT = R(nc.sbuf_tensor("sb_winT", [128, DC * 96], F32))
    sb_wdtT = R(nc.sbuf_tensor("sb_wdtT", [RDT, D], F32))
    sb_G = R(nc.sbuf_tensor("sb_G", [128, DC * NST], F32))
    sb_P = R(nc.sbuf_tensor("sb_P", [128, DC * NST], F32))
    sb_hin = R(nc.sbuf_tensor("sb_hin", [128, DC * NST], F32))
    sb_ht = R(nc.sbuf_tensor("sb_ht", [128, DC * NST], F32))
    sb_gm = R(nc.sbuf_tensor("sb_gm", [128, DC * NST], F32))
    sb_pm = R(nc.sbuf_tensor("sb_pm", [128, DC * NST], F32))
    sb_Gj = [R(nc.sbuf_tensor(f"sb_Gj{i}", [128, DC * NST], F32)) for i in range(2)]
    sb_Pj = [R(nc.sbuf_tensor(f"sb_Pj{i}", [128, DC * NST], F32)) for i in range(2)]
    sb_sumdall = R(nc.sbuf_tensor("sb_sumdall", [128, DC * 16], F32))
    sb_sumdtot = R(nc.sbuf_tensor("sb_sumdtot", [128, DC], F32))
    sb_d1l = R(nc.sbuf_tensor("sb_d1l", [128, DC * T1], F32))
    sb_u1l = R(nc.sbuf_tensor("sb_u1l", [128, DC * T1], F32))
    sb_xs = [R(nc.sbuf_tensor(f"sb_xs{i}", [128, T1], F32)) for i in range(2)]
    sb_dlr = R(nc.sbuf_tensor("sb_dlr", [RDT, T1], F32))
    sb_bcb1 = R(nc.sbuf_tensor("sb_bcb1", [32, T1], BF16))
    sb_rep1 = [R(nc.sbuf_tensor(f"sb_rep1_{i}", [128, T1], BF16)) for i in range(1)]
    sb_sa = [R(nc.sbuf_tensor(f"sb_sa{i}", [128, T1], BF16)) for i in range(2)]
    sb_sb = [R(nc.sbuf_tensor(f"sb_sb{i}", [128, T1], F32)) for i in range(2)]
    sb_sbc = [R(nc.sbuf_tensor(f"sb_sbc{i}", [128, T1], BF16)) for i in range(2)]
    sb_sh = R(nc.sbuf_tensor("sb_sh", [128, T1], BF16))
    sb_S = R(nc.sbuf_tensor("sb_S", [128, T1], F32))
    sb_Gparts = R(nc.sbuf_tensor("sb_Gparts", [128, 4 * DC * NST], F32))
    sb_scar = R(nc.sbuf_tensor("sb_scar", [128, DC], F32))
    sb_wt = R(nc.sbuf_tensor("sb_wt", [128, DC * NST], BF16))

    for dc in range(DC):
        ld["winT"] = sdma(sb_winT[:, dc * 96:(dc + 1) * 96],
                          w_inT[dc * 128:(dc + 1) * 128, :])
    ld["wdtT"] = sdma(sb_wdtT[:], w_dtT)

    xs_free = [[], []]
    a_free = [0, 0]
    b_free = [0, 0]
    h_free = [0, 0]
    rep1_free = [0, 0]
    bank_free = [[] for _ in range(8)]
    v_softadd = 0
    bst_done = [0, 0]
    ev_bc_last = 0
    dlr_free_pe = 0
    bcb1_free_pe = 0
    stash_last = 0

    for t in range(nt1):
        c0 = t * T1
        pool_cut = w.cnt["pool"]
        lr_mm = [0, 0]
        bc_mm = [0, 0]
        for dc in range(DC):
            s = dc % 2
            xr = w.dma(f"dX{s}", sb_xs[s][:], x_T[dc * 128:(dc + 1) * 128, c0:c0 + T1],
                       waits=xs_free[s])
            for cc in range(2):
                first = dc == 0
                last = dc == DC - 1
                wl = [(f"dX{s}", xr)]
                if first:
                    wl += [("dS", ld["winT"])] + bank_free[cc] + bank_free[2 + cc]
                lr_mm[cc] = w.pe(
                    lambda e, s=s, cc=cc, dc=dc, first=first, last=last: e.matmul(
                        bank(cc, RDT), sb_winT[:, dc * 96:dc * 96 + RDT],
                        sb_xs[s][:, cc * 512:(cc + 1) * 512],
                        start=first, stop=last, skip_group_check=True),
                    waits=wl)
                bc_mm[cc] = w.pe(
                    lambda e, s=s, cc=cc, dc=dc, first=first, last=last: e.matmul(
                        bank(2 + cc, 32), sb_winT[:, dc * 96 + RDT:(dc + 1) * 96],
                        sb_xs[s][:, cc * 512:(cc + 1) * 512],
                        start=first, stop=last, skip_group_check=True),
                    waits=[])
            xs_free[s] = [("pe", bc_mm[1])]
        ev_lr = [0, 0]
        ev_bc = [0, 0]
        for cc in range(2):
            ev_lr[cc] = w.act(
                lambda e, cc=cc: e.copy(out=sb_dlr[:, cc * 512:(cc + 1) * 512],
                                        in_=bank(cc, RDT)),
                waits=[("pe", lr_mm[cc]), ("pe", dlr_free_pe)])
            ev_bc[cc] = w.act(
                lambda e, cc=cc: e.copy(out=sb_bcb1[:, cc * 512:(cc + 1) * 512],
                                        in_=bank(2 + cc, 32)),
                waits=[("pe", bc_mm[cc]), ("pe", bcb1_free_pe)])
        bank_free[0] = [("act", ev_lr[0])]
        bank_free[1] = [("act", ev_lr[1])]
        bank_free[2] = [("act", ev_bc[0])]
        bank_free[3] = [("act", ev_bc[1])]
        ev_bc_last = max(ev_bc)
        w.dma("dT", bcstash[:, c0:c0 + T1], sb_bcb1[:],
              waits=[("act", ev_bc_last), ("dT", w.cnt["dT"])])

        for dc in range(DC):
            for cc in range(2):
                bki = 4 + cc
                pcol = dc * T1 + cc * 512
                mmv = w.pe(
                    lambda e, dc=dc, cc=cc, bki=bki: e.matmul(
                        bank(bki), sb_wdtT[:, dc * 128:(dc + 1) * 128],
                        sb_dlr[:, cc * 512:(cc + 1) * 512],
                        start=True, stop=True),
                    waits=bank_free[bki] + [("act", max(ev_lr)),
                                            ("dS", ld["wdtT"] if t == 0 and dc == 0 else 0)])
                dlr_free_pe = mmv
                v1 = w.act(lambda e, dc=dc, bki=bki: e.activation(
                    out=sb_S[:, 0:512], in_=bank(bki), func=AF.Abs,
                    bias=sb_bdt[:, dc:dc + 1]),
                    waits=[("pe", mmv), ("dve", max(a_free[0], a_free[1])),
                           ("pool", pool_cut if dc == 0 and cc == 0 else 0),
                           ("dS", ld["bdt"] if t == 0 and dc == 0 else 0),
                           ("dT", stash_last if dc == 0 and cc == 0 else 0)])
                v2 = w.act(lambda e: e.activation(
                    out=sb_sb[1][:, 0:512], in_=sb_S[:, 0:512], func=AF.Exp,
                    scale=-1.0),
                    waits=[("act", v1)])
                v3 = w.act(lambda e: e.activation(
                    out=sb_S[:, 512:1024], in_=sb_sb[1][:, 0:512], func=AF.Ln,
                    bias=1.0),
                    waits=[("act", v2), ("pool", v_softadd)])
                v4 = w.act(lambda e, dc=dc, bki=bki: e.activation(
                    out=sb_sb[1][:, 512:1024], in_=bank(bki), func=AF.Relu,
                    bias=sb_bdt[:, dc:dc + 1]),
                    waits=[("act", v3)])
                bank_free[bki] = [("act", v4)]
                v_softadd = w.pool(lambda e, pcol=pcol: e.tensor_tensor(
                    out=sb_d1l[:, pcol:pcol + 512], in0=sb_S[:, 512:1024],
                    in1=sb_sb[1][:, 512:1024], op=ALU.add),
                    waits=[("act", v4)])
        delta_done = v_softadd

        for dc in range(DC):
            w.dve(lambda e, dc=dc, t=t: e.reduce_sum(
                out=sb_sumdall[:, dc * 16 + t:dc * 16 + t + 1],
                in_=sb_d1l[:, dc * T1:(dc + 1) * T1], axis=AX),
                waits=[("pool", delta_done)])

        u_done = 0
        for dc in range(DC):
            s = dc % 2
            stash_last = w.dma("dT", dstash[dc * 128:(dc + 1) * 128, c0:c0 + T1],
                               sb_d1l[:, dc * T1:(dc + 1) * T1],
                               waits=[("pool", delta_done), ("dT", w.cnt["dT"])])
            xr = w.dma(f"dX{s}", sb_xs[s][:], x_T[dc * 128:(dc + 1) * 128, c0:c0 + T1],
                       waits=xs_free[s])
            u_done = w.pool(lambda e, dc=dc, s=s: e.tensor_tensor(
                out=sb_u1l[:, dc * T1:(dc + 1) * T1], in0=sb_d1l[:, dc * T1:(dc + 1) * T1],
                in1=sb_xs[s][:], op=ALU.mult),
                waits=[(f"dX{s}", xr), ("pool", delta_done)])
            xs_free[s] = [("pool", u_done)]

        if t == 0:
            v_scar = w.dve(lambda e: e.memset(sb_scar[:], 0.0), waits=[])
        r_done = 0
        for dc in range(DC):
            vS = w.dve(lambda e, dc=dc: e.tensor_tensor_scan(
                out=sb_S[:], data0=sb_d1l[:, dc * T1:(dc + 1) * T1],
                data1=sb_d1l[:, dc * T1:(dc + 1) * T1],
                initial=sb_scar[:, dc:dc + 1], op0=ALU.add, op1=ALU.bypass),
                waits=[("pool", u_done), ("dT", stash_last), ("act", r_done)])
            vC = w.dve(lambda e, dc=dc: e.tensor_copy(
                out=sb_scar[:, dc:dc + 1], in_=sb_S[:, T1 - 1:T1]),
                waits=[("dve", vS)])
            r_done = w.act(lambda e, dc=dc: e.activation(
                out=sb_d1l[:, dc * T1:(dc + 1) * T1], in_=sb_S[:],
                func=AF.Identity, scale=-1.0, bias=sb_scar[:, dc:dc + 1]),
                waits=[("dve", vC)])

        for n in range(NST):
            sn = n % 2
            bks = 0 if sn == 0 else 2
            mm1 = w.pe(lambda e, n=n, bks=bks: e.matmul(
                bank(bks), sb_sel[:, n * 128:(n + 1) * 128], sb_bcb1[:, 0:512],
                start=True, stop=True),
                waits=bank_free[bks] + bank_free[bks + 1] +
                      [("act", ev_bc_last), ("dS", ld["sel"] if t == 0 and n == 0 else 0)])
            mm2 = w.pe(lambda e, n=n, bks=bks: e.matmul(
                bank(bks + 1), sb_sel[:, n * 128:(n + 1) * 128], sb_bcb1[:, 512:1024],
                start=True, stop=True),
                waits=[])
            bcb1_free_pe = mm2
            evr = w.act(lambda e, bks=bks: e.copy(
                out=sb_rep1[0][:], in_=bank(bks, 128, 0, 1024)),
                waits=[("pe", mm2), ("pool", rep1_free[0])])
            bank_free[bks] = [("act", evr)]
            bank_free[bks + 1] = [("act", evr)]
            for dc in range(DC):
                col = dc * NST + n
                sa = dc % 2
                va = w.act(lambda e, dc=dc, col=col, sa=sa: e.activation(
                    out=sb_sa[sa][:], in_=sb_d1l[:, dc * T1:(dc + 1) * T1], func=AF.Exp,
                    scale=sb_Aneg[:, col:col + 1]),
                    waits=[("act", r_done), ("dve", a_free[sa]),
                           ("pool", v_aneg if t == 0 and n == 0 and dc == 0 else 0)])
                vb = w.pool(lambda e, dc=dc, sa=sa, sn=sn: e.tensor_tensor(
                    out=sb_sb[sa][:], in0=sb_u1l[:, dc * T1:(dc + 1) * T1],
                    in1=sb_rep1[0][:], op=ALU.mult),
                    waits=[("pool", u_done), ("dve", b_free[sa]), ("act", evr)])
                vcast = w.dve(lambda e, sa=sa: e.tensor_copy(
                    out=sb_sbc[sa][:], in_=sb_sb[sa][:]),
                    waits=[("pool", vb), (f"dZ{sa}", bst_done[sa])])
                bst_done[sa] = w.dma(f"dZ{sa}",
                                     bstash[:, col * l_core + c0:col * l_core + c0 + T1],
                                     sb_sbc[sa][:], waits=[("dve", vcast)])
                vg = w.dve(lambda e, sa=sa, col=col, t=t: e.scalar_tensor_tensor(
                    out=sb_sh[:], in0=sb_sa[sa][:], scalar=1.0, in1=sb_sb[sa][:],
                    op0=ALU.bypass, op1=ALU.mult,
                    accum_out=sb_Gparts[:, t * DC * NST + col:t * DC * NST + col + 1]),
                    waits=[("act", va), ("pool", vb)])
                a_free[sa] = vg
                b_free[sa] = vg
            rep1_free[0] = w.cnt["pool"]

    r1_dve = w.cnt["dve"]
    r1_pool = w.cnt["pool"]
    r1_act = w.cnt["act"]
    r1_pe = w.cnt["pe"]

    vG = w.dve(lambda e: e.memset(sb_G[:], 0.0), waits=[("dve", r1_dve)])
    for t in range(nt1):
        for dc in range(DC):
            if t == nt1 - 1:
                vsf = w.dve(lambda e, dc=dc: e.memset(sb_sumdtot[:, dc:dc + 1], 0.0),
                            waits=[])
            else:
                vsf = w.dve(lambda e, dc=dc, t=t: e.reduce_sum(
                    out=sb_sumdtot[:, dc:dc + 1],
                    in_=sb_sumdall[:, dc * 16 + t + 1:dc * 16 + nt1], axis=AX),
                    waits=[("dve", r1_dve)])
            w.act(lambda e, dc=dc: e.activation(
                out=sb_wt[:, dc * NST:(dc + 1) * NST],
                in_=sb_Aneg[:, dc * NST:(dc + 1) * NST], func=AF.Exp,
                scale=sb_sumdtot[:, dc:dc + 1]),
                waits=[("dve", vsf)])
        va_wt = w.cnt["act"]
        vmul = w.dve(lambda e, t=t: e.tensor_tensor(
            out=sb_ht[:], in0=sb_Gparts[:, t * DC * NST:(t + 1) * DC * NST],
            in1=sb_wt[:], op=ALU.mult),
            waits=[("act", va_wt), ("dve", vG)])
        vG = w.dve(lambda e: e.tensor_tensor(
            out=sb_gm[:], in0=sb_G[:], in1=sb_ht[:], op=ALU.add),
            waits=[("dve", vmul)])
        vG = w.dve(lambda e: e.tensor_copy(out=sb_G[:], in_=sb_gm[:]),
                   waits=[("dve", vG)])
    vP = 0
    for dc in range(DC):
        vsd = w.dve(lambda e, dc=dc: e.reduce_sum(
            out=sb_sumdtot[:, dc:dc + 1],
            in_=sb_sumdall[:, dc * 16:dc * 16 + nt1], axis=AX),
            waits=[("dve", r1_dve), ("act", w.cnt["act"])])
        vP = w.act(lambda e, dc=dc: e.activation(
            out=sb_P[:, dc * NST:(dc + 1) * NST],
            in_=sb_Aneg[:, dc * NST:(dc + 1) * NST], func=AF.Exp,
            scale=sb_sumdtot[:, dc:dc + 1]),
            waits=[("dve", vsd)])

    d_g = w.dma("dP", gp_src[0:128, :], sb_G[:], waits=[("dve", vG)])
    d_p = w.dma("dP", gp_src[128:256, :], sb_P[:],
                waits=[("act", vP), ("dP", d_g)])
    cc_done = w.cc(lambda e: e.collective_compute(
        "AllGather", ALU.bypass, replica_groups=[list(range(NCORES))],
        ins=[gp_src.ap().opt()], outs=[gp_dst.ap().opt()]),
        waits=[("dP", d_p), ("pool", r1_pool)])
    vH = w.dve(lambda e: e.memset(sb_hin[:], 0.0), waits=[("dve", vG)])
    gp_slot_free = [0, 0]
    for j in range(NCORES):
        s = j % 2
        gj = w.dma(f"dJ{s}", sb_Gj[s][:], gp_dst[j * 256:j * 256 + 128, :],
                   waits=[("cc", cc_done), ("dve", gp_slot_free[s]),
                          (f"dJ{s}", w.cnt[f"dJ{s}"])])
        pj = w.dma(f"dJ{s}", sb_Pj[s][:], gp_dst[j * 256 + 128:(j + 1) * 256, :],
                   waits=[(f"dJ{s}", gj)])
        v1 = w.dve(lambda e, s=s, j=j: e.tensor_scalar(
            out=sb_gm[:], in0=sb_Gj[s][:], scalar1=sb_msk[:, j:j + 1], scalar2=None,
            op0=ALU.mult),
            waits=[(f"dJ{s}", pj), ("dve", vH)])
        v2 = w.dve(lambda e, s=s, j=j: e.tensor_scalar(
            out=sb_pm[:], in0=sb_Pj[s][:], scalar1=sb_msk[:, j:j + 1],
            scalar2=sb_imsk[:, j:j + 1], op0=ALU.mult, op1=ALU.add),
            waits=[("dve", v1)])
        gp_slot_free[s] = v2
        v3 = w.dve(lambda e: e.tensor_tensor(
            out=sb_ht[:], in0=sb_hin[:], in1=sb_pm[:], op=ALU.mult),
            waits=[("dve", v2)])
        vH = w.dve(lambda e: e.tensor_tensor(
            out=sb_hin[:], in0=sb_ht[:], in1=sb_gm[:], op=ALU.add),
            waits=[("dve", v3)])
    v_state_r2 = w.dve(lambda e: e.tensor_copy(out=sb_state[:], in_=sb_hin[:]),
                       waits=[("dve", vH)])

    r1_stack.close()

    r2_stack = ExitStack()
    R = r2_stack.enter_context
    NG, GSZ = 8, 2
    sb_repC = R(nc.sbuf_tensor("sb_repC", [128, GSZ * T2], BF16))
    sb_zz = [R(nc.sbuf_tensor(f"sb_zz{i}", [128, DC * T2], F32)) for i in range(2)]
    sb_x1bf = R(nc.sbuf_tensor("sb_x1bf", [128, DC * T2], BF16))
    sb_gu = R(nc.sbuf_tensor("sb_gu", [128, HM * T2], BF16))
    sb_bcb2 = R(nc.sbuf_tensor("sb_bcb2", [32, T2], BF16))
    sb_dl2 = [R(nc.sbuf_tensor(f"sb_dl2_{i}", [128, T2], F32)) for i in range(2)]
    sb_x2 = [R(nc.sbuf_tensor(f"sb_x2_{i}", [128, T2], F32)) for i in range(2)]
    sb_a2 = [R(nc.sbuf_tensor(f"sb_a2_{i}", [128, T2], F32)) for i in range(2)]
    sb_b2 = [R(nc.sbuf_tensor(f"sb_b2_{i}", [128, 2 * T2], BF16)) for i in range(2)]
    sb_h2 = [R(nc.sbuf_tensor(f"sb_h2_{i}", [128, T2], BF16)) for i in range(2)]
    sb_p2 = [R(nc.sbuf_tensor(f"sb_p2_{i}", [128, T2], BF16)) for i in range(2)]
    sb_zacc = [R(nc.sbuf_tensor(f"sb_zacc{i}", [128, T2], F32)) for i in range(4)]
    sb_zb = [R(nc.sbuf_tensor(f"sb_zb{i}", [128, T2], BF16)) for i in range(1)]
    sb_z2b = [R(nc.sbuf_tensor(f"sb_z2b{i}", [128, T2], BF16)) for i in range(1)]
    sb_rows = R(nc.sbuf_tensor("sb_rows", [128, 3 * T2], F32))
    sb_wg = [R(nc.sbuf_tensor(f"sb_wg{i}", [128, 8 * 128], BF16)) for i in range(2)]
    sb_wu = [R(nc.sbuf_tensor(f"sb_wu{i}", [128, 8 * 128], BF16)) for i in range(2)]
    sb_wd = [R(nc.sbuf_tensor(f"sb_wd{i}", [128, 22 * 128], BF16)) for i in range(2)]
    sb_gt = [R(nc.sbuf_tensor(f"sb_gt{i}", [128, T2], BF16)) for i in range(1)]
    sb_ut = [R(nc.sbuf_tensor(f"sb_ut{i}", [128, T2], BF16)) for i in range(1)]
    sb_xo = [R(nc.sbuf_tensor(f"sb_xo{i}", [128, T2], F32)) for i in range(3)]
    sb_br = R(nc.sbuf_tensor("sb_br", [128, 2 * T2], F32))

    def rr(base, slot):
        return sb_rows[base:base + 1, slot * T2:(slot + 1) * T2]

    r1_cut = [("dve", v_state_r2), ("pool", r1_pool), ("act", max(r1_act, vP)),
              ("pe", r1_pe)]
    for st_name in ("vector", "scalar", "gpsimd", "tensor"):
        w.barrier(st_name, r1_cut)
    w.barrier("sync", r1_cut + dma_all())

    bank_free = [[] for _ in range(8)]
    dl2_free = [[], []]
    x2_free = [[], []]
    a2_free = [0, 0]
    bfree3 = [[0, 0], [0, 0]]
    h2_free = [("dve", 0), ("dve", 0)]
    p2_free = [("dve", 0), ("dve", 0)]
    zb_free = [0, 0]
    gt_free = [0, 0]
    wg_free = [0, 0]
    wu_free = [0, 0]
    wd_free = [0, 0]
    xo_free = [[], [], []]
    zz_free = [[], []]
    repC_free = []
    rep_mm_last = 0
    rows_free = []
    br_free = []
    x1bf_free = []
    gu_free = []
    H = {}

    def emit_scanA(t):
        nonlocal repC_free
        c0 = t * T2
        zb_ = t % 2
        sbz = sb_zz[zb_]
        global_state = {}
        bc_ready = w.dma("dV", sb_bcb2[:], bcstash[:, c0:c0 + T2],
                         waits=[("pe", rep_mm_last), ("dV", w.cnt["dV"])])
        vz_carry = [None] * DC
        zacc0_free = [("pool", 0), ("dve", 0)]
        for g in range(NG):
            n0 = g * GSZ
            repC_ready = 0
            for ni in range(GSZ):
                n = n0 + ni
                bki = ni % 2
                mmC = w.pe(lambda e, n=n, bki=bki: e.matmul(
                    bank(bki), sb_sel[:, (NST + n) * 128:(NST + n + 1) * 128], sb_bcb2[:],
                    start=True, stop=True),
                    waits=bank_free[bki] + [("dV", bc_ready)])
                mmlast = mmC
                repC_ready = w.act(lambda e, ni=ni, bki=bki: e.copy(
                    out=sb_repC[:, ni * T2:(ni + 1) * T2], in_=bank(bki)),
                    waits=[("pe", mmC)] + repC_free)
                bank_free[bki] = [("act", repC_ready)]
            global_state["rep_mm_last"] = mmlast
            for dcp in range(0, DC, 2):
                lane_state = []
                for lane in range(2):
                    dc = dcp + lane
                    s = lane
                    ez = "pool" if lane == 0 else "dve"
                    eop = w.pool if lane == 0 else w.dve
                    d_rdy = w.dma(f"dD{s}", sb_dl2[s][:],
                                  dstash[dc * 128:(dc + 1) * 128, c0:c0 + T2],
                                  waits=dl2_free[s])
                    if g == 0:
                        x_rdy = w.dma(f"dX{s}", sb_x2[s][:],
                                      x_T[dc * 128:(dc + 1) * 128, c0:c0 + T2],
                                      waits=x2_free[s])
                        zsl = 0 if lane == 0 else 2
                        vy = w.act(lambda e, s=s, dc=dc, zsl=zsl: e.activation(
                            out=sb_zacc[zsl][:], in_=sb_x2[s][:], func=AF.Identity,
                            scale=sb_D1[:, dc:dc + 1]),
                            waits=[(f"dX{s}", x_rdy), zacc0_free[lane],
                                   ("pool", v_d1 if t == 0 and dc == 0 else 0)])
                        vz = ("act", vy)
                        x2_free[s] = [vz]
                    else:
                        vz = vz_carry[dc]
                    lane_state.append({"dc": dc, "s": s, "d_rdy": d_rdy,
                                       "vz": vz, "act_last": 0})
                for ni in range(GSZ):
                    n = n0 + ni
                    for lane in range(2):
                        L = lane_state[lane]
                        dc, s = L["dc"], L["s"]
                        col = dc * NST + n
                        va = w.act(lambda e, s=s, col=col: e.activation(
                            out=sb_a2[s][:], in_=sb_dl2[s][:], func=AF.Exp,
                            scale=sb_Aneg[:, col:col + 1]),
                            waits=[(f"dD{s}", L["d_rdy"]), ("dve", a2_free[s])])
                        L["act_last"] = va
                        L["va"] = va
                    for lane in range(2):
                        L = lane_state[lane]
                        dc, s = L["dc"], L["s"]
                        col = dc * NST + n
                        bslot = (g * GSZ + ni) % 2
                        L["vb"] = w.dma(f"dK{s}{bslot}",
                                        sb_b2[s][:, bslot * T2:(bslot + 1) * T2],
                                        bstash[:, col * l_core + c0:col * l_core + c0 + T2],
                                        waits=[("dve", bfree3[s][bslot])])
                        L["bslot"] = bslot
                    for lane in range(2):
                        L = lane_state[lane]
                        dc, s = L["dc"], L["s"]
                        col = dc * NST + n
                        bslot = L["bslot"]
                        vh = w.dve(lambda e, s=s, col=col, bslot=bslot: e.tensor_tensor_scan(
                            out=sb_h2[s][:], data0=sb_a2[s][:],
                            data1=sb_b2[s][:, bslot * T2:(bslot + 1) * T2],
                            initial=sb_state[:, col:col + 1], op0=ALU.mult, op1=ALU.add),
                            waits=[("act", L["va"]), (f"dK{s}{bslot}", L["vb"]),
                                   ("dve", max(v_state_r2, h2_free[s][1] if h2_free[s][0] == "dve" else 0)),
                                   h2_free[s] if h2_free[s][0] != "dve" else None])
                        vst = w.dve(lambda e, s=s, col=col: e.tensor_copy(
                            out=sb_state[:, col:col + 1], in_=sb_h2[s][:, T2 - 1:T2]),
                            waits=[("dve", vh)])
                        a2_free[s] = vst
                        bfree3[s][bslot] = vh
                        L["vh"] = vh
                    for lane in range(2):
                        L = lane_state[lane]
                        s = L["s"]
                        ep = "dve" if lane == 0 else "pool"
                        pop = w.dve if lane == 0 else w.pool
                        vp = pop(lambda e, s=s, ni=ni: e.tensor_tensor(
                            out=sb_p2[s][:], in0=sb_h2[s][:],
                            in1=sb_repC[:, ni * T2:(ni + 1) * T2], op=ALU.mult),
                            waits=[("dve", L["vh"]), ("act", repC_ready), p2_free[s]])
                        h2_free[s] = (ep, vp)
                        L["vp"] = (ep, vp)
                    for lane in range(2):
                        L = lane_state[lane]
                        dc, s = L["dc"], L["s"]
                        ez = "pool" if lane == 0 else "dve"
                        eop = w.pool if lane == 0 else w.dve
                        zsl = 0 if lane == 0 else 2
                        into_z = ni == GSZ - 1
                        from_z = g > 0 and ni == 0
                        wl = [L["vp"], L["vz"]]
                        if g == 0 and ni == GSZ - 1:
                            wl += zz_free[zb_]
                        vz = (ez, eop(lambda e, s=s, ni=ni, dc=dc, into_z=into_z, zsl=zsl,
                                      from_z=from_z, sbz=sbz: e.tensor_tensor(
                            out=(sbz[:, dc * T2:(dc + 1) * T2] if into_z
                                 else sb_zacc[zsl + (ni + 1) % 2][:]),
                            in0=(sbz[:, dc * T2:(dc + 1) * T2] if from_z
                                 else sb_zacc[zsl + ni % 2][:]),
                            in1=sb_p2[s][:], op=ALU.add),
                            waits=wl))
                        if g == 0 and ni == 0:
                            zacc0_free[lane] = vz
                        p2_free[s] = vz
                        L["vz"] = vz
                for lane in range(2):
                    L = lane_state[lane]
                    dl2_free[L["s"]] = [("act", L["act_last"])]
                    vz_carry[L["dc"]] = L["vz"]
            repC_free = [("dve", w.cnt["dve"]), ("pool", w.cnt["pool"])]
        global_state["vz_fin"] = list(vz_carry)
        return global_state

    def emit_statsrows(t, gs):
        zb_ = t % 2
        sbz = sb_zz[zb_]
        st2 = 0
        for dc in range(DC):
            s = dc % 2
            vzb = w.act(lambda e, dc=dc, sbz=sbz: e.copy(
                out=sb_zb[0][:], in_=sbz[:, dc * T2:(dc + 1) * T2]),
                waits=[gs["vz_fin"][dc], ("pe", zb_free[0])])
            vz2b = w.act(lambda e, dc=dc, sbz=sbz: e.square(
                out=sb_z2b[0][:], in_=sbz[:, dc * T2:(dc + 1) * T2]),
                waits=[("act", vzb), ("pe", zb_free[1])])
            wl = [("act", vz2b)]
            if dc == 0:
                wl += bank_free[2] + bank_free[3] + [("dS", ld["onescol"] if t == 0 else 0)]
            stz = w.pe(lambda e, dc=dc: e.matmul(
                bank(2, 1), sb_onescol[:], sb_zb[0][:],
                start=(dc == 0), stop=(dc == DC - 1), skip_group_check=True),
                waits=wl)
            st2 = w.pe(lambda e, dc=dc: e.matmul(
                bank(3, 1), sb_onescol[:], sb_z2b[0][:],
                start=(dc == 0), stop=(dc == DC - 1), skip_group_check=True),
                waits=[])
            zb_free[0] = stz
            zb_free[1] = st2
        vm = w.dve(lambda e: e.tensor_scalar(
            out=rr(0, 0), in0=bank(2, 1), scalar1=1.0 / D, scalar2=None, op0=ALU.mult),
            waits=[("pe", st2)] + rows_free)
        vq = w.dve(lambda e: e.tensor_scalar(
            out=rr(0, 1), in0=bank(3, 1), scalar1=1.0 / D, scalar2=None, op0=ALU.mult),
            waits=[("dve", vm)])
        vt = w.dve(lambda e: e.tensor_tensor(out=rr(0, 2), in0=rr(0, 0), in1=rr(0, 0), op=ALU.mult),
                   waits=[("dve", vq)])
        vv = w.dve(lambda e: e.tensor_tensor(out=rr(32, 0), in0=rr(0, 1), in1=rr(0, 2), op=ALU.subtract),
                   waits=[("dve", vt)])
        vve = w.dve(lambda e: e.tensor_scalar(out=rr(32, 1), in0=rr(32, 0), scalar1=EPS,
                                              scalar2=None, op0=ALU.add),
                    waits=[("dve", vv)])
        vsq = w.act(lambda e: e.activation(out=rr(32, 2), in_=rr(32, 1), func=AF.Sqrt),
                    waits=[("dve", vve)])
        vr = w.dve(lambda e: e.reciprocal(out=rr(0, 1), in_=rr(32, 2)), waits=[("act", vsq)])
        vmr = w.dve(lambda e: e.tensor_tensor(out=rr(0, 2), in0=rr(0, 0), in1=rr(0, 1), op=ALU.mult),
                    waits=[("dve", vr)])
        bank_free[2] = [("dve", vm)]
        bank_free[3] = [("dve", vq)]
        gs["vr"] = vr
        gs["vmr"] = vmr

    def emit_ffnA(t, gs):
        nonlocal br_free
        x1_dones = gs.setdefault("x1_dones", [0, 0])
        zb_ = t % 2
        sbz = sb_zz[zb_]
        vr, vmr = gs["vr"], gs["vmr"]
        mmr = w.pe(lambda e: e.matmul(bank(4), sb_onesrow[:], rr(0, 1), start=True, stop=True),
                   waits=bank_free[4] + [("dve", vr),
                                         ("dS", ld["onesrow"] if t == 0 else 0)])
        mmm = w.pe(lambda e: e.matmul(bank(5), sb_onesrow[:], rr(0, 2), start=True, stop=True),
                   waits=bank_free[5] + [("dve", vmr)])
        gs["rows_pe"] = mmm
        evr1 = w.act(lambda e: e.copy(out=sb_br[:, 0:T2], in_=bank(4)),
                     waits=[("pe", mmr)] + br_free)
        evm1 = w.act(lambda e: e.copy(out=sb_br[:, T2:2 * T2], in_=bank(5)),
                     waits=[("pe", mmm)])
        bank_free[4] = [("act", evr1)]
        bank_free[5] = [("act", evm1)]
        x1_last = 0
        for dc in range(DC):
            s = dc % 2
            e1 = w.dve(lambda e, dc=dc, s=s, sbz=sbz: e.tensor_tensor(
                out=sb_xo[s][:], in0=sbz[:, dc * T2:(dc + 1) * T2], in1=sb_br[:, 0:T2],
                op=ALU.mult),
                waits=[("act", evr1)] + xo_free[s] + (x1bf_free if dc < 2 else []))
            x1_last = w.dve(lambda e, dc=dc, s=s: e.tensor_tensor(
                out=sb_x1bf[:, dc * T2:(dc + 1) * T2], in0=sb_xo[s][:], in1=sb_br[:, T2:2 * T2],
                op=ALU.subtract),
                waits=[("act", evm1), ("dve", e1)])
            xo_free[s] = [("dve", x1_last)]
            x1_dones[s] = x1_last
        gs["x1_last"] = x1_last
        gu_last = 0
        for m in range(HM):
            g_mm = 0
            u_mm = 0
            for hf in range(2):
                wgr = w.dma(f"dA{hf}", sb_wg[hf][:],
                            wbf_g[:, (m * DC + hf * 8) * 128:(m * DC + hf * 8 + 8) * 128],
                            waits=[("pe", wg_free[hf])])
                wur = w.dma(f"dB{hf}", sb_wu[hf][:],
                            wbf_u[:, (m * DC + hf * 8) * 128:(m * DC + hf * 8 + 8) * 128],
                            waits=[("pe", wu_free[hf])])
                for kc8 in range(8):
                    kc = hf * 8 + kc8
                    wl = []
                    if kc8 == 0:
                        wl = [(f"dA{hf}", wgr), ("dve", x1_last)]
                        if kc == 0:
                            wl += bank_free[4]
                    g_mm = w.pe(lambda e, m=m, kc=kc, hf=hf, kc8=kc8: e.matmul(
                        bank(4), sb_wg[hf][:, kc8 * 128:(kc8 + 1) * 128],
                        sb_x1bf[:, kc * T2:(kc + 1) * T2],
                        start=(kc == 0), stop=(kc == DC - 1)),
                        waits=wl)
                wg_free[hf] = g_mm
            for hf in range(2):
                wur2 = 0
                for kc8 in range(8):
                    kc = hf * 8 + kc8
                    wl = []
                    if kc8 == 0:
                        wl = [(f"dB{hf}", w.cnt[f"dB{hf}"])]
                        if kc == 0:
                            wl += bank_free[5]
                    u_mm = w.pe(lambda e, m=m, kc=kc, hf=hf, kc8=kc8: e.matmul(
                        bank(5), sb_wu[hf][:, kc8 * 128:(kc8 + 1) * 128],
                        sb_x1bf[:, kc * T2:(kc + 1) * T2],
                        start=(kc == 0), stop=(kc == DC - 1)),
                        waits=wl)
                wu_free[hf] = u_mm
            vg = w.act(lambda e, m=m: e.activation(
                out=sb_gt[0][:], in_=bank(4), func=AF.Silu, scale=sb_sg[:, m:m + 1]),
                waits=[("pe", g_mm), ("pool", gt_free[0]),
                       ("dS", ld["sg"] if t == 0 and m == 0 else 0)])
            bank_free[4] = [("act", vg)]
            vup = w.act(lambda e, m=m: e.activation(
                out=sb_ut[0][:], in_=bank(5), func=AF.Identity, scale=sb_su[:, m:m + 1]),
                waits=[("pe", u_mm),
                       ("dS", ld["su"] if t == 0 and m == 0 else 0)])
            bank_free[5] = [("act", vup)]
            wl = [("act", vup)]
            if m < 2:
                wl += gu_free
            gu_last = w.pool(lambda e, m=m: e.tensor_tensor(
                out=sb_gu[:, m * T2:(m + 1) * T2], in0=sb_gt[0][:], in1=sb_ut[0][:],
                op=ALU.mult),
                waits=wl)
            gt_free[0] = gu_last
        gs["gu_last"] = gu_last

    def emit_ffnB(t, gs):
        nonlocal br_free
        x1_dones = gs["x1_dones"]
        zb_ = t % 2
        sbz = sb_zz[zb_]
        gu_last = gs["gu_last"]
        st4 = 0
        z2_fin = [None] * DC
        for dc in range(DC):
            s = dc % 2
            d_mm = 0
            for hf in range(2):
                wdr = w.dma(f"dE{hf}", sb_wd[hf][:],
                            wbf_d[:, (dc * HM + hf * 22) * 128:(dc * HM + hf * 22 + 22) * 128],
                            waits=[("pe", wd_free[hf])])
                for kh22 in range(22):
                    kh = hf * 22 + kh22
                    wl = []
                    if kh22 == 0:
                        wl = [(f"dE{hf}", wdr), ("pool", gu_last)]
                        if kh == 0:
                            wl += bank_free[6]
                    d_mm = w.pe(lambda e, dc=dc, kh=kh, hf=hf, kh22=kh22: e.matmul(
                        bank(6), sb_wd[hf][:, kh22 * 128:(kh22 + 1) * 128],
                        sb_gu[:, kh * T2:(kh + 1) * T2],
                        start=(kh == 0), stop=(kh == HM - 1)),
                        waits=wl)
                wd_free[hf] = d_mm
            s2 = dc % 2
            ev6 = w.act(lambda e, dc=dc, s2=s2: e.activation(
                out=sb_xo[s2][:], in_=bank(6), func=AF.Identity,
                scale=sb_sd[:, dc:dc + 1]),
                waits=[("pe", d_mm), ("dS", ld["sd"] if t == 0 and dc == 0 else 0)] +
                      xo_free[s2])
            bank_free[6] = [("act", ev6)]
            z2 = w.pool(lambda e, dc=dc, s2=s2, sbz=sbz: e.tensor_tensor(
                out=sbz[:, dc * T2:(dc + 1) * T2], in0=sb_xo[s2][:],
                in1=sb_x1bf[:, dc * T2:(dc + 1) * T2], op=ALU.add),
                waits=[("act", ev6), ("dve", gs["x1_last"])])
            xo_free[s2] = [("pool", z2)]
            z2_fin[dc] = ("pool", z2)
            gs["x1bf_consumed"] = z2
            vzb = w.act(lambda e, dc=dc, sbz=sbz: e.copy(
                out=sb_zb[0][:], in_=sbz[:, dc * T2:(dc + 1) * T2]),
                waits=[("pool", z2), ("pe", zb_free[0])])
            vz2b = w.act(lambda e, dc=dc, sbz=sbz: e.square(
                out=sb_z2b[0][:], in_=sbz[:, dc * T2:(dc + 1) * T2]),
                waits=[("act", vzb), ("pe", zb_free[1])])
            wl = [("act", vz2b)]
            if dc == 0:
                wl += bank_free[4] + bank_free[5]
            stz = w.pe(lambda e, dc=dc: e.matmul(
                bank(4, 1), sb_onescol[:], sb_zb[0][:],
                start=(dc == 0), stop=(dc == DC - 1), skip_group_check=True),
                waits=wl)
            st4 = w.pe(lambda e, dc=dc: e.matmul(
                bank(5, 1), sb_onescol[:], sb_z2b[0][:],
                start=(dc == 0), stop=(dc == DC - 1), skip_group_check=True),
                waits=[])
            zb_free[0] = stz
            zb_free[1] = st4
        vm2 = w.dve(lambda e: e.tensor_scalar(
            out=rr(0, 0), in0=bank(4, 1), scalar1=1.0 / D, scalar2=None, op0=ALU.mult),
            waits=[("pe", st4), ("pe", gs["rows_pe"])])
        vq2 = w.dve(lambda e: e.tensor_scalar(
            out=rr(0, 1), in0=bank(5, 1), scalar1=1.0 / D, scalar2=None, op0=ALU.mult),
            waits=[("dve", vm2)])
        vt2 = w.dve(lambda e: e.tensor_tensor(out=rr(0, 2), in0=rr(0, 0), in1=rr(0, 0), op=ALU.mult),
                    waits=[("dve", vq2)])
        vv2 = w.dve(lambda e: e.tensor_tensor(out=rr(32, 0), in0=rr(0, 1), in1=rr(0, 2), op=ALU.subtract),
                    waits=[("dve", vt2)])
        vve2 = w.dve(lambda e: e.tensor_scalar(out=rr(32, 1), in0=rr(32, 0), scalar1=EPS,
                                               scalar2=None, op0=ALU.add),
                     waits=[("dve", vv2)])
        vsq2 = w.act(lambda e: e.activation(out=rr(32, 2), in_=rr(32, 1), func=AF.Sqrt),
                     waits=[("dve", vve2)])
        vr2 = w.dve(lambda e: e.reciprocal(out=rr(0, 1), in_=rr(32, 2)), waits=[("act", vsq2)])
        vmr2 = w.dve(lambda e: e.tensor_tensor(out=rr(0, 2), in0=rr(0, 0), in1=rr(0, 1), op=ALU.mult),
                     waits=[("dve", vr2)])
        bank_free[4] = [("dve", vm2)]
        bank_free[5] = [("dve", vq2)]
        mmr3 = w.pe(lambda e: e.matmul(bank(6), sb_onesrow[:], rr(0, 1), start=True, stop=True),
                    waits=bank_free[6] + [("dve", vr2)])
        mmm3 = w.pe(lambda e: e.matmul(bank(7), sb_onesrow[:], rr(0, 2), start=True, stop=True),
                    waits=bank_free[7] + [("dve", vmr2)])
        evr3 = w.act(lambda e: e.copy(out=sb_br[:, 0:T2], in_=bank(6)),
                     waits=[("pe", mmr3), ("dve", x1_dones[0]), ("dve", x1_dones[1])])
        evm3 = w.act(lambda e: e.copy(out=sb_br[:, T2:2 * T2], in_=bank(7)),
                     waits=[("pe", mmm3)])
        bank_free[6] = [("act", evr3)]
        bank_free[7] = [("act", evm3)]
        o_last = [0, 0]
        c0 = t * T2
        for dc in range(DC):
            s = dc % 2
            e1 = w.dve(lambda e, dc=dc, s=s, sbz=sbz: e.tensor_tensor(
                out=sb_xo[s][:], in0=sbz[:, dc * T2:(dc + 1) * T2], in1=sb_br[:, 0:T2],
                op=ALU.mult),
                waits=[("act", evr3)] + xo_free[s])
            e2 = w.dve(lambda e, dc=dc, s=s: e.tensor_tensor(
                out=sb_xo[2][:], in0=sb_xo[s][:], in1=sb_br[:, T2:2 * T2],
                op=ALU.subtract),
                waits=[("act", evm3), ("dve", e1)] + xo_free[2])
            od = w.dma(f"dO{s}", out_T[dc * 128:(dc + 1) * 128, c0:c0 + T2], sb_xo[2][:],
                       waits=[("dve", e2)])
            xo_free[s] = [("dve", e2)]
            xo_free[2] = [(f"dO{s}", od)]
            o_last[s] = e2
        bank_free[6] = [("dve", max(o_last))]
        bank_free[7] = [("dve", max(o_last))]
        zz_free[zb_] = [("dve", max(o_last)),
                        ("act", w.cnt["act"]), ("pool", w.cnt["pool"])]
        nonlocal_rows = [("pe", mmm3)]
        return nonlocal_rows

    for t in range(nt2):
        gs = emit_scanA(t)
        rep_mm_last = gs["rep_mm_last"]
        if t > 0:
            rows_free = emit_ffnB(t - 1, H[t - 1])
            x1bf_free = [("pool", H[t - 1].get("x1bf_consumed", 0))]
            gu_free = [("pe", w.cnt["pe"])]
        emit_statsrows(t, gs)
        emit_ffnA(t, gs)
        H[t] = gs
    rows_free = emit_ffnB(nt2 - 1, H[nt2 - 1])

    r2_stack.close()

    w.barrier("sync", dma_all())

    with nc.Block() as block:
        @block.sync
        def _(e):
            for f in w.ops["sync"]:
                f(e)

        @block.tensor
        def _(e):
            for f in w.ops["tensor"]:
                f(e)

        @block.scalar
        def _(e):
            for f in w.ops["scalar"]:
                f(e)

        @block.vector
        def _(e):
            for f in w.ops["vector"]:
                f(e)

        @block.gpsimd
        def _(e):
            for f in w.ops["gpsimd"]:
                f(e)

    stack.close()
    return nc


def _host_inputs(L0_attn_x, L0_attn_W_in, L0_attn_W_dt, L0_attn_b_dt, L0_attn_A_log,
                 L0_attn_D, L0_ffn_wq_gate, L0_ffn_s_gate, L0_ffn_wq_up, L0_ffn_s_up,
                 L0_ffn_wq_down, L0_ffn_s_down, l_core):
    bf16 = ml_dtypes.bfloat16
    x = np.asarray(L0_attn_x, dtype=np.float32)
    w_inT = np.ascontiguousarray(np.asarray(L0_attn_W_in, np.float32).T)
    w_dtT = np.ascontiguousarray(np.asarray(L0_attn_W_dt, np.float32).T)
    b_dt_t = np.ascontiguousarray(np.asarray(L0_attn_b_dt, np.float32).reshape(DC, 128).T)
    a_t = np.ascontiguousarray(
        np.asarray(L0_attn_A_log, np.float32).reshape(DC, 128, NST)
        .transpose(1, 0, 2).reshape(128, DC * NST))
    d_t = np.ascontiguousarray(np.asarray(L0_attn_D, np.float32).reshape(DC, 128).T)
    s_g = np.ascontiguousarray(np.asarray(L0_ffn_s_gate, np.float32).reshape(HM, 128).T)
    s_u = np.ascontiguousarray(np.asarray(L0_ffn_s_up, np.float32).reshape(HM, 128).T)
    s_d = np.ascontiguousarray(np.asarray(L0_ffn_s_down, np.float32).reshape(DC, 128).T)

    wq_gate = np.asarray(L0_ffn_wq_gate, np.int32).reshape(HM, 128, DC, 128)
    wq_g = np.ascontiguousarray(wq_gate.transpose(3, 0, 2, 1).reshape(128, HM * DC * 128))
    wq_up = np.asarray(L0_ffn_wq_up, np.int32).reshape(HM, 128, DC, 128)
    wq_u = np.ascontiguousarray(wq_up.transpose(3, 0, 2, 1).reshape(128, HM * DC * 128))
    wq_down = np.asarray(L0_ffn_wq_down, np.int32).reshape(DC, 128, HM, 128)
    wq_d = np.ascontiguousarray(wq_down.transpose(3, 0, 2, 1).reshape(128, DC * HM * 128))

    sel = np.zeros((32, 2 * NST * 128), dtype=bf16)
    for r in range(32):
        sel[r, r * 128:(r + 1) * 128] = bf16(1.0)
    onescol = np.ones((128, 1), dtype=bf16)
    onesrow = np.ones((1, 128), dtype=np.float32)

    in_maps = []
    for c in range(NCORES):
        b, q = divmod(c, 4)
        xs = np.ascontiguousarray(x[b, q * l_core:(q + 1) * l_core, :].T)
        msk = np.zeros((128, NCORES), dtype=np.float32)
        for j in range(4 * b, c):
            msk[:, j] = 1.0
        imsk = np.ascontiguousarray(1.0 - msk)
        in_maps.append({
            "x_T": xs, "w_inT": w_inT, "w_dtT": w_dtT, "b_dt_t": b_dt_t,
            "a_t": a_t, "d_t": d_t, "s_g": s_g, "s_u": s_u, "s_d": s_d,
            "wq_g": wq_g, "wq_u": wq_u, "wq_d": wq_d, "msk": msk, "imsk": imsk,
            "sel": sel, "onescol": onescol, "onesrow": onesrow,
        })
    return in_maps


def kernel(**inputs):
    l_core = 4096
    in_maps = _host_inputs(l_core=l_core, **inputs)
    nc = build_nc(l_core)
    res = run_bass_kernel_spmd(nc, in_maps, core_ids=list(range(NCORES)))
    out = np.empty((2, 16384, D), dtype=np.float32)
    for c in range(NCORES):
        b, q = divmod(c, 4)
        out[b, q * l_core:(q + 1) * l_core, :] = res.results[c]["out_T"].T
    return out


# version 08
# speedup vs baseline: 1.2589x; 1.0014x over previous
import numpy as np
import ml_dtypes
from contextlib import ExitStack
import concourse.bass as bass
import concourse.mybir as mybir
from concourse.bass_utils import run_bass_kernel_spmd

F32 = mybir.dt.float32
BF16 = mybir.dt.bfloat16
I32 = mybir.dt.int32
AF = mybir.ActivationFunctionType
ALU = mybir.AluOpType
AX = mybir.AxisListType.X

D = 2048
NST = 16
RDT = 64
HID = 5632
DC = D // 128
HM = HID // 128
NCORES = 8
EPS = 1e-5
T1 = 1024
T2 = 512
WCH = 4096


class Walk:

    def __init__(self, nc, sems):
        self.nc = nc
        self.sems = sems
        self.cnt = {k: 0 for k in sems}
        self.ops = {k: [] for k in ("sync", "tensor", "scalar", "vector", "gpsimd")}

    def _emit(self, stream, own, waits, fn, inc_amt=1, self_serialize=False):
        best = {}
        for item in waits:
            if item is None:
                continue
            s, v = item
            if v and v > 0 and v > best.get(s, 0):
                best[s] = v
        if self_serialize and own and self.cnt[own] > 0:
            if best.get(own, 0) < self.cnt[own]:
                best[own] = self.cnt[own]
        clean = sorted(best.items())
        sems = self.sems
        own_sem = sems[own] if own else None

        def run(e, waits=tuple(clean), fn=fn, own_sem=own_sem, inc_amt=inc_amt):
            for s, v in waits:
                e.wait_ge(sems[s], v)
            ins = fn(e)
            if own_sem is not None:
                ins.then_inc(own_sem, inc_amt)

        self.ops[stream].append(run)
        if own:
            self.cnt[own] += inc_amt
            return self.cnt[own]
        return None

    def dma(self, q, out, in_, waits=()):
        return self._emit("sync", q, waits,
                          lambda e: e.dma_start(out=out, in_=in_), 16)

    def pe(self, fn, waits=()):
        return self._emit("tensor", "pe", waits, fn)

    def act(self, fn, waits=()):
        return self._emit("scalar", "act", waits, fn, self_serialize=True)

    def dve(self, fn, waits=()):
        return self._emit("vector", "dve", waits, fn, self_serialize=True)

    def pool(self, fn, waits=()):
        return self._emit("gpsimd", "pool", waits, fn, self_serialize=True)

    def cc(self, fn, waits=()):
        return self._emit("gpsimd", "cc", waits, fn, 1)

    def barrier(self, stream, waits):
        clean = tuple((s, v) for (s, v) in waits if v and v > 0)
        sems = self.sems

        def run(e, waits=clean):
            for s, v in waits:
                e.wait_ge(sems[s], v)

        self.ops[stream].append(run)


def ep_wait(eng, val):
    return (eng, val)


def build_nc(l_core):
    nt1 = l_core // T1
    nt2 = l_core // T2
    nc = bass.Bass("TRN2", target_bir_lowering=False)

    x_T = nc.dram_tensor("x_T", [D, l_core], F32, kind="ExternalInput").ap()
    w_inT = nc.dram_tensor("w_inT", [D, 96], F32, kind="ExternalInput").ap()
    w_dtT = nc.dram_tensor("w_dtT", [RDT, D], F32, kind="ExternalInput").ap()
    b_dt_t = nc.dram_tensor("b_dt_t", [128, DC], F32, kind="ExternalInput").ap()
    a_t = nc.dram_tensor("a_t", [128, DC * NST], F32, kind="ExternalInput").ap()
    d_t = nc.dram_tensor("d_t", [128, DC], F32, kind="ExternalInput").ap()
    s_g = nc.dram_tensor("s_g", [128, HM], F32, kind="ExternalInput").ap()
    s_u = nc.dram_tensor("s_u", [128, HM], F32, kind="ExternalInput").ap()
    s_d = nc.dram_tensor("s_d", [128, DC], F32, kind="ExternalInput").ap()
    wq_g = nc.dram_tensor("wq_g", [128, HM * DC * 128], I32, kind="ExternalInput").ap()
    wq_u = nc.dram_tensor("wq_u", [128, HM * DC * 128], I32, kind="ExternalInput").ap()
    wq_d = nc.dram_tensor("wq_d", [128, DC * HM * 128], I32, kind="ExternalInput").ap()
    msk_in = nc.dram_tensor("msk", [128, NCORES], F32, kind="ExternalInput").ap()
    imsk_in = nc.dram_tensor("imsk", [128, NCORES], F32, kind="ExternalInput").ap()
    sel_in = nc.dram_tensor("sel", [32, 2 * NST * 128], BF16, kind="ExternalInput").ap()
    onescol_in = nc.dram_tensor("onescol", [128, 1], BF16, kind="ExternalInput").ap()
    onesrow_in = nc.dram_tensor("onesrow", [1, 128], F32, kind="ExternalInput").ap()

    out_T = nc.dram_tensor("out_T", [D, l_core], F32, kind="ExternalOutput").ap()

    dstash = nc.dram_tensor("dstash", [D, l_core], F32)
    bcstash = nc.dram_tensor("bcstash", [32, l_core], BF16)
    bstash = nc.dram_tensor("bstash", [128, DC * NST * l_core], BF16)
    wbf_g = nc.dram_tensor("wbf_g", [128, HM * DC * 128], BF16)
    wbf_u = nc.dram_tensor("wbf_u", [128, HM * DC * 128], BF16)
    wbf_d = nc.dram_tensor("wbf_d", [128, DC * HM * 128], BF16)
    gp_src = nc.dram_tensor("gp_src", [256, DC * NST], F32)
    gp_dst = nc.dram_tensor("gp_dst", [256 * NCORES, DC * NST], F32)

    stack = ExitStack()
    sems = {}
    dma_sem_names = (["dS", "dT", "dV", "dP"]
                     + [f"d{k}{i}" for k in "CIOXDABEJ" for i in range(2)]
                     + [f"dK{i}{j}" for i in range(2) for j in range(2)]
                     + ["dZ0", "dZ1"])
    for name in dma_sem_names + ["pe", "act", "dve", "pool", "cc"]:
        sems[name] = stack.enter_context(nc.semaphore(name))
    w = Walk(nc, sems)

    E = stack.enter_context
    sb_bdt = E(nc.sbuf_tensor("sb_bdt", [128, DC], F32))
    sb_Apos = E(nc.sbuf_tensor("sb_Apos", [128, DC * NST], F32))
    sb_Aneg = E(nc.sbuf_tensor("sb_Aneg", [128, DC * NST], F32))
    sb_Dt = E(nc.sbuf_tensor("sb_Dt", [128, DC], F32))
    sb_D1 = E(nc.sbuf_tensor("sb_D1", [128, DC], F32))
    sb_sg = E(nc.sbuf_tensor("sb_sg", [128, HM], F32))
    sb_su = E(nc.sbuf_tensor("sb_su", [128, HM], F32))
    sb_sd = E(nc.sbuf_tensor("sb_sd", [128, DC], F32))
    sb_msk = E(nc.sbuf_tensor("sb_msk", [128, NCORES], F32))
    sb_imsk = E(nc.sbuf_tensor("sb_imsk", [128, NCORES], F32))
    sb_sel = E(nc.sbuf_tensor("sb_sel", [32, 2 * NST * 128], BF16))
    sb_onescol = E(nc.sbuf_tensor("sb_onescol", [128, 1], BF16))
    sb_onesrow = E(nc.sbuf_tensor("sb_onesrow", [1, 128], F32))
    sb_state = E(nc.sbuf_tensor("sb_state", [128, DC * NST], F32))

    ps8 = E(nc.psum_tensor("ps8", [128, 4096], F32))

    def bank(i, pn=128, c0=0, cn=512):
        return ps8[0:pn, i * 512 + c0:i * 512 + c0 + cn]

    def sdma(out, in_):
        return w.dma("dS", out, in_, waits=[("dS", w.cnt["dS"])])

    def dma_all():
        return [(n, w.cnt[n]) for n in dma_sem_names]

    ld = {}
    ld["bdt"] = sdma(sb_bdt[:], b_dt_t)
    ld["alog"] = sdma(sb_Apos[:], a_t)
    ld["dt"] = sdma(sb_Dt[:], d_t)
    ld["sg"] = sdma(sb_sg[:], s_g)
    ld["su"] = sdma(sb_su[:], s_u)
    ld["sd"] = sdma(sb_sd[:], s_d)
    ld["msk"] = sdma(sb_msk[:], msk_in)
    ld["imsk"] = sdma(sb_imsk[:], imsk_in)
    ld["sel"] = sdma(sb_sel[:], sel_in)
    ld["onescol"] = sdma(sb_onescol[:], onescol_in)
    ld["onesrow"] = sdma(sb_onesrow[:], onesrow_in)

    v_apos = w.act(lambda e: e.activation(out=sb_Apos[:], in_=sb_Apos[:], func=AF.Exp),
                   waits=[("dS", ld["alog"])])
    v_aneg = w.pool(lambda e: e.tensor_scalar(out=sb_Aneg[:], in0=sb_Apos[:],
                                              scalar1=-1.0, scalar2=None, op0=ALU.mult),
                    waits=[("act", v_apos)])
    v_d1 = w.pool(lambda e: e.tensor_scalar(out=sb_D1[:], in0=sb_Dt[:],
                                            scalar1=1.0, scalar2=None, op0=ALU.add),
                  waits=[("dS", ld["dt"])])
    v_state0 = w.dve(lambda e: e.memset(sb_state[:], 0.0))

    conv_stack = ExitStack()
    cv_i = [conv_stack.enter_context(nc.sbuf_tensor(f"cv_i{i}", [128, WCH], I32)) for i in range(2)]
    cv_b = [conv_stack.enter_context(nc.sbuf_tensor(f"cv_b{i}", [128, WCH], BF16)) for i in range(2)]
    nch = HM * DC * 128 // WCH
    cv_in_done = [0, 0]
    cv_ts_done = [0, 0]
    cv_out_done = [0, 0]
    ci = 0
    for src, dst in ((wq_g, wbf_g), (wq_u, wbf_u), (wq_d, wbf_d)):
        for ch in range(nch):
            s = ci % 2
            cv_in_done[s] = w.dma(f"dC{s}", cv_i[s][:], src[:, ch * WCH:(ch + 1) * WCH],
                                  waits=[("dve", cv_ts_done[s])])
            cv_ts_done[s] = w.dve(
                lambda e, s=s: e.tensor_scalar(out=cv_b[s][:], in0=cv_i[s][:],
                                               scalar1=128.0, scalar2=None,
                                               op0=ALU.subtract),
                waits=[(f"dC{s}", cv_in_done[s]), (f"dI{s}", cv_out_done[s])])
            cv_out_done[s] = w.dma(f"dI{s}", dst[:, ch * WCH:(ch + 1) * WCH], cv_b[s][:],
                                   waits=[("dve", cv_ts_done[s])])
            ci += 1
    conv_done_dve = w.cnt["dve"]
    conv_stack.close()

    w.barrier("sync", [("dve", conv_done_dve)] + dma_all())

    r1_stack = ExitStack()
    R = r1_stack.enter_context
    sb_winT = R(nc.sbuf_tensor("sb_winT", [128, DC * 96], F32))
    sb_wdtT = R(nc.sbuf_tensor("sb_wdtT", [RDT, D], F32))
    sb_G = R(nc.sbuf_tensor("sb_G", [128, DC * NST], F32))
    sb_P = R(nc.sbuf_tensor("sb_P", [128, DC * NST], F32))
    sb_hin = R(nc.sbuf_tensor("sb_hin", [128, DC * NST], F32))
    sb_ht = R(nc.sbuf_tensor("sb_ht", [128, DC * NST], F32))
    sb_gm = R(nc.sbuf_tensor("sb_gm", [128, DC * NST], F32))
    sb_pm = R(nc.sbuf_tensor("sb_pm", [128, DC * NST], F32))
    sb_Gj = [R(nc.sbuf_tensor(f"sb_Gj{i}", [128, DC * NST], F32)) for i in range(2)]
    sb_Pj = [R(nc.sbuf_tensor(f"sb_Pj{i}", [128, DC * NST], F32)) for i in range(2)]
    sb_sumdall = R(nc.sbuf_tensor("sb_sumdall", [128, DC * 16], F32))
    sb_sumdtot = R(nc.sbuf_tensor("sb_sumdtot", [128, DC], F32))
    sb_d1l = R(nc.sbuf_tensor("sb_d1l", [128, DC * T1], F32))
    sb_u1l = R(nc.sbuf_tensor("sb_u1l", [128, DC * T1], F32))
    sb_xs = [R(nc.sbuf_tensor(f"sb_xs{i}", [128, T1], F32)) for i in range(2)]
    sb_dlr = R(nc.sbuf_tensor("sb_dlr", [RDT, T1], F32))
    sb_bcb1 = R(nc.sbuf_tensor("sb_bcb1", [32, T1], BF16))
    sb_rep1 = [R(nc.sbuf_tensor(f"sb_rep1_{i}", [128, T1], BF16)) for i in range(1)]
    sb_sa = [R(nc.sbuf_tensor(f"sb_sa{i}", [128, T1], BF16)) for i in range(2)]
    sb_sb = [R(nc.sbuf_tensor(f"sb_sb{i}", [128, T1], F32)) for i in range(2)]
    sb_sbc = [R(nc.sbuf_tensor(f"sb_sbc{i}", [128, T1], BF16)) for i in range(2)]
    sb_sh = R(nc.sbuf_tensor("sb_sh", [128, T1], BF16))
    sb_S = R(nc.sbuf_tensor("sb_S", [128, T1], F32))
    sb_Gparts = R(nc.sbuf_tensor("sb_Gparts", [128, 4 * DC * NST], F32))
    sb_scar = R(nc.sbuf_tensor("sb_scar", [128, DC], F32))
    sb_wt = R(nc.sbuf_tensor("sb_wt", [128, DC * NST], BF16))

    for dc in range(DC):
        ld["winT"] = sdma(sb_winT[:, dc * 96:(dc + 1) * 96],
                          w_inT[dc * 128:(dc + 1) * 128, :])
    ld["wdtT"] = sdma(sb_wdtT[:], w_dtT)

    xs_free = [[], []]
    a_free = [0, 0]
    b_free = [0, 0]
    h_free = [0, 0]
    rep1_free = [0, 0]
    bank_free = [[] for _ in range(8)]
    v_softadd = 0
    bst_done = [0, 0]
    ev_bc_last = 0
    dlr_free_pe = 0
    bcb1_free_pe = 0
    stash_last = 0

    for t in range(nt1):
        c0 = t * T1
        pool_cut = w.cnt["pool"]
        lr_mm = [0, 0]
        bc_mm = [0, 0]
        for dc in range(DC):
            s = dc % 2
            xr = w.dma(f"dX{s}", sb_xs[s][:], x_T[dc * 128:(dc + 1) * 128, c0:c0 + T1],
                       waits=xs_free[s])
            for cc in range(2):
                first = dc == 0
                last = dc == DC - 1
                wl = [(f"dX{s}", xr)]
                if first:
                    wl += [("dS", ld["winT"])] + bank_free[cc] + bank_free[2 + cc]
                lr_mm[cc] = w.pe(
                    lambda e, s=s, cc=cc, dc=dc, first=first, last=last: e.matmul(
                        bank(cc, RDT), sb_winT[:, dc * 96:dc * 96 + RDT],
                        sb_xs[s][:, cc * 512:(cc + 1) * 512],
                        start=first, stop=last, skip_group_check=True),
                    waits=wl)
                bc_mm[cc] = w.pe(
                    lambda e, s=s, cc=cc, dc=dc, first=first, last=last: e.matmul(
                        bank(2 + cc, 32), sb_winT[:, dc * 96 + RDT:(dc + 1) * 96],
                        sb_xs[s][:, cc * 512:(cc + 1) * 512],
                        start=first, stop=last, skip_group_check=True),
                    waits=[])
            xs_free[s] = [("pe", bc_mm[1])]
        ev_lr = [0, 0]
        ev_bc = [0, 0]
        for cc in range(2):
            ev_lr[cc] = w.act(
                lambda e, cc=cc: e.copy(out=sb_dlr[:, cc * 512:(cc + 1) * 512],
                                        in_=bank(cc, RDT)),
                waits=[("pe", lr_mm[cc]), ("pe", dlr_free_pe)])
            ev_bc[cc] = w.act(
                lambda e, cc=cc: e.copy(out=sb_bcb1[:, cc * 512:(cc + 1) * 512],
                                        in_=bank(2 + cc, 32)),
                waits=[("pe", bc_mm[cc]), ("pe", bcb1_free_pe)])
        bank_free[0] = [("act", ev_lr[0])]
        bank_free[1] = [("act", ev_lr[1])]
        bank_free[2] = [("act", ev_bc[0])]
        bank_free[3] = [("act", ev_bc[1])]
        ev_bc_last = max(ev_bc)
        w.dma("dT", bcstash[:, c0:c0 + T1], sb_bcb1[:],
              waits=[("act", ev_bc_last), ("dT", w.cnt["dT"])])

        for dc in range(DC):
            for cc in range(2):
                bki = 4 + cc
                pcol = dc * T1 + cc * 512
                mmv = w.pe(
                    lambda e, dc=dc, cc=cc, bki=bki: e.matmul(
                        bank(bki), sb_wdtT[:, dc * 128:(dc + 1) * 128],
                        sb_dlr[:, cc * 512:(cc + 1) * 512],
                        start=True, stop=True),
                    waits=bank_free[bki] + [("act", max(ev_lr)),
                                            ("dS", ld["wdtT"] if t == 0 and dc == 0 else 0)])
                dlr_free_pe = mmv
                v1 = w.act(lambda e, dc=dc, bki=bki: e.activation(
                    out=sb_S[:, 0:512], in_=bank(bki), func=AF.Abs,
                    bias=sb_bdt[:, dc:dc + 1]),
                    waits=[("pe", mmv), ("dve", max(a_free[0], a_free[1])),
                           ("pool", pool_cut if dc == 0 and cc == 0 else 0),
                           ("dS", ld["bdt"] if t == 0 and dc == 0 else 0),
                           ("dT", stash_last if dc == 0 and cc == 0 else 0)])
                v2 = w.act(lambda e: e.activation(
                    out=sb_sb[1][:, 0:512], in_=sb_S[:, 0:512], func=AF.Exp,
                    scale=-1.0),
                    waits=[("act", v1)])
                v3 = w.act(lambda e: e.activation(
                    out=sb_S[:, 512:1024], in_=sb_sb[1][:, 0:512], func=AF.Ln,
                    bias=1.0),
                    waits=[("act", v2), ("pool", v_softadd)])
                v4 = w.act(lambda e, dc=dc, bki=bki: e.activation(
                    out=sb_sb[1][:, 512:1024], in_=bank(bki), func=AF.Relu,
                    bias=sb_bdt[:, dc:dc + 1]),
                    waits=[("act", v3)])
                bank_free[bki] = [("act", v4)]
                v_softadd = w.pool(lambda e, pcol=pcol: e.tensor_tensor(
                    out=sb_d1l[:, pcol:pcol + 512], in0=sb_S[:, 512:1024],
                    in1=sb_sb[1][:, 512:1024], op=ALU.add),
                    waits=[("act", v4)])
        delta_done = v_softadd

        for dc in range(DC):
            w.dve(lambda e, dc=dc, t=t: e.reduce_sum(
                out=sb_sumdall[:, dc * 16 + t:dc * 16 + t + 1],
                in_=sb_d1l[:, dc * T1:(dc + 1) * T1], axis=AX),
                waits=[("pool", delta_done)])

        u_done = 0
        for dc in range(DC):
            s = dc % 2
            stash_last = w.dma("dT", dstash[dc * 128:(dc + 1) * 128, c0:c0 + T1],
                               sb_d1l[:, dc * T1:(dc + 1) * T1],
                               waits=[("pool", delta_done), ("dT", w.cnt["dT"])])
            xr = w.dma(f"dX{s}", sb_xs[s][:], x_T[dc * 128:(dc + 1) * 128, c0:c0 + T1],
                       waits=xs_free[s])
            u_done = w.pool(lambda e, dc=dc, s=s: e.tensor_tensor(
                out=sb_u1l[:, dc * T1:(dc + 1) * T1], in0=sb_d1l[:, dc * T1:(dc + 1) * T1],
                in1=sb_xs[s][:], op=ALU.mult),
                waits=[(f"dX{s}", xr), ("pool", delta_done)])
            xs_free[s] = [("pool", u_done)]

        if t == 0:
            v_scar = w.dve(lambda e: e.memset(sb_scar[:], 0.0), waits=[])
        r_done = 0
        for dc in range(DC):
            vS = w.dve(lambda e, dc=dc: e.tensor_tensor_scan(
                out=sb_S[:], data0=sb_d1l[:, dc * T1:(dc + 1) * T1],
                data1=sb_d1l[:, dc * T1:(dc + 1) * T1],
                initial=sb_scar[:, dc:dc + 1], op0=ALU.add, op1=ALU.bypass),
                waits=[("pool", u_done), ("dT", stash_last), ("act", r_done)])
            vC = w.dve(lambda e, dc=dc: e.tensor_copy(
                out=sb_scar[:, dc:dc + 1], in_=sb_S[:, T1 - 1:T1]),
                waits=[("dve", vS)])
            r_done = w.act(lambda e, dc=dc: e.activation(
                out=sb_d1l[:, dc * T1:(dc + 1) * T1], in_=sb_S[:],
                func=AF.Identity, scale=-1.0, bias=sb_scar[:, dc:dc + 1]),
                waits=[("dve", vC)])

        for n in range(NST):
            sn = n % 2
            bks = 0 if sn == 0 else 2
            mm1 = w.pe(lambda e, n=n, bks=bks: e.matmul(
                bank(bks), sb_sel[:, n * 128:(n + 1) * 128], sb_bcb1[:, 0:512],
                start=True, stop=True),
                waits=bank_free[bks] + bank_free[bks + 1] +
                      [("act", ev_bc_last), ("dS", ld["sel"] if t == 0 and n == 0 else 0)])
            mm2 = w.pe(lambda e, n=n, bks=bks: e.matmul(
                bank(bks + 1), sb_sel[:, n * 128:(n + 1) * 128], sb_bcb1[:, 512:1024],
                start=True, stop=True),
                waits=[])
            bcb1_free_pe = mm2
            evr = w.act(lambda e, bks=bks: e.copy(
                out=sb_rep1[0][:], in_=bank(bks, 128, 0, 1024)),
                waits=[("pe", mm2), ("pool", rep1_free[0])])
            bank_free[bks] = [("act", evr)]
            bank_free[bks + 1] = [("act", evr)]
            for dc in range(DC):
                col = dc * NST + n
                sa = dc % 2
                va = w.act(lambda e, dc=dc, col=col, sa=sa: e.activation(
                    out=sb_sa[sa][:], in_=sb_d1l[:, dc * T1:(dc + 1) * T1], func=AF.Exp,
                    scale=sb_Aneg[:, col:col + 1]),
                    waits=[("act", r_done), ("dve", a_free[sa]),
                           ("pool", v_aneg if t == 0 and n == 0 and dc == 0 else 0)])
                vb = w.pool(lambda e, dc=dc, sa=sa, sn=sn: e.tensor_tensor(
                    out=sb_sb[sa][:], in0=sb_u1l[:, dc * T1:(dc + 1) * T1],
                    in1=sb_rep1[0][:], op=ALU.mult),
                    waits=[("pool", u_done), ("dve", b_free[sa]), ("act", evr)])
                vcast = w.dve(lambda e, sa=sa: e.tensor_copy(
                    out=sb_sbc[sa][:], in_=sb_sb[sa][:]),
                    waits=[("pool", vb), (f"dZ{sa}", bst_done[sa])])
                bst_done[sa] = w.dma(f"dZ{sa}",
                                     bstash[:, col * l_core + c0:col * l_core + c0 + T1],
                                     sb_sbc[sa][:], waits=[("dve", vcast)])
                vg = w.dve(lambda e, sa=sa, col=col, t=t: e.scalar_tensor_tensor(
                    out=sb_sh[:], in0=sb_sa[sa][:], scalar=1.0, in1=sb_sb[sa][:],
                    op0=ALU.bypass, op1=ALU.mult,
                    accum_out=sb_Gparts[:, t * DC * NST + col:t * DC * NST + col + 1]),
                    waits=[("act", va), ("pool", vb)])
                a_free[sa] = vg
                b_free[sa] = vg
            rep1_free[0] = w.cnt["pool"]

    r1_dve = w.cnt["dve"]
    r1_pool = w.cnt["pool"]
    r1_act = w.cnt["act"]
    r1_pe = w.cnt["pe"]

    vG = w.dve(lambda e: e.memset(sb_G[:], 0.0), waits=[("dve", r1_dve)])
    for t in range(nt1):
        for dc in range(DC):
            if t == nt1 - 1:
                vsf = w.dve(lambda e, dc=dc: e.memset(sb_sumdtot[:, dc:dc + 1], 0.0),
                            waits=[])
            else:
                vsf = w.dve(lambda e, dc=dc, t=t: e.reduce_sum(
                    out=sb_sumdtot[:, dc:dc + 1],
                    in_=sb_sumdall[:, dc * 16 + t + 1:dc * 16 + nt1], axis=AX),
                    waits=[("dve", r1_dve)])
            w.act(lambda e, dc=dc: e.activation(
                out=sb_wt[:, dc * NST:(dc + 1) * NST],
                in_=sb_Aneg[:, dc * NST:(dc + 1) * NST], func=AF.Exp,
                scale=sb_sumdtot[:, dc:dc + 1]),
                waits=[("dve", vsf)])
        va_wt = w.cnt["act"]
        vmul = w.dve(lambda e, t=t: e.tensor_tensor(
            out=sb_ht[:], in0=sb_Gparts[:, t * DC * NST:(t + 1) * DC * NST],
            in1=sb_wt[:], op=ALU.mult),
            waits=[("act", va_wt), ("dve", vG)])
        vG = w.dve(lambda e: e.tensor_tensor(
            out=sb_gm[:], in0=sb_G[:], in1=sb_ht[:], op=ALU.add),
            waits=[("dve", vmul)])
        vG = w.dve(lambda e: e.tensor_copy(out=sb_G[:], in_=sb_gm[:]),
                   waits=[("dve", vG)])
    vP = 0
    for dc in range(DC):
        vsd = w.dve(lambda e, dc=dc: e.reduce_sum(
            out=sb_sumdtot[:, dc:dc + 1],
            in_=sb_sumdall[:, dc * 16:dc * 16 + nt1], axis=AX),
            waits=[("dve", r1_dve), ("act", w.cnt["act"])])
        vP = w.act(lambda e, dc=dc: e.activation(
            out=sb_P[:, dc * NST:(dc + 1) * NST],
            in_=sb_Aneg[:, dc * NST:(dc + 1) * NST], func=AF.Exp,
            scale=sb_sumdtot[:, dc:dc + 1]),
            waits=[("dve", vsd)])

    d_g = w.dma("dP", gp_src[0:128, :], sb_G[:], waits=[("dve", vG)])
    d_p = w.dma("dP", gp_src[128:256, :], sb_P[:],
                waits=[("act", vP), ("dP", d_g)])
    cc_done = w.cc(lambda e: e.collective_compute(
        "AllGather", ALU.bypass, replica_groups=[list(range(NCORES))],
        ins=[gp_src.ap().opt()], outs=[gp_dst.ap().opt()]),
        waits=[("dP", d_p), ("pool", r1_pool)])
    vH = w.dve(lambda e: e.memset(sb_hin[:], 0.0), waits=[("dve", vG)])
    gp_slot_free = [0, 0]
    for j in range(NCORES):
        s = j % 2
        gj = w.dma(f"dJ{s}", sb_Gj[s][:], gp_dst[j * 256:j * 256 + 128, :],
                   waits=[("cc", cc_done), ("dve", gp_slot_free[s]),
                          (f"dJ{s}", w.cnt[f"dJ{s}"])])
        pj = w.dma(f"dJ{s}", sb_Pj[s][:], gp_dst[j * 256 + 128:(j + 1) * 256, :],
                   waits=[(f"dJ{s}", gj)])
        v1 = w.dve(lambda e, s=s, j=j: e.tensor_scalar(
            out=sb_gm[:], in0=sb_Gj[s][:], scalar1=sb_msk[:, j:j + 1], scalar2=None,
            op0=ALU.mult),
            waits=[(f"dJ{s}", pj), ("dve", vH)])
        v2 = w.dve(lambda e, s=s, j=j: e.tensor_scalar(
            out=sb_pm[:], in0=sb_Pj[s][:], scalar1=sb_msk[:, j:j + 1],
            scalar2=sb_imsk[:, j:j + 1], op0=ALU.mult, op1=ALU.add),
            waits=[("dve", v1)])
        gp_slot_free[s] = v2
        v3 = w.dve(lambda e: e.tensor_tensor(
            out=sb_ht[:], in0=sb_hin[:], in1=sb_pm[:], op=ALU.mult),
            waits=[("dve", v2)])
        vH = w.dve(lambda e: e.tensor_tensor(
            out=sb_hin[:], in0=sb_ht[:], in1=sb_gm[:], op=ALU.add),
            waits=[("dve", v3)])
    v_state_r2 = w.dve(lambda e: e.tensor_copy(out=sb_state[:], in_=sb_hin[:]),
                       waits=[("dve", vH)])

    r1_stack.close()

    r2_stack = ExitStack()
    R = r2_stack.enter_context
    NG, GSZ = 8, 2
    sb_repC = R(nc.sbuf_tensor("sb_repC", [128, GSZ * T2], BF16))
    sb_zz = [R(nc.sbuf_tensor(f"sb_zz{i}", [128, DC * T2], F32)) for i in range(2)]
    sb_x1bf = R(nc.sbuf_tensor("sb_x1bf", [128, DC * T2], BF16))
    sb_gu = R(nc.sbuf_tensor("sb_gu", [128, HM * T2], BF16))
    sb_bcb2 = R(nc.sbuf_tensor("sb_bcb2", [32, T2], BF16))
    sb_dl2 = [R(nc.sbuf_tensor(f"sb_dl2_{i}", [128, T2], F32)) for i in range(2)]
    sb_x2 = [R(nc.sbuf_tensor(f"sb_x2_{i}", [128, T2], F32)) for i in range(2)]
    sb_a2 = [R(nc.sbuf_tensor(f"sb_a2_{i}", [128, T2], F32)) for i in range(2)]
    sb_b2 = [R(nc.sbuf_tensor(f"sb_b2_{i}", [128, 2 * T2], BF16)) for i in range(2)]
    sb_h2 = [R(nc.sbuf_tensor(f"sb_h2_{i}", [128, T2], BF16)) for i in range(2)]
    sb_p2 = [R(nc.sbuf_tensor(f"sb_p2_{i}", [128, T2], BF16)) for i in range(2)]
    sb_zacc = [R(nc.sbuf_tensor(f"sb_zacc{i}", [128, T2], F32)) for i in range(4)]
    sb_zb = [R(nc.sbuf_tensor(f"sb_zb{i}", [128, T2], BF16)) for i in range(1)]
    sb_z2b = [R(nc.sbuf_tensor(f"sb_z2b{i}", [128, T2], BF16)) for i in range(1)]
    sb_rows = R(nc.sbuf_tensor("sb_rows", [128, 3 * T2], F32))
    sb_wg = [R(nc.sbuf_tensor(f"sb_wg{i}", [128, 8 * 128], BF16)) for i in range(2)]
    sb_wu = [R(nc.sbuf_tensor(f"sb_wu{i}", [128, 8 * 128], BF16)) for i in range(2)]
    sb_wd = [R(nc.sbuf_tensor(f"sb_wd{i}", [128, 22 * 128], BF16)) for i in range(2)]
    sb_gt = [R(nc.sbuf_tensor(f"sb_gt{i}", [128, T2], BF16)) for i in range(1)]
    sb_ut = [R(nc.sbuf_tensor(f"sb_ut{i}", [128, T2], BF16)) for i in range(1)]
    sb_xo = [R(nc.sbuf_tensor(f"sb_xo{i}", [128, T2], F32)) for i in range(3)]
    sb_br = R(nc.sbuf_tensor("sb_br", [128, 2 * T2], F32))

    def rr(base, slot):
        return sb_rows[base:base + 1, slot * T2:(slot + 1) * T2]

    r1_cut = [("dve", v_state_r2), ("pool", r1_pool), ("act", max(r1_act, vP)),
              ("pe", r1_pe)]
    for st_name in ("vector", "scalar", "gpsimd", "tensor"):
        w.barrier(st_name, r1_cut)
    w.barrier("sync", r1_cut + dma_all())

    bank_free = [[] for _ in range(8)]
    dl2_free = [[], []]
    x2_free = [[], []]
    a2_free = [0, 0]
    bfree3 = [[0, 0], [0, 0]]
    h2_free = [("dve", 0), ("dve", 0)]
    p2_free = [("dve", 0), ("dve", 0)]
    zb_free = [0, 0]
    gt_free = [0, 0]
    wg_free = [0, 0]
    wu_free = [0, 0]
    wd_free = [0, 0]
    xo_free = [[], [], []]
    zz_free = [[], []]
    repC_free = []
    rep_mm_last = 0
    rows_free = []
    br_free = []
    x1bf_free = []
    gu_free = []
    H = {}

    def emit_scanA(t):
        nonlocal repC_free
        c0 = t * T2
        zb_ = t % 2
        sbz = sb_zz[zb_]
        global_state = {}
        bc_ready = w.dma("dV", sb_bcb2[:], bcstash[:, c0:c0 + T2],
                         waits=[("pe", rep_mm_last), ("dV", w.cnt["dV"])])
        vz_carry = [None] * DC
        zacc0_free = [("pool", 0), ("dve", 0)]
        for g in range(NG):
            n0 = g * GSZ
            repC_ready = 0
            for ni in range(GSZ):
                n = n0 + ni
                bki = ni % 2
                mmC = w.pe(lambda e, n=n, bki=bki: e.matmul(
                    bank(bki), sb_sel[:, (NST + n) * 128:(NST + n + 1) * 128], sb_bcb2[:],
                    start=True, stop=True),
                    waits=bank_free[bki] + [("dV", bc_ready)])
                mmlast = mmC
                repC_ready = w.act(lambda e, ni=ni, bki=bki: e.copy(
                    out=sb_repC[:, ni * T2:(ni + 1) * T2], in_=bank(bki)),
                    waits=[("pe", mmC)] + repC_free)
                bank_free[bki] = [("act", repC_ready)]
            global_state["rep_mm_last"] = mmlast
            for dcp in range(0, DC, 2):
                lane_state = []
                for lane in range(2):
                    dc = dcp + lane
                    s = lane
                    ez = "pool" if lane == 0 else "dve"
                    eop = w.pool if lane == 0 else w.dve
                    d_rdy = w.dma(f"dD{s}", sb_dl2[s][:],
                                  dstash[dc * 128:(dc + 1) * 128, c0:c0 + T2],
                                  waits=dl2_free[s])
                    if g == 0:
                        x_rdy = w.dma(f"dX{s}", sb_x2[s][:],
                                      x_T[dc * 128:(dc + 1) * 128, c0:c0 + T2],
                                      waits=x2_free[s])
                        zsl = 0 if lane == 0 else 2
                        vy = w.act(lambda e, s=s, dc=dc, zsl=zsl: e.activation(
                            out=sb_zacc[zsl][:], in_=sb_x2[s][:], func=AF.Identity,
                            scale=sb_D1[:, dc:dc + 1]),
                            waits=[(f"dX{s}", x_rdy), zacc0_free[lane],
                                   ("pool", v_d1 if t == 0 and dc == 0 else 0)])
                        vz = ("act", vy)
                        x2_free[s] = [vz]
                    else:
                        vz = vz_carry[dc]
                    lane_state.append({"dc": dc, "s": s, "d_rdy": d_rdy,
                                       "vz": vz, "act_last": 0})
                for ni in range(GSZ):
                    n = n0 + ni
                    for lane in range(2):
                        L = lane_state[lane]
                        dc, s = L["dc"], L["s"]
                        col = dc * NST + n
                        va = w.act(lambda e, s=s, col=col: e.activation(
                            out=sb_a2[s][:], in_=sb_dl2[s][:], func=AF.Exp,
                            scale=sb_Aneg[:, col:col + 1]),
                            waits=[(f"dD{s}", L["d_rdy"]), ("dve", a2_free[s])])
                        L["act_last"] = va
                        L["va"] = va
                    for lane in range(2):
                        L = lane_state[lane]
                        dc, s = L["dc"], L["s"]
                        col = dc * NST + n
                        bslot = (g * GSZ + ni) % 2
                        L["vb"] = w.dma(f"dK{s}{bslot}",
                                        sb_b2[s][:, bslot * T2:(bslot + 1) * T2],
                                        bstash[:, col * l_core + c0:col * l_core + c0 + T2],
                                        waits=[("dve", bfree3[s][bslot])])
                        L["bslot"] = bslot
                    for lane in range(2):
                        L = lane_state[lane]
                        dc, s = L["dc"], L["s"]
                        col = dc * NST + n
                        bslot = L["bslot"]
                        vh = w.dve(lambda e, s=s, col=col, bslot=bslot: e.tensor_tensor_scan(
                            out=sb_h2[s][:], data0=sb_a2[s][:],
                            data1=sb_b2[s][:, bslot * T2:(bslot + 1) * T2],
                            initial=sb_state[:, col:col + 1], op0=ALU.mult, op1=ALU.add),
                            waits=[("act", L["va"]), (f"dK{s}{bslot}", L["vb"]),
                                   ("dve", max(v_state_r2, h2_free[s][1] if h2_free[s][0] == "dve" else 0)),
                                   h2_free[s] if h2_free[s][0] != "dve" else None])
                        vst = w.dve(lambda e, s=s, col=col: e.tensor_copy(
                            out=sb_state[:, col:col + 1], in_=sb_h2[s][:, T2 - 1:T2]),
                            waits=[("dve", vh)])
                        a2_free[s] = vst
                        bfree3[s][bslot] = vh
                        L["vh"] = vh
                    for lane in range(2):
                        L = lane_state[lane]
                        s = L["s"]
                        vp = w.pool(lambda e, s=s, ni=ni: e.tensor_tensor(
                            out=sb_p2[s][:], in0=sb_h2[s][:],
                            in1=sb_repC[:, ni * T2:(ni + 1) * T2], op=ALU.mult),
                            waits=[("dve", L["vh"]), ("act", repC_ready), p2_free[s]])
                        h2_free[s] = ("pool", vp)
                        L["vp"] = ("pool", vp)
                    for lane in range(2):
                        L = lane_state[lane]
                        dc, s = L["dc"], L["s"]
                        ez = "pool" if lane == 0 else "dve"
                        eop = w.pool if lane == 0 else w.dve
                        zsl = 0 if lane == 0 else 2
                        into_z = ni == GSZ - 1
                        from_z = g > 0 and ni == 0
                        wl = [L["vp"], L["vz"]]
                        if g == 0 and ni == GSZ - 1:
                            wl += zz_free[zb_]
                        vz = (ez, eop(lambda e, s=s, ni=ni, dc=dc, into_z=into_z, zsl=zsl,
                                      from_z=from_z, sbz=sbz: e.tensor_tensor(
                            out=(sbz[:, dc * T2:(dc + 1) * T2] if into_z
                                 else sb_zacc[zsl + (ni + 1) % 2][:]),
                            in0=(sbz[:, dc * T2:(dc + 1) * T2] if from_z
                                 else sb_zacc[zsl + ni % 2][:]),
                            in1=sb_p2[s][:], op=ALU.add),
                            waits=wl))
                        if g == 0 and ni == 0:
                            zacc0_free[lane] = vz
                        p2_free[s] = vz
                        L["vz"] = vz
                for lane in range(2):
                    L = lane_state[lane]
                    dl2_free[L["s"]] = [("act", L["act_last"])]
                    vz_carry[L["dc"]] = L["vz"]
            repC_free = [("dve", w.cnt["dve"]), ("pool", w.cnt["pool"])]
        global_state["vz_fin"] = list(vz_carry)
        return global_state

    def emit_statsrows(t, gs):
        zb_ = t % 2
        sbz = sb_zz[zb_]
        st2 = 0
        for dc in range(DC):
            s = dc % 2
            vzb = w.act(lambda e, dc=dc, sbz=sbz: e.copy(
                out=sb_zb[0][:], in_=sbz[:, dc * T2:(dc + 1) * T2]),
                waits=[gs["vz_fin"][dc], ("pe", zb_free[0])])
            vz2b = w.act(lambda e, dc=dc, sbz=sbz: e.square(
                out=sb_z2b[0][:], in_=sbz[:, dc * T2:(dc + 1) * T2]),
                waits=[("act", vzb), ("pe", zb_free[1])])
            wl = [("act", vz2b)]
            if dc == 0:
                wl += bank_free[2] + bank_free[3] + [("dS", ld["onescol"] if t == 0 else 0)]
            stz = w.pe(lambda e, dc=dc: e.matmul(
                bank(2, 1), sb_onescol[:], sb_zb[0][:],
                start=(dc == 0), stop=(dc == DC - 1), skip_group_check=True),
                waits=wl)
            st2 = w.pe(lambda e, dc=dc: e.matmul(
                bank(3, 1), sb_onescol[:], sb_z2b[0][:],
                start=(dc == 0), stop=(dc == DC - 1), skip_group_check=True),
                waits=[])
            zb_free[0] = stz
            zb_free[1] = st2
        vm = w.dve(lambda e: e.tensor_scalar(
            out=rr(0, 0), in0=bank(2, 1), scalar1=1.0 / D, scalar2=None, op0=ALU.mult),
            waits=[("pe", st2)] + rows_free)
        vq = w.dve(lambda e: e.tensor_scalar(
            out=rr(0, 1), in0=bank(3, 1), scalar1=1.0 / D, scalar2=None, op0=ALU.mult),
            waits=[("dve", vm)])
        vt = w.dve(lambda e: e.tensor_tensor(out=rr(0, 2), in0=rr(0, 0), in1=rr(0, 0), op=ALU.mult),
                   waits=[("dve", vq)])
        vv = w.dve(lambda e: e.tensor_tensor(out=rr(32, 0), in0=rr(0, 1), in1=rr(0, 2), op=ALU.subtract),
                   waits=[("dve", vt)])
        vve = w.dve(lambda e: e.tensor_scalar(out=rr(32, 1), in0=rr(32, 0), scalar1=EPS,
                                              scalar2=None, op0=ALU.add),
                    waits=[("dve", vv)])
        vsq = w.act(lambda e: e.activation(out=rr(32, 2), in_=rr(32, 1), func=AF.Sqrt),
                    waits=[("dve", vve)])
        vr = w.dve(lambda e: e.reciprocal(out=rr(0, 1), in_=rr(32, 2)), waits=[("act", vsq)])
        vmr = w.dve(lambda e: e.tensor_tensor(out=rr(0, 2), in0=rr(0, 0), in1=rr(0, 1), op=ALU.mult),
                    waits=[("dve", vr)])
        bank_free[2] = [("dve", vm)]
        bank_free[3] = [("dve", vq)]
        gs["vr"] = vr
        gs["vmr"] = vmr

    def emit_ffnA(t, gs):
        nonlocal br_free
        x1_dones = gs.setdefault("x1_dones", [0, 0])
        zb_ = t % 2
        sbz = sb_zz[zb_]
        vr, vmr = gs["vr"], gs["vmr"]
        mmr = w.pe(lambda e: e.matmul(bank(4), sb_onesrow[:], rr(0, 1), start=True, stop=True),
                   waits=bank_free[4] + [("dve", vr),
                                         ("dS", ld["onesrow"] if t == 0 else 0)])
        mmm = w.pe(lambda e: e.matmul(bank(5), sb_onesrow[:], rr(0, 2), start=True, stop=True),
                   waits=bank_free[5] + [("dve", vmr)])
        gs["rows_pe"] = mmm
        evr1 = w.act(lambda e: e.copy(out=sb_br[:, 0:T2], in_=bank(4)),
                     waits=[("pe", mmr)] + br_free)
        evm1 = w.act(lambda e: e.copy(out=sb_br[:, T2:2 * T2], in_=bank(5)),
                     waits=[("pe", mmm)])
        bank_free[4] = [("act", evr1)]
        bank_free[5] = [("act", evm1)]
        x1_last = 0
        for dc in range(DC):
            s = dc % 2
            e1 = w.dve(lambda e, dc=dc, s=s, sbz=sbz: e.tensor_tensor(
                out=sb_xo[s][:], in0=sbz[:, dc * T2:(dc + 1) * T2], in1=sb_br[:, 0:T2],
                op=ALU.mult),
                waits=[("act", evr1)] + xo_free[s] + (x1bf_free if dc < 2 else []))
            x1_last = w.dve(lambda e, dc=dc, s=s: e.tensor_tensor(
                out=sb_x1bf[:, dc * T2:(dc + 1) * T2], in0=sb_xo[s][:], in1=sb_br[:, T2:2 * T2],
                op=ALU.subtract),
                waits=[("act", evm1), ("dve", e1)])
            xo_free[s] = [("dve", x1_last)]
            x1_dones[s] = x1_last
        gs["x1_last"] = x1_last
        gu_last = 0
        for m in range(HM):
            g_mm = 0
            u_mm = 0
            for hf in range(2):
                wgr = w.dma(f"dA{hf}", sb_wg[hf][:],
                            wbf_g[:, (m * DC + hf * 8) * 128:(m * DC + hf * 8 + 8) * 128],
                            waits=[("pe", wg_free[hf])])
                wur = w.dma(f"dB{hf}", sb_wu[hf][:],
                            wbf_u[:, (m * DC + hf * 8) * 128:(m * DC + hf * 8 + 8) * 128],
                            waits=[("pe", wu_free[hf])])
                for kc8 in range(8):
                    kc = hf * 8 + kc8
                    wl = []
                    if kc8 == 0:
                        wl = [(f"dA{hf}", wgr), ("dve", x1_last)]
                        if kc == 0:
                            wl += bank_free[4]
                    g_mm = w.pe(lambda e, m=m, kc=kc, hf=hf, kc8=kc8: e.matmul(
                        bank(4), sb_wg[hf][:, kc8 * 128:(kc8 + 1) * 128],
                        sb_x1bf[:, kc * T2:(kc + 1) * T2],
                        start=(kc == 0), stop=(kc == DC - 1)),
                        waits=wl)
                wg_free[hf] = g_mm
            for hf in range(2):
                wur2 = 0
                for kc8 in range(8):
                    kc = hf * 8 + kc8
                    wl = []
                    if kc8 == 0:
                        wl = [(f"dB{hf}", w.cnt[f"dB{hf}"])]
                        if kc == 0:
                            wl += bank_free[5]
                    u_mm = w.pe(lambda e, m=m, kc=kc, hf=hf, kc8=kc8: e.matmul(
                        bank(5), sb_wu[hf][:, kc8 * 128:(kc8 + 1) * 128],
                        sb_x1bf[:, kc * T2:(kc + 1) * T2],
                        start=(kc == 0), stop=(kc == DC - 1)),
                        waits=wl)
                wu_free[hf] = u_mm
            vg = w.act(lambda e, m=m: e.activation(
                out=sb_gt[0][:], in_=bank(4), func=AF.Silu, scale=sb_sg[:, m:m + 1]),
                waits=[("pe", g_mm), ("pool", gt_free[0]),
                       ("dS", ld["sg"] if t == 0 and m == 0 else 0)])
            bank_free[4] = [("act", vg)]
            vup = w.act(lambda e, m=m: e.activation(
                out=sb_ut[0][:], in_=bank(5), func=AF.Identity, scale=sb_su[:, m:m + 1]),
                waits=[("pe", u_mm),
                       ("dS", ld["su"] if t == 0 and m == 0 else 0)])
            bank_free[5] = [("act", vup)]
            wl = [("act", vup)]
            if m < 2:
                wl += gu_free
            gu_last = w.pool(lambda e, m=m: e.tensor_tensor(
                out=sb_gu[:, m * T2:(m + 1) * T2], in0=sb_gt[0][:], in1=sb_ut[0][:],
                op=ALU.mult),
                waits=wl)
            gt_free[0] = gu_last
        gs["gu_last"] = gu_last

    def emit_ffnB(t, gs):
        nonlocal br_free
        x1_dones = gs["x1_dones"]
        zb_ = t % 2
        sbz = sb_zz[zb_]
        gu_last = gs["gu_last"]
        st4 = 0
        z2_fin = [None] * DC
        for dc in range(DC):
            s = dc % 2
            d_mm = 0
            for hf in range(2):
                wdr = w.dma(f"dE{hf}", sb_wd[hf][:],
                            wbf_d[:, (dc * HM + hf * 22) * 128:(dc * HM + hf * 22 + 22) * 128],
                            waits=[("pe", wd_free[hf])])
                for kh22 in range(22):
                    kh = hf * 22 + kh22
                    wl = []
                    if kh22 == 0:
                        wl = [(f"dE{hf}", wdr), ("pool", gu_last)]
                        if kh == 0:
                            wl += bank_free[6]
                    d_mm = w.pe(lambda e, dc=dc, kh=kh, hf=hf, kh22=kh22: e.matmul(
                        bank(6), sb_wd[hf][:, kh22 * 128:(kh22 + 1) * 128],
                        sb_gu[:, kh * T2:(kh + 1) * T2],
                        start=(kh == 0), stop=(kh == HM - 1)),
                        waits=wl)
                wd_free[hf] = d_mm
            s2 = dc % 2
            ev6 = w.act(lambda e, dc=dc, s2=s2: e.activation(
                out=sb_xo[s2][:], in_=bank(6), func=AF.Identity,
                scale=sb_sd[:, dc:dc + 1]),
                waits=[("pe", d_mm), ("dS", ld["sd"] if t == 0 and dc == 0 else 0)] +
                      xo_free[s2])
            bank_free[6] = [("act", ev6)]
            z2 = w.pool(lambda e, dc=dc, s2=s2, sbz=sbz: e.tensor_tensor(
                out=sbz[:, dc * T2:(dc + 1) * T2], in0=sb_xo[s2][:],
                in1=sb_x1bf[:, dc * T2:(dc + 1) * T2], op=ALU.add),
                waits=[("act", ev6), ("dve", gs["x1_last"])])
            xo_free[s2] = [("pool", z2)]
            z2_fin[dc] = ("pool", z2)
            gs["x1bf_consumed"] = z2
            vzb = w.act(lambda e, dc=dc, sbz=sbz: e.copy(
                out=sb_zb[0][:], in_=sbz[:, dc * T2:(dc + 1) * T2]),
                waits=[("pool", z2), ("pe", zb_free[0])])
            vz2b = w.act(lambda e, dc=dc, sbz=sbz: e.square(
                out=sb_z2b[0][:], in_=sbz[:, dc * T2:(dc + 1) * T2]),
                waits=[("act", vzb), ("pe", zb_free[1])])
            wl = [("act", vz2b)]
            if dc == 0:
                wl += bank_free[4] + bank_free[5]
            stz = w.pe(lambda e, dc=dc: e.matmul(
                bank(4, 1), sb_onescol[:], sb_zb[0][:],
                start=(dc == 0), stop=(dc == DC - 1), skip_group_check=True),
                waits=wl)
            st4 = w.pe(lambda e, dc=dc: e.matmul(
                bank(5, 1), sb_onescol[:], sb_z2b[0][:],
                start=(dc == 0), stop=(dc == DC - 1), skip_group_check=True),
                waits=[])
            zb_free[0] = stz
            zb_free[1] = st4
        vm2 = w.dve(lambda e: e.tensor_scalar(
            out=rr(0, 0), in0=bank(4, 1), scalar1=1.0 / D, scalar2=None, op0=ALU.mult),
            waits=[("pe", st4), ("pe", gs["rows_pe"])])
        vq2 = w.dve(lambda e: e.tensor_scalar(
            out=rr(0, 1), in0=bank(5, 1), scalar1=1.0 / D, scalar2=None, op0=ALU.mult),
            waits=[("dve", vm2)])
        vt2 = w.dve(lambda e: e.tensor_tensor(out=rr(0, 2), in0=rr(0, 0), in1=rr(0, 0), op=ALU.mult),
                    waits=[("dve", vq2)])
        vv2 = w.dve(lambda e: e.tensor_tensor(out=rr(32, 0), in0=rr(0, 1), in1=rr(0, 2), op=ALU.subtract),
                    waits=[("dve", vt2)])
        vve2 = w.dve(lambda e: e.tensor_scalar(out=rr(32, 1), in0=rr(32, 0), scalar1=EPS,
                                               scalar2=None, op0=ALU.add),
                     waits=[("dve", vv2)])
        vsq2 = w.act(lambda e: e.activation(out=rr(32, 2), in_=rr(32, 1), func=AF.Sqrt),
                     waits=[("dve", vve2)])
        vr2 = w.dve(lambda e: e.reciprocal(out=rr(0, 1), in_=rr(32, 2)), waits=[("act", vsq2)])
        vmr2 = w.dve(lambda e: e.tensor_tensor(out=rr(0, 2), in0=rr(0, 0), in1=rr(0, 1), op=ALU.mult),
                     waits=[("dve", vr2)])
        bank_free[4] = [("dve", vm2)]
        bank_free[5] = [("dve", vq2)]
        mmr3 = w.pe(lambda e: e.matmul(bank(6), sb_onesrow[:], rr(0, 1), start=True, stop=True),
                    waits=bank_free[6] + [("dve", vr2)])
        mmm3 = w.pe(lambda e: e.matmul(bank(7), sb_onesrow[:], rr(0, 2), start=True, stop=True),
                    waits=bank_free[7] + [("dve", vmr2)])
        evr3 = w.act(lambda e: e.copy(out=sb_br[:, 0:T2], in_=bank(6)),
                     waits=[("pe", mmr3), ("dve", x1_dones[0]), ("dve", x1_dones[1])])
        evm3 = w.act(lambda e: e.copy(out=sb_br[:, T2:2 * T2], in_=bank(7)),
                     waits=[("pe", mmm3)])
        bank_free[6] = [("act", evr3)]
        bank_free[7] = [("act", evm3)]
        o_last = [0, 0]
        c0 = t * T2
        for dc in range(DC):
            s = dc % 2
            e1 = w.dve(lambda e, dc=dc, s=s, sbz=sbz: e.tensor_tensor(
                out=sb_xo[s][:], in0=sbz[:, dc * T2:(dc + 1) * T2], in1=sb_br[:, 0:T2],
                op=ALU.mult),
                waits=[("act", evr3)] + xo_free[s])
            e2 = w.dve(lambda e, dc=dc, s=s: e.tensor_tensor(
                out=sb_xo[2][:], in0=sb_xo[s][:], in1=sb_br[:, T2:2 * T2],
                op=ALU.subtract),
                waits=[("act", evm3), ("dve", e1)] + xo_free[2])
            od = w.dma(f"dO{s}", out_T[dc * 128:(dc + 1) * 128, c0:c0 + T2], sb_xo[2][:],
                       waits=[("dve", e2)])
            xo_free[s] = [("dve", e2)]
            xo_free[2] = [(f"dO{s}", od)]
            o_last[s] = e2
        bank_free[6] = [("dve", max(o_last))]
        bank_free[7] = [("dve", max(o_last))]
        zz_free[zb_] = [("dve", max(o_last)),
                        ("act", w.cnt["act"]), ("pool", w.cnt["pool"])]
        nonlocal_rows = [("pe", mmm3)]
        return nonlocal_rows

    for t in range(nt2):
        gs = emit_scanA(t)
        rep_mm_last = gs["rep_mm_last"]
        if t > 0:
            rows_free = emit_ffnB(t - 1, H[t - 1])
            x1bf_free = [("pool", H[t - 1].get("x1bf_consumed", 0))]
            gu_free = [("pe", w.cnt["pe"])]
        emit_statsrows(t, gs)
        emit_ffnA(t, gs)
        H[t] = gs
    rows_free = emit_ffnB(nt2 - 1, H[nt2 - 1])

    r2_stack.close()

    w.barrier("sync", dma_all())

    with nc.Block() as block:
        @block.sync
        def _(e):
            for f in w.ops["sync"]:
                f(e)

        @block.tensor
        def _(e):
            for f in w.ops["tensor"]:
                f(e)

        @block.scalar
        def _(e):
            for f in w.ops["scalar"]:
                f(e)

        @block.vector
        def _(e):
            for f in w.ops["vector"]:
                f(e)

        @block.gpsimd
        def _(e):
            for f in w.ops["gpsimd"]:
                f(e)

    stack.close()
    return nc


def _host_inputs(L0_attn_x, L0_attn_W_in, L0_attn_W_dt, L0_attn_b_dt, L0_attn_A_log,
                 L0_attn_D, L0_ffn_wq_gate, L0_ffn_s_gate, L0_ffn_wq_up, L0_ffn_s_up,
                 L0_ffn_wq_down, L0_ffn_s_down, l_core):
    bf16 = ml_dtypes.bfloat16
    x = np.asarray(L0_attn_x, dtype=np.float32)
    w_inT = np.ascontiguousarray(np.asarray(L0_attn_W_in, np.float32).T)
    w_dtT = np.ascontiguousarray(np.asarray(L0_attn_W_dt, np.float32).T)
    b_dt_t = np.ascontiguousarray(np.asarray(L0_attn_b_dt, np.float32).reshape(DC, 128).T)
    a_t = np.ascontiguousarray(
        np.asarray(L0_attn_A_log, np.float32).reshape(DC, 128, NST)
        .transpose(1, 0, 2).reshape(128, DC * NST))
    d_t = np.ascontiguousarray(np.asarray(L0_attn_D, np.float32).reshape(DC, 128).T)
    s_g = np.ascontiguousarray(np.asarray(L0_ffn_s_gate, np.float32).reshape(HM, 128).T)
    s_u = np.ascontiguousarray(np.asarray(L0_ffn_s_up, np.float32).reshape(HM, 128).T)
    s_d = np.ascontiguousarray(np.asarray(L0_ffn_s_down, np.float32).reshape(DC, 128).T)

    wq_gate = np.asarray(L0_ffn_wq_gate, np.int32).reshape(HM, 128, DC, 128)
    wq_g = np.ascontiguousarray(wq_gate.transpose(3, 0, 2, 1).reshape(128, HM * DC * 128))
    wq_up = np.asarray(L0_ffn_wq_up, np.int32).reshape(HM, 128, DC, 128)
    wq_u = np.ascontiguousarray(wq_up.transpose(3, 0, 2, 1).reshape(128, HM * DC * 128))
    wq_down = np.asarray(L0_ffn_wq_down, np.int32).reshape(DC, 128, HM, 128)
    wq_d = np.ascontiguousarray(wq_down.transpose(3, 0, 2, 1).reshape(128, DC * HM * 128))

    sel = np.zeros((32, 2 * NST * 128), dtype=bf16)
    for r in range(32):
        sel[r, r * 128:(r + 1) * 128] = bf16(1.0)
    onescol = np.ones((128, 1), dtype=bf16)
    onesrow = np.ones((1, 128), dtype=np.float32)

    in_maps = []
    for c in range(NCORES):
        b, q = divmod(c, 4)
        xs = np.ascontiguousarray(x[b, q * l_core:(q + 1) * l_core, :].T)
        msk = np.zeros((128, NCORES), dtype=np.float32)
        for j in range(4 * b, c):
            msk[:, j] = 1.0
        imsk = np.ascontiguousarray(1.0 - msk)
        in_maps.append({
            "x_T": xs, "w_inT": w_inT, "w_dtT": w_dtT, "b_dt_t": b_dt_t,
            "a_t": a_t, "d_t": d_t, "s_g": s_g, "s_u": s_u, "s_d": s_d,
            "wq_g": wq_g, "wq_u": wq_u, "wq_d": wq_d, "msk": msk, "imsk": imsk,
            "sel": sel, "onescol": onescol, "onesrow": onesrow,
        })
    return in_maps


def kernel(**inputs):
    l_core = 4096
    in_maps = _host_inputs(l_core=l_core, **inputs)
    nc = build_nc(l_core)
    res = run_bass_kernel_spmd(nc, in_maps, core_ids=list(range(NCORES)))
    out = np.empty((2, 16384, D), dtype=np.float32)
    for c in range(NCORES):
        b, q = divmod(c, 4)
        out[b, q * l_core:(q + 1) * l_core, :] = res.results[c]["out_T"].T
    return out


# version 09
# speedup vs baseline: 1.3001x; 1.0327x over previous
import numpy as np
import ml_dtypes
from contextlib import ExitStack
import concourse.bass as bass
import concourse.mybir as mybir
from concourse.bass_utils import run_bass_kernel_spmd

F32 = mybir.dt.float32
BF16 = mybir.dt.bfloat16
I32 = mybir.dt.int32
AF = mybir.ActivationFunctionType
ALU = mybir.AluOpType
AX = mybir.AxisListType.X

D = 2048
NST = 16
RDT = 64
HID = 5632
DC = D // 128
HM = HID // 128
NCORES = 8
EPS = 1e-5
T1 = 1024
T2 = 512
WCH = 4096


class Walk:

    def __init__(self, nc, sems):
        self.nc = nc
        self.sems = sems
        self.cnt = {k: 0 for k in sems}
        self.ops = {k: [] for k in ("sync", "tensor", "scalar", "vector", "gpsimd")}

    def _emit(self, stream, own, waits, fn, inc_amt=1, self_serialize=False):
        best = {}
        for item in waits:
            if item is None:
                continue
            s, v = item
            if v and v > 0 and v > best.get(s, 0):
                best[s] = v
        if self_serialize and own and self.cnt[own] > 0:
            if best.get(own, 0) < self.cnt[own]:
                best[own] = self.cnt[own]
        clean = sorted(best.items())
        sems = self.sems
        own_sem = sems[own] if own else None

        def run(e, waits=tuple(clean), fn=fn, own_sem=own_sem, inc_amt=inc_amt):
            for s, v in waits:
                e.wait_ge(sems[s], v)
            ins = fn(e)
            if own_sem is not None:
                ins.then_inc(own_sem, inc_amt)

        self.ops[stream].append(run)
        if own:
            self.cnt[own] += inc_amt
            return self.cnt[own]
        return None

    def dma(self, q, out, in_, waits=()):
        return self._emit("sync", q, waits,
                          lambda e: e.dma_start(out=out, in_=in_), 16)

    def pe(self, fn, waits=()):
        return self._emit("tensor", "pe", waits, fn)

    def act(self, fn, waits=()):
        return self._emit("scalar", "act", waits, fn, self_serialize=True)

    def dve(self, fn, waits=()):
        return self._emit("vector", "dve", waits, fn, self_serialize=True)

    def pool(self, fn, waits=()):
        return self._emit("gpsimd", "pool", waits, fn, self_serialize=True)

    def cc(self, fn, waits=()):
        return self._emit("gpsimd", "cc", waits, fn, 1)

    def barrier(self, stream, waits):
        clean = tuple((s, v) for (s, v) in waits if v and v > 0)
        sems = self.sems

        def run(e, waits=clean):
            for s, v in waits:
                e.wait_ge(sems[s], v)

        self.ops[stream].append(run)


def ep_wait(eng, val):
    return (eng, val)


def build_nc(l_core):
    nt1 = l_core // T1
    nt2 = l_core // T2
    nc = bass.Bass("TRN2", target_bir_lowering=False)

    x_T = nc.dram_tensor("x_T", [D, l_core], F32, kind="ExternalInput").ap()
    w_inT = nc.dram_tensor("w_inT", [D, 96], F32, kind="ExternalInput").ap()
    w_dtT = nc.dram_tensor("w_dtT", [RDT, D], F32, kind="ExternalInput").ap()
    b_dt_t = nc.dram_tensor("b_dt_t", [128, DC], F32, kind="ExternalInput").ap()
    a_t = nc.dram_tensor("a_t", [128, DC * NST], F32, kind="ExternalInput").ap()
    d_t = nc.dram_tensor("d_t", [128, DC], F32, kind="ExternalInput").ap()
    s_g = nc.dram_tensor("s_g", [128, HM], F32, kind="ExternalInput").ap()
    s_u = nc.dram_tensor("s_u", [128, HM], F32, kind="ExternalInput").ap()
    s_d = nc.dram_tensor("s_d", [128, DC], F32, kind="ExternalInput").ap()
    wq_g = nc.dram_tensor("wq_g", [128, HM * DC * 128], I32, kind="ExternalInput").ap()
    wq_u = nc.dram_tensor("wq_u", [128, HM * DC * 128], I32, kind="ExternalInput").ap()
    wq_d = nc.dram_tensor("wq_d", [128, DC * HM * 128], I32, kind="ExternalInput").ap()
    msk_in = nc.dram_tensor("msk", [128, NCORES], F32, kind="ExternalInput").ap()
    imsk_in = nc.dram_tensor("imsk", [128, NCORES], F32, kind="ExternalInput").ap()
    sel_in = nc.dram_tensor("sel", [32, 2 * NST * 128], BF16, kind="ExternalInput").ap()
    onescol_in = nc.dram_tensor("onescol", [128, 1], BF16, kind="ExternalInput").ap()
    onesrow_in = nc.dram_tensor("onesrow", [1, 128], F32, kind="ExternalInput").ap()

    out_T = nc.dram_tensor("out_T", [D, l_core], F32, kind="ExternalOutput").ap()

    dstash = nc.dram_tensor("dstash", [D, l_core], F32)
    bcstash = nc.dram_tensor("bcstash", [32, l_core], BF16)
    bstash = nc.dram_tensor("bstash", [128, DC * NST * l_core], BF16)
    wbf_g = nc.dram_tensor("wbf_g", [128, HM * DC * 128], BF16)
    wbf_u = nc.dram_tensor("wbf_u", [128, HM * DC * 128], BF16)
    wbf_d = nc.dram_tensor("wbf_d", [128, DC * HM * 128], BF16)
    gp_src = nc.dram_tensor("gp_src", [256, DC * NST], F32)
    gp_dst = nc.dram_tensor("gp_dst", [256 * NCORES, DC * NST], F32)

    stack = ExitStack()
    sems = {}
    dma_sem_names = (["dS", "dT", "dV", "dP"]
                     + [f"d{k}{i}" for k in "CIOXDABEJ" for i in range(2)]
                     + [f"dK{i}{j}" for i in range(2) for j in range(2)]
                     + ["dZ0", "dZ1"])
    for name in dma_sem_names + ["pe", "act", "dve", "pool", "cc"]:
        sems[name] = stack.enter_context(nc.semaphore(name))
    w = Walk(nc, sems)

    E = stack.enter_context
    sb_bdt = E(nc.sbuf_tensor("sb_bdt", [128, DC], F32))
    sb_Apos = E(nc.sbuf_tensor("sb_Apos", [128, DC * NST], F32))
    sb_Aneg = E(nc.sbuf_tensor("sb_Aneg", [128, DC * NST], F32))
    sb_Dt = E(nc.sbuf_tensor("sb_Dt", [128, DC], F32))
    sb_D1 = E(nc.sbuf_tensor("sb_D1", [128, DC], F32))
    sb_sg = E(nc.sbuf_tensor("sb_sg", [128, HM], F32))
    sb_su = E(nc.sbuf_tensor("sb_su", [128, HM], F32))
    sb_sd = E(nc.sbuf_tensor("sb_sd", [128, DC], F32))
    sb_msk = E(nc.sbuf_tensor("sb_msk", [128, NCORES], F32))
    sb_imsk = E(nc.sbuf_tensor("sb_imsk", [128, NCORES], F32))
    sb_sel = E(nc.sbuf_tensor("sb_sel", [32, 2 * NST * 128], BF16))
    sb_onescol = E(nc.sbuf_tensor("sb_onescol", [128, 1], BF16))
    sb_onesrow = E(nc.sbuf_tensor("sb_onesrow", [1, 128], F32))
    sb_state = E(nc.sbuf_tensor("sb_state", [128, DC * NST], F32))

    ps8 = E(nc.psum_tensor("ps8", [128, 4096], F32))

    def bank(i, pn=128, c0=0, cn=512):
        return ps8[0:pn, i * 512 + c0:i * 512 + c0 + cn]

    def sdma(out, in_):
        return w.dma("dS", out, in_, waits=[("dS", w.cnt["dS"])])

    def dma_all():
        return [(n, w.cnt[n]) for n in dma_sem_names]

    ld = {}
    ld["bdt"] = sdma(sb_bdt[:], b_dt_t)
    ld["alog"] = sdma(sb_Apos[:], a_t)
    ld["dt"] = sdma(sb_Dt[:], d_t)
    ld["sg"] = sdma(sb_sg[:], s_g)
    ld["su"] = sdma(sb_su[:], s_u)
    ld["sd"] = sdma(sb_sd[:], s_d)
    ld["msk"] = sdma(sb_msk[:], msk_in)
    ld["imsk"] = sdma(sb_imsk[:], imsk_in)
    ld["sel"] = sdma(sb_sel[:], sel_in)
    ld["onescol"] = sdma(sb_onescol[:], onescol_in)
    ld["onesrow"] = sdma(sb_onesrow[:], onesrow_in)

    v_apos = w.act(lambda e: e.activation(out=sb_Apos[:], in_=sb_Apos[:], func=AF.Exp),
                   waits=[("dS", ld["alog"])])
    v_aneg = w.pool(lambda e: e.tensor_scalar(out=sb_Aneg[:], in0=sb_Apos[:],
                                              scalar1=-1.0, scalar2=None, op0=ALU.mult),
                    waits=[("act", v_apos)])
    v_d1 = w.pool(lambda e: e.tensor_scalar(out=sb_D1[:], in0=sb_Dt[:],
                                            scalar1=1.0, scalar2=None, op0=ALU.add),
                  waits=[("dS", ld["dt"])])
    v_state0 = w.dve(lambda e: e.memset(sb_state[:], 0.0))

    conv_stack = ExitStack()
    cv_i = [conv_stack.enter_context(nc.sbuf_tensor(f"cv_i{i}", [128, WCH], I32)) for i in range(2)]
    cv_b = [conv_stack.enter_context(nc.sbuf_tensor(f"cv_b{i}", [128, WCH], BF16)) for i in range(2)]
    nch = HM * DC * 128 // WCH
    cv_in_done = [0, 0]
    cv_ts_done = [0, 0]
    cv_out_done = [0, 0]
    ci = 0
    for src, dst in ((wq_g, wbf_g), (wq_u, wbf_u), (wq_d, wbf_d)):
        for ch in range(nch):
            s = ci % 2
            cv_in_done[s] = w.dma(f"dC{s}", cv_i[s][:], src[:, ch * WCH:(ch + 1) * WCH],
                                  waits=[("dve", cv_ts_done[s])])
            cv_ts_done[s] = w.dve(
                lambda e, s=s: e.tensor_scalar(out=cv_b[s][:], in0=cv_i[s][:],
                                               scalar1=128.0, scalar2=None,
                                               op0=ALU.subtract),
                waits=[(f"dC{s}", cv_in_done[s]), (f"dI{s}", cv_out_done[s])])
            cv_out_done[s] = w.dma(f"dI{s}", dst[:, ch * WCH:(ch + 1) * WCH], cv_b[s][:],
                                   waits=[("dve", cv_ts_done[s])])
            ci += 1
    conv_done_dve = w.cnt["dve"]
    conv_stack.close()

    w.barrier("sync", [("dve", conv_done_dve)] + dma_all())

    r1_stack = ExitStack()
    R = r1_stack.enter_context
    sb_winT = R(nc.sbuf_tensor("sb_winT", [128, DC * 96], F32))
    sb_wdtT = R(nc.sbuf_tensor("sb_wdtT", [RDT, D], F32))
    sb_G = R(nc.sbuf_tensor("sb_G", [128, DC * NST], F32))
    sb_P = R(nc.sbuf_tensor("sb_P", [128, DC * NST], F32))
    sb_hin = R(nc.sbuf_tensor("sb_hin", [128, DC * NST], F32))
    sb_ht = R(nc.sbuf_tensor("sb_ht", [128, DC * NST], F32))
    sb_gm = R(nc.sbuf_tensor("sb_gm", [128, DC * NST], F32))
    sb_pm = R(nc.sbuf_tensor("sb_pm", [128, DC * NST], F32))
    sb_ASend = sb_pm
    sb_Gj = [R(nc.sbuf_tensor(f"sb_Gj{i}", [128, DC * NST], F32)) for i in range(2)]
    sb_Pj = [R(nc.sbuf_tensor(f"sb_Pj{i}", [128, DC * NST], F32)) for i in range(2)]
    sb_sumdall = R(nc.sbuf_tensor("sb_sumdall", [128, DC * 16], F32))
    sb_sumdtot = R(nc.sbuf_tensor("sb_sumdtot", [128, DC], F32))
    sb_d1l = R(nc.sbuf_tensor("sb_d1l", [128, DC * T1], F32))
    sb_u1l = R(nc.sbuf_tensor("sb_u1l", [128, DC * T1], F32))
    sb_xs = [R(nc.sbuf_tensor(f"sb_xs{i}", [128, T1], F32)) for i in range(2)]
    sb_dlr = R(nc.sbuf_tensor("sb_dlr", [RDT, T1], F32))
    sb_bcb1 = R(nc.sbuf_tensor("sb_bcb1", [32, T1], BF16))
    sb_rep1 = [R(nc.sbuf_tensor(f"sb_rep1_{i}", [128, T1], BF16)) for i in range(1)]
    sb_sa = [R(nc.sbuf_tensor(f"sb_sa{i}", [128, T1], BF16)) for i in range(2)]
    sb_sb = [R(nc.sbuf_tensor(f"sb_sb{i}", [128, T1], F32)) for i in range(2)]
    sb_sbc = [R(nc.sbuf_tensor(f"sb_sbc{i}", [128, T1], BF16)) for i in range(2)]
    sb_sh = R(nc.sbuf_tensor("sb_sh", [128, T1], BF16))
    sb_S = R(nc.sbuf_tensor("sb_S", [128, T1], F32))
    sb_Gparts = R(nc.sbuf_tensor("sb_Gparts", [128, 4 * DC * NST], F32))
    sb_scar = R(nc.sbuf_tensor("sb_scar", [128, DC], F32))
    sb_wt = R(nc.sbuf_tensor("sb_wt", [128, DC * NST], BF16))

    for dc in range(DC):
        ld["winT"] = sdma(sb_winT[:, dc * 96:(dc + 1) * 96],
                          w_inT[dc * 128:(dc + 1) * 128, :])
    ld["wdtT"] = sdma(sb_wdtT[:], w_dtT)

    xs_free = [[], []]
    a_free = [0, 0]
    b_free = [0, 0]
    h_free = [0, 0]
    rep1_free = [0, 0]
    bank_free = [[] for _ in range(8)]
    v_softadd = 0
    bst_done = [0, 0]
    ev_bc_last = 0
    dlr_free_pe = 0
    bcb1_free_pe = 0
    stash_last = 0

    for t in range(nt1):
        c0 = t * T1
        pool_cut = w.cnt["pool"]
        lr_mm = [0, 0]
        bc_mm = [0, 0]
        for dc in range(DC):
            s = dc % 2
            xr = w.dma(f"dX{s}", sb_xs[s][:], x_T[dc * 128:(dc + 1) * 128, c0:c0 + T1],
                       waits=xs_free[s])
            for cc in range(2):
                first = dc == 0
                last = dc == DC - 1
                wl = [(f"dX{s}", xr)]
                if first:
                    wl += [("dS", ld["winT"])] + bank_free[cc] + bank_free[2 + cc]
                lr_mm[cc] = w.pe(
                    lambda e, s=s, cc=cc, dc=dc, first=first, last=last: e.matmul(
                        bank(cc, RDT), sb_winT[:, dc * 96:dc * 96 + RDT],
                        sb_xs[s][:, cc * 512:(cc + 1) * 512],
                        start=first, stop=last, skip_group_check=True),
                    waits=wl)
                bc_mm[cc] = w.pe(
                    lambda e, s=s, cc=cc, dc=dc, first=first, last=last: e.matmul(
                        bank(2 + cc, 32), sb_winT[:, dc * 96 + RDT:(dc + 1) * 96],
                        sb_xs[s][:, cc * 512:(cc + 1) * 512],
                        start=first, stop=last, skip_group_check=True),
                    waits=[])
            xs_free[s] = [("pe", bc_mm[1])]
        ev_lr = [0, 0]
        ev_bc = [0, 0]
        for cc in range(2):
            ev_lr[cc] = w.act(
                lambda e, cc=cc: e.copy(out=sb_dlr[:, cc * 512:(cc + 1) * 512],
                                        in_=bank(cc, RDT)),
                waits=[("pe", lr_mm[cc]), ("pe", dlr_free_pe)])
            ev_bc[cc] = w.act(
                lambda e, cc=cc: e.copy(out=sb_bcb1[:, cc * 512:(cc + 1) * 512],
                                        in_=bank(2 + cc, 32)),
                waits=[("pe", bc_mm[cc]), ("pe", bcb1_free_pe)])
        bank_free[0] = [("act", ev_lr[0])]
        bank_free[1] = [("act", ev_lr[1])]
        bank_free[2] = [("act", ev_bc[0])]
        bank_free[3] = [("act", ev_bc[1])]
        ev_bc_last = max(ev_bc)
        w.dma("dT", bcstash[:, c0:c0 + T1], sb_bcb1[:],
              waits=[("act", ev_bc_last), ("dT", w.cnt["dT"])])

        for dc in range(DC):
            for cc in range(2):
                bki = 4 + cc
                pcol = dc * T1 + cc * 512
                mmv = w.pe(
                    lambda e, dc=dc, cc=cc, bki=bki: e.matmul(
                        bank(bki), sb_wdtT[:, dc * 128:(dc + 1) * 128],
                        sb_dlr[:, cc * 512:(cc + 1) * 512],
                        start=True, stop=True),
                    waits=bank_free[bki] + [("act", max(ev_lr)),
                                            ("dS", ld["wdtT"] if t == 0 and dc == 0 else 0)])
                dlr_free_pe = mmv
                v1 = w.act(lambda e, dc=dc, bki=bki: e.activation(
                    out=sb_S[:, 0:512], in_=bank(bki), func=AF.Abs,
                    bias=sb_bdt[:, dc:dc + 1]),
                    waits=[("pe", mmv), ("dve", max(a_free[0], a_free[1])),
                           ("pool", pool_cut if dc == 0 and cc == 0 else 0),
                           ("dS", ld["bdt"] if t == 0 and dc == 0 else 0),
                           ("dT", stash_last if dc == 0 and cc == 0 else 0)])
                v2 = w.act(lambda e: e.activation(
                    out=sb_sb[1][:, 0:512], in_=sb_S[:, 0:512], func=AF.Exp,
                    scale=-1.0),
                    waits=[("act", v1)])
                v3 = w.act(lambda e: e.activation(
                    out=sb_S[:, 512:1024], in_=sb_sb[1][:, 0:512], func=AF.Ln,
                    bias=1.0),
                    waits=[("act", v2), ("pool", v_softadd)])
                v4 = w.act(lambda e, dc=dc, bki=bki: e.activation(
                    out=sb_sb[1][:, 512:1024], in_=bank(bki), func=AF.Relu,
                    bias=sb_bdt[:, dc:dc + 1]),
                    waits=[("act", v3)])
                bank_free[bki] = [("act", v4)]
                v_softadd = w.pool(lambda e, pcol=pcol: e.tensor_tensor(
                    out=sb_d1l[:, pcol:pcol + 512], in0=sb_S[:, 512:1024],
                    in1=sb_sb[1][:, 512:1024], op=ALU.add),
                    waits=[("act", v4)])
        delta_done = v_softadd

        for dc in range(DC):
            w.dve(lambda e, dc=dc, t=t: e.reduce_sum(
                out=sb_sumdall[:, dc * 16 + t:dc * 16 + t + 1],
                in_=sb_d1l[:, dc * T1:(dc + 1) * T1], axis=AX),
                waits=[("pool", delta_done)])

        u_done = 0
        for dc in range(DC):
            s = dc % 2
            stash_last = w.dma("dT", dstash[dc * 128:(dc + 1) * 128, c0:c0 + T1],
                               sb_d1l[:, dc * T1:(dc + 1) * T1],
                               waits=[("pool", delta_done), ("dT", w.cnt["dT"])])
            xr = w.dma(f"dX{s}", sb_xs[s][:], x_T[dc * 128:(dc + 1) * 128, c0:c0 + T1],
                       waits=xs_free[s])
            u_done = w.pool(lambda e, dc=dc, s=s: e.tensor_tensor(
                out=sb_u1l[:, dc * T1:(dc + 1) * T1], in0=sb_d1l[:, dc * T1:(dc + 1) * T1],
                in1=sb_xs[s][:], op=ALU.mult),
                waits=[(f"dX{s}", xr), ("pool", delta_done)])
            xs_free[s] = [("pool", u_done)]

        if t == 0:
            v_scar = w.dve(lambda e: e.memset(sb_scar[:], 0.0), waits=[])

        def Sblk(dc):
            return sb_S[:] if dc == 0 else sb_d1l[:, (dc - 1) * T1:dc * T1]

        asend_done = 0
        for dc in range(DC):
            vS = w.dve(lambda e, dc=dc: e.tensor_tensor_scan(
                out=Sblk(dc), data0=sb_d1l[:, dc * T1:(dc + 1) * T1],
                data1=sb_d1l[:, dc * T1:(dc + 1) * T1],
                initial=sb_scar[:, dc:dc + 1], op0=ALU.add, op1=ALU.bypass),
                waits=[("pool", u_done), ("dT", stash_last)])
            vC = w.dve(lambda e, dc=dc: e.tensor_copy(
                out=sb_scar[:, dc:dc + 1], in_=Sblk(dc)[:, T1 - 1:T1]),
                waits=[("dve", vS)])
            asend_done = w.dve(lambda e, dc=dc: e.tensor_scalar(
                out=sb_ASend[:, dc * NST:(dc + 1) * NST],
                in0=sb_Aneg[:, dc * NST:(dc + 1) * NST],
                scalar1=sb_scar[:, dc:dc + 1], scalar2=None, op0=ALU.mult),
                waits=[("dve", vC), ("pool", v_aneg if t == 0 and dc == 0 else 0)])

        for n in range(NST):
            sn = n % 2
            bks = 0 if sn == 0 else 2
            mm1 = w.pe(lambda e, n=n, bks=bks: e.matmul(
                bank(bks), sb_sel[:, n * 128:(n + 1) * 128], sb_bcb1[:, 0:512],
                start=True, stop=True),
                waits=bank_free[bks] + bank_free[bks + 1] +
                      [("act", ev_bc_last), ("dS", ld["sel"] if t == 0 and n == 0 else 0)])
            mm2 = w.pe(lambda e, n=n, bks=bks: e.matmul(
                bank(bks + 1), sb_sel[:, n * 128:(n + 1) * 128], sb_bcb1[:, 512:1024],
                start=True, stop=True),
                waits=[])
            bcb1_free_pe = mm2
            evr = w.act(lambda e, bks=bks: e.copy(
                out=sb_rep1[0][:], in_=bank(bks, 128, 0, 1024)),
                waits=[("pe", mm2), ("pool", rep1_free[0])])
            bank_free[bks] = [("act", evr)]
            bank_free[bks + 1] = [("act", evr)]
            for dc in range(DC):
                col = dc * NST + n
                sa = dc % 2
                va = w.act(lambda e, dc=dc, col=col, sa=sa: e.activation(
                    out=sb_sa[sa][:], in_=Sblk(dc), func=AF.Exp,
                    scale=sb_Apos[:, col:col + 1],
                    bias=sb_ASend[:, col:col + 1]),
                    waits=[("dve", max(asend_done, a_free[sa])),
                           ("act", v_apos if t == 0 and n == 0 and dc == 0 else 0)])
                vb = w.pool(lambda e, dc=dc, sa=sa, sn=sn: e.tensor_tensor(
                    out=sb_sb[sa][:], in0=sb_u1l[:, dc * T1:(dc + 1) * T1],
                    in1=sb_rep1[0][:], op=ALU.mult),
                    waits=[("pool", u_done), ("dve", b_free[sa]), ("act", evr)])
                vcast = w.dve(lambda e, sa=sa: e.tensor_copy(
                    out=sb_sbc[sa][:], in_=sb_sb[sa][:]),
                    waits=[("pool", vb), (f"dZ{sa}", bst_done[sa])])
                bst_done[sa] = w.dma(f"dZ{sa}",
                                     bstash[:, col * l_core + c0:col * l_core + c0 + T1],
                                     sb_sbc[sa][:], waits=[("dve", vcast)])
                vg = w.dve(lambda e, sa=sa, col=col, t=t: e.scalar_tensor_tensor(
                    out=sb_sh[:], in0=sb_sa[sa][:], scalar=1.0, in1=sb_sb[sa][:],
                    op0=ALU.bypass, op1=ALU.mult,
                    accum_out=sb_Gparts[:, t * DC * NST + col:t * DC * NST + col + 1]),
                    waits=[("act", va), ("pool", vb)])
                a_free[sa] = vg
                b_free[sa] = vg
            rep1_free[0] = w.cnt["pool"]

    r1_dve = w.cnt["dve"]
    r1_pool = w.cnt["pool"]
    r1_act = w.cnt["act"]
    r1_pe = w.cnt["pe"]

    vG = w.dve(lambda e: e.memset(sb_G[:], 0.0), waits=[("dve", r1_dve)])
    for t in range(nt1):
        for dc in range(DC):
            if t == nt1 - 1:
                vsf = w.dve(lambda e, dc=dc: e.memset(sb_sumdtot[:, dc:dc + 1], 0.0),
                            waits=[])
            else:
                vsf = w.dve(lambda e, dc=dc, t=t: e.reduce_sum(
                    out=sb_sumdtot[:, dc:dc + 1],
                    in_=sb_sumdall[:, dc * 16 + t + 1:dc * 16 + nt1], axis=AX),
                    waits=[("dve", r1_dve)])
            w.act(lambda e, dc=dc: e.activation(
                out=sb_wt[:, dc * NST:(dc + 1) * NST],
                in_=sb_Aneg[:, dc * NST:(dc + 1) * NST], func=AF.Exp,
                scale=sb_sumdtot[:, dc:dc + 1]),
                waits=[("dve", vsf)])
        va_wt = w.cnt["act"]
        vmul = w.dve(lambda e, t=t: e.tensor_tensor(
            out=sb_ht[:], in0=sb_Gparts[:, t * DC * NST:(t + 1) * DC * NST],
            in1=sb_wt[:], op=ALU.mult),
            waits=[("act", va_wt), ("dve", vG)])
        vG = w.dve(lambda e: e.tensor_tensor(
            out=sb_gm[:], in0=sb_G[:], in1=sb_ht[:], op=ALU.add),
            waits=[("dve", vmul)])
        vG = w.dve(lambda e: e.tensor_copy(out=sb_G[:], in_=sb_gm[:]),
                   waits=[("dve", vG)])
    vP = 0
    for dc in range(DC):
        vsd = w.dve(lambda e, dc=dc: e.reduce_sum(
            out=sb_sumdtot[:, dc:dc + 1],
            in_=sb_sumdall[:, dc * 16:dc * 16 + nt1], axis=AX),
            waits=[("dve", r1_dve), ("act", w.cnt["act"])])
        vP = w.act(lambda e, dc=dc: e.activation(
            out=sb_P[:, dc * NST:(dc + 1) * NST],
            in_=sb_Aneg[:, dc * NST:(dc + 1) * NST], func=AF.Exp,
            scale=sb_sumdtot[:, dc:dc + 1]),
            waits=[("dve", vsd)])

    d_g = w.dma("dP", gp_src[0:128, :], sb_G[:], waits=[("dve", vG)])
    d_p = w.dma("dP", gp_src[128:256, :], sb_P[:],
                waits=[("act", vP), ("dP", d_g)])
    cc_done = w.cc(lambda e: e.collective_compute(
        "AllGather", ALU.bypass, replica_groups=[list(range(NCORES))],
        ins=[gp_src.ap().opt()], outs=[gp_dst.ap().opt()]),
        waits=[("dP", d_p), ("pool", r1_pool)])
    vH = w.dve(lambda e: e.memset(sb_hin[:], 0.0), waits=[("dve", vG)])
    gp_slot_free = [0, 0]
    for j in range(NCORES):
        s = j % 2
        gj = w.dma(f"dJ{s}", sb_Gj[s][:], gp_dst[j * 256:j * 256 + 128, :],
                   waits=[("cc", cc_done), ("dve", gp_slot_free[s]),
                          (f"dJ{s}", w.cnt[f"dJ{s}"])])
        pj = w.dma(f"dJ{s}", sb_Pj[s][:], gp_dst[j * 256 + 128:(j + 1) * 256, :],
                   waits=[(f"dJ{s}", gj)])
        v1 = w.dve(lambda e, s=s, j=j: e.tensor_scalar(
            out=sb_gm[:], in0=sb_Gj[s][:], scalar1=sb_msk[:, j:j + 1], scalar2=None,
            op0=ALU.mult),
            waits=[(f"dJ{s}", pj), ("dve", vH)])
        v2 = w.dve(lambda e, s=s, j=j: e.tensor_scalar(
            out=sb_pm[:], in0=sb_Pj[s][:], scalar1=sb_msk[:, j:j + 1],
            scalar2=sb_imsk[:, j:j + 1], op0=ALU.mult, op1=ALU.add),
            waits=[("dve", v1), ("act", r1_act if j == 0 else 0)])
        gp_slot_free[s] = v2
        v3 = w.dve(lambda e: e.tensor_tensor(
            out=sb_ht[:], in0=sb_hin[:], in1=sb_pm[:], op=ALU.mult),
            waits=[("dve", v2)])
        vH = w.dve(lambda e: e.tensor_tensor(
            out=sb_hin[:], in0=sb_ht[:], in1=sb_gm[:], op=ALU.add),
            waits=[("dve", v3)])
    v_state_r2 = w.dve(lambda e: e.tensor_copy(out=sb_state[:], in_=sb_hin[:]),
                       waits=[("dve", vH)])

    r1_stack.close()

    r2_stack = ExitStack()
    R = r2_stack.enter_context
    NG, GSZ = 8, 2
    sb_repC = R(nc.sbuf_tensor("sb_repC", [128, GSZ * T2], BF16))
    sb_zz = [R(nc.sbuf_tensor(f"sb_zz{i}", [128, DC * T2], F32)) for i in range(2)]
    sb_x1bf = R(nc.sbuf_tensor("sb_x1bf", [128, DC * T2], BF16))
    sb_gu = R(nc.sbuf_tensor("sb_gu", [128, HM * T2], BF16))
    sb_bcb2 = R(nc.sbuf_tensor("sb_bcb2", [32, T2], BF16))
    sb_dl2 = [R(nc.sbuf_tensor(f"sb_dl2_{i}", [128, T2], F32)) for i in range(2)]
    sb_x2 = [R(nc.sbuf_tensor(f"sb_x2_{i}", [128, T2], F32)) for i in range(2)]
    sb_a2 = [R(nc.sbuf_tensor(f"sb_a2_{i}", [128, T2], F32)) for i in range(2)]
    sb_b2 = [R(nc.sbuf_tensor(f"sb_b2_{i}", [128, 2 * T2], BF16)) for i in range(2)]
    sb_h2 = [R(nc.sbuf_tensor(f"sb_h2_{i}", [128, T2], BF16)) for i in range(2)]
    sb_p2 = [R(nc.sbuf_tensor(f"sb_p2_{i}", [128, T2], BF16)) for i in range(2)]
    sb_zacc = [R(nc.sbuf_tensor(f"sb_zacc{i}", [128, T2], F32)) for i in range(4)]
    sb_zb = [R(nc.sbuf_tensor(f"sb_zb{i}", [128, T2], BF16)) for i in range(1)]
    sb_z2b = [R(nc.sbuf_tensor(f"sb_z2b{i}", [128, T2], BF16)) for i in range(1)]
    sb_rows = R(nc.sbuf_tensor("sb_rows", [128, 3 * T2], F32))
    sb_wg = [R(nc.sbuf_tensor(f"sb_wg{i}", [128, 8 * 128], BF16)) for i in range(2)]
    sb_wu = [R(nc.sbuf_tensor(f"sb_wu{i}", [128, 8 * 128], BF16)) for i in range(2)]
    sb_wd = [R(nc.sbuf_tensor(f"sb_wd{i}", [128, 22 * 128], BF16)) for i in range(2)]
    sb_gt = [R(nc.sbuf_tensor(f"sb_gt{i}", [128, T2], BF16)) for i in range(1)]
    sb_ut = [R(nc.sbuf_tensor(f"sb_ut{i}", [128, T2], BF16)) for i in range(1)]
    sb_xo = [R(nc.sbuf_tensor(f"sb_xo{i}", [128, T2], F32)) for i in range(3)]
    sb_br = R(nc.sbuf_tensor("sb_br", [128, 2 * T2], F32))

    def rr(base, slot):
        return sb_rows[base:base + 1, slot * T2:(slot + 1) * T2]

    r1_cut = [("dve", v_state_r2), ("pool", r1_pool), ("act", max(r1_act, vP)),
              ("pe", r1_pe)]
    for st_name in ("vector", "scalar", "gpsimd", "tensor"):
        w.barrier(st_name, r1_cut)
    w.barrier("sync", r1_cut + dma_all())

    bank_free = [[] for _ in range(8)]
    dl2_free = [[], []]
    x2_free = [[], []]
    a2_free = [0, 0]
    bfree3 = [[0, 0], [0, 0]]
    h2_free = [("dve", 0), ("dve", 0)]
    p2_free = [("dve", 0), ("dve", 0)]
    zb_free = [0, 0]
    gt_free = [0, 0]
    wg_free = [0, 0]
    wu_free = [0, 0]
    wd_free = [0, 0]
    xo_free = [[], [], []]
    zz_free = [[], []]
    repC_free = []
    rep_mm_last = 0
    rows_free = []
    br_free = []
    x1bf_free = []
    gu_free = []
    H = {}

    def emit_scanA(t):
        nonlocal repC_free
        c0 = t * T2
        zb_ = t % 2
        sbz = sb_zz[zb_]
        global_state = {}
        bc_ready = w.dma("dV", sb_bcb2[:], bcstash[:, c0:c0 + T2],
                         waits=[("pe", rep_mm_last), ("dV", w.cnt["dV"])])
        vz_carry = [None] * DC
        zacc0_free = [("pool", 0), ("dve", 0)]
        for g in range(NG):
            n0 = g * GSZ
            repC_ready = 0
            for ni in range(GSZ):
                n = n0 + ni
                bki = ni % 2
                mmC = w.pe(lambda e, n=n, bki=bki: e.matmul(
                    bank(bki), sb_sel[:, (NST + n) * 128:(NST + n + 1) * 128], sb_bcb2[:],
                    start=True, stop=True),
                    waits=bank_free[bki] + [("dV", bc_ready)])
                mmlast = mmC
                repC_ready = w.act(lambda e, ni=ni, bki=bki: e.copy(
                    out=sb_repC[:, ni * T2:(ni + 1) * T2], in_=bank(bki)),
                    waits=[("pe", mmC)] + repC_free)
                bank_free[bki] = [("act", repC_ready)]
            global_state["rep_mm_last"] = mmlast
            for dcp in range(0, DC, 2):
                lane_state = []
                for lane in range(2):
                    dc = dcp + lane
                    s = lane
                    ez = "pool" if lane == 0 else "dve"
                    eop = w.pool if lane == 0 else w.dve
                    d_rdy = w.dma(f"dD{s}", sb_dl2[s][:],
                                  dstash[dc * 128:(dc + 1) * 128, c0:c0 + T2],
                                  waits=dl2_free[s])
                    if g == 0:
                        x_rdy = w.dma(f"dX{s}", sb_x2[s][:],
                                      x_T[dc * 128:(dc + 1) * 128, c0:c0 + T2],
                                      waits=x2_free[s])
                        zsl = 0 if lane == 0 else 2
                        vy = w.act(lambda e, s=s, dc=dc, zsl=zsl: e.activation(
                            out=sb_zacc[zsl][:], in_=sb_x2[s][:], func=AF.Identity,
                            scale=sb_D1[:, dc:dc + 1]),
                            waits=[(f"dX{s}", x_rdy), zacc0_free[lane],
                                   ("pool", v_d1 if t == 0 and dc == 0 else 0)])
                        vz = ("act", vy)
                        x2_free[s] = [vz]
                    else:
                        vz = vz_carry[dc]
                    lane_state.append({"dc": dc, "s": s, "d_rdy": d_rdy,
                                       "vz": vz, "act_last": 0})
                for ni in range(GSZ):
                    n = n0 + ni
                    for lane in range(2):
                        L = lane_state[lane]
                        dc, s = L["dc"], L["s"]
                        col = dc * NST + n
                        va = w.act(lambda e, s=s, col=col: e.activation(
                            out=sb_a2[s][:], in_=sb_dl2[s][:], func=AF.Exp,
                            scale=sb_Aneg[:, col:col + 1]),
                            waits=[(f"dD{s}", L["d_rdy"]), ("dve", a2_free[s])])
                        L["act_last"] = va
                        L["va"] = va
                    for lane in range(2):
                        L = lane_state[lane]
                        dc, s = L["dc"], L["s"]
                        col = dc * NST + n
                        bslot = (g * GSZ + ni) % 2
                        L["vb"] = w.dma(f"dK{s}{bslot}",
                                        sb_b2[s][:, bslot * T2:(bslot + 1) * T2],
                                        bstash[:, col * l_core + c0:col * l_core + c0 + T2],
                                        waits=[("dve", bfree3[s][bslot])])
                        L["bslot"] = bslot
                    for lane in range(2):
                        L = lane_state[lane]
                        dc, s = L["dc"], L["s"]
                        col = dc * NST + n
                        bslot = L["bslot"]
                        vh = w.dve(lambda e, s=s, col=col, bslot=bslot: e.tensor_tensor_scan(
                            out=sb_h2[s][:], data0=sb_a2[s][:],
                            data1=sb_b2[s][:, bslot * T2:(bslot + 1) * T2],
                            initial=sb_state[:, col:col + 1], op0=ALU.mult, op1=ALU.add),
                            waits=[("act", L["va"]), (f"dK{s}{bslot}", L["vb"]),
                                   ("dve", max(v_state_r2, h2_free[s][1] if h2_free[s][0] == "dve" else 0)),
                                   h2_free[s] if h2_free[s][0] != "dve" else None])
                        vst = w.dve(lambda e, s=s, col=col: e.tensor_copy(
                            out=sb_state[:, col:col + 1], in_=sb_h2[s][:, T2 - 1:T2]),
                            waits=[("dve", vh)])
                        a2_free[s] = vst
                        bfree3[s][bslot] = vh
                        L["vh"] = vh
                    for lane in range(2):
                        L = lane_state[lane]
                        s = L["s"]
                        ep = "dve" if lane == 0 else "pool"
                        pop = w.dve if lane == 0 else w.pool
                        vp = pop(lambda e, s=s, ni=ni: e.tensor_tensor(
                            out=sb_p2[s][:], in0=sb_h2[s][:],
                            in1=sb_repC[:, ni * T2:(ni + 1) * T2], op=ALU.mult),
                            waits=[("dve", L["vh"]), ("act", repC_ready), p2_free[s]])
                        h2_free[s] = (ep, vp)
                        L["vp"] = (ep, vp)
                    for lane in range(2):
                        L = lane_state[lane]
                        dc, s = L["dc"], L["s"]
                        ez = "pool" if lane == 0 else "dve"
                        eop = w.pool if lane == 0 else w.dve
                        zsl = 0 if lane == 0 else 2
                        into_z = ni == GSZ - 1
                        from_z = g > 0 and ni == 0
                        wl = [L["vp"], L["vz"]]
                        if g == 0 and ni == GSZ - 1:
                            wl += zz_free[zb_]
                        vz = (ez, eop(lambda e, s=s, ni=ni, dc=dc, into_z=into_z, zsl=zsl,
                                      from_z=from_z, sbz=sbz: e.tensor_tensor(
                            out=(sbz[:, dc * T2:(dc + 1) * T2] if into_z
                                 else sb_zacc[zsl + (ni + 1) % 2][:]),
                            in0=(sbz[:, dc * T2:(dc + 1) * T2] if from_z
                                 else sb_zacc[zsl + ni % 2][:]),
                            in1=sb_p2[s][:], op=ALU.add),
                            waits=wl))
                        if g == 0 and ni == 0:
                            zacc0_free[lane] = vz
                        p2_free[s] = vz
                        L["vz"] = vz
                for lane in range(2):
                    L = lane_state[lane]
                    dl2_free[L["s"]] = [("act", L["act_last"])]
                    vz_carry[L["dc"]] = L["vz"]
            repC_free = [("dve", w.cnt["dve"]), ("pool", w.cnt["pool"])]
        global_state["vz_fin"] = list(vz_carry)
        return global_state

    def emit_statsrows(t, gs):
        zb_ = t % 2
        sbz = sb_zz[zb_]
        st2 = 0
        for dc in range(DC):
            s = dc % 2
            vzb = w.act(lambda e, dc=dc, sbz=sbz: e.copy(
                out=sb_zb[0][:], in_=sbz[:, dc * T2:(dc + 1) * T2]),
                waits=[gs["vz_fin"][dc], ("pe", zb_free[0])])
            vz2b = w.act(lambda e, dc=dc, sbz=sbz: e.square(
                out=sb_z2b[0][:], in_=sbz[:, dc * T2:(dc + 1) * T2]),
                waits=[("act", vzb), ("pe", zb_free[1])])
            wl = [("act", vz2b)]
            if dc == 0:
                wl += bank_free[2] + bank_free[3] + [("dS", ld["onescol"] if t == 0 else 0)]
            stz = w.pe(lambda e, dc=dc: e.matmul(
                bank(2, 1), sb_onescol[:], sb_zb[0][:],
                start=(dc == 0), stop=(dc == DC - 1), skip_group_check=True),
                waits=wl)
            st2 = w.pe(lambda e, dc=dc: e.matmul(
                bank(3, 1), sb_onescol[:], sb_z2b[0][:],
                start=(dc == 0), stop=(dc == DC - 1), skip_group_check=True),
                waits=[])
            zb_free[0] = stz
            zb_free[1] = st2
        vm = w.dve(lambda e: e.tensor_scalar(
            out=rr(0, 0), in0=bank(2, 1), scalar1=1.0 / D, scalar2=None, op0=ALU.mult),
            waits=[("pe", st2)] + rows_free)
        vq = w.dve(lambda e: e.tensor_scalar(
            out=rr(0, 1), in0=bank(3, 1), scalar1=1.0 / D, scalar2=None, op0=ALU.mult),
            waits=[("dve", vm)])
        vt = w.dve(lambda e: e.tensor_tensor(out=rr(0, 2), in0=rr(0, 0), in1=rr(0, 0), op=ALU.mult),
                   waits=[("dve", vq)])
        vv = w.dve(lambda e: e.tensor_tensor(out=rr(32, 0), in0=rr(0, 1), in1=rr(0, 2), op=ALU.subtract),
                   waits=[("dve", vt)])
        vve = w.dve(lambda e: e.tensor_scalar(out=rr(32, 1), in0=rr(32, 0), scalar1=EPS,
                                              scalar2=None, op0=ALU.add),
                    waits=[("dve", vv)])
        vsq = w.act(lambda e: e.activation(out=rr(32, 2), in_=rr(32, 1), func=AF.Sqrt),
                    waits=[("dve", vve)])
        vr = w.dve(lambda e: e.reciprocal(out=rr(0, 1), in_=rr(32, 2)), waits=[("act", vsq)])
        vmr = w.dve(lambda e: e.tensor_tensor(out=rr(0, 2), in0=rr(0, 0), in1=rr(0, 1), op=ALU.mult),
                    waits=[("dve", vr)])
        bank_free[2] = [("dve", vm)]
        bank_free[3] = [("dve", vq)]
        gs["vr"] = vr
        gs["vmr"] = vmr

    def emit_ffnA(t, gs):
        nonlocal br_free
        x1_dones = gs.setdefault("x1_dones", [0, 0])
        zb_ = t % 2
        sbz = sb_zz[zb_]
        vr, vmr = gs["vr"], gs["vmr"]
        mmr = w.pe(lambda e: e.matmul(bank(4), sb_onesrow[:], rr(0, 1), start=True, stop=True),
                   waits=bank_free[4] + [("dve", vr),
                                         ("dS", ld["onesrow"] if t == 0 else 0)])
        mmm = w.pe(lambda e: e.matmul(bank(5), sb_onesrow[:], rr(0, 2), start=True, stop=True),
                   waits=bank_free[5] + [("dve", vmr)])
        gs["rows_pe"] = mmm
        evr1 = w.act(lambda e: e.copy(out=sb_br[:, 0:T2], in_=bank(4)),
                     waits=[("pe", mmr)] + br_free)
        evm1 = w.act(lambda e: e.copy(out=sb_br[:, T2:2 * T2], in_=bank(5)),
                     waits=[("pe", mmm)])
        bank_free[4] = [("act", evr1)]
        bank_free[5] = [("act", evm1)]
        x1_last = 0
        for dc in range(DC):
            s = dc % 2
            e1 = w.dve(lambda e, dc=dc, s=s, sbz=sbz: e.tensor_tensor(
                out=sb_xo[s][:], in0=sbz[:, dc * T2:(dc + 1) * T2], in1=sb_br[:, 0:T2],
                op=ALU.mult),
                waits=[("act", evr1)] + xo_free[s] + (x1bf_free if dc < 2 else []))
            x1_last = w.dve(lambda e, dc=dc, s=s: e.tensor_tensor(
                out=sb_x1bf[:, dc * T2:(dc + 1) * T2], in0=sb_xo[s][:], in1=sb_br[:, T2:2 * T2],
                op=ALU.subtract),
                waits=[("act", evm1), ("dve", e1)])
            xo_free[s] = [("dve", x1_last)]
            x1_dones[s] = x1_last
        gs["x1_last"] = x1_last
        gu_last = 0
        for m in range(HM):
            g_mm = 0
            u_mm = 0
            for hf in range(2):
                wgr = w.dma(f"dA{hf}", sb_wg[hf][:],
                            wbf_g[:, (m * DC + hf * 8) * 128:(m * DC + hf * 8 + 8) * 128],
                            waits=[("pe", wg_free[hf])])
                wur = w.dma(f"dB{hf}", sb_wu[hf][:],
                            wbf_u[:, (m * DC + hf * 8) * 128:(m * DC + hf * 8 + 8) * 128],
                            waits=[("pe", wu_free[hf])])
                for kc8 in range(8):
                    kc = hf * 8 + kc8
                    wl = []
                    if kc8 == 0:
                        wl = [(f"dA{hf}", wgr), ("dve", x1_last)]
                        if kc == 0:
                            wl += bank_free[4]
                    g_mm = w.pe(lambda e, m=m, kc=kc, hf=hf, kc8=kc8: e.matmul(
                        bank(4), sb_wg[hf][:, kc8 * 128:(kc8 + 1) * 128],
                        sb_x1bf[:, kc * T2:(kc + 1) * T2],
                        start=(kc == 0), stop=(kc == DC - 1)),
                        waits=wl)
                wg_free[hf] = g_mm
            for hf in range(2):
                wur2 = 0
                for kc8 in range(8):
                    kc = hf * 8 + kc8
                    wl = []
                    if kc8 == 0:
                        wl = [(f"dB{hf}", w.cnt[f"dB{hf}"])]
                        if kc == 0:
                            wl += bank_free[5]
                    u_mm = w.pe(lambda e, m=m, kc=kc, hf=hf, kc8=kc8: e.matmul(
                        bank(5), sb_wu[hf][:, kc8 * 128:(kc8 + 1) * 128],
                        sb_x1bf[:, kc * T2:(kc + 1) * T2],
                        start=(kc == 0), stop=(kc == DC - 1)),
                        waits=wl)
                wu_free[hf] = u_mm
            vg = w.act(lambda e, m=m: e.activation(
                out=sb_gt[0][:], in_=bank(4), func=AF.Silu, scale=sb_sg[:, m:m + 1]),
                waits=[("pe", g_mm), ("pool", gt_free[0]),
                       ("dS", ld["sg"] if t == 0 and m == 0 else 0)])
            bank_free[4] = [("act", vg)]
            vup = w.act(lambda e, m=m: e.activation(
                out=sb_ut[0][:], in_=bank(5), func=AF.Identity, scale=sb_su[:, m:m + 1]),
                waits=[("pe", u_mm),
                       ("dS", ld["su"] if t == 0 and m == 0 else 0)])
            bank_free[5] = [("act", vup)]
            wl = [("act", vup)]
            if m < 2:
                wl += gu_free
            gu_last = w.pool(lambda e, m=m: e.tensor_tensor(
                out=sb_gu[:, m * T2:(m + 1) * T2], in0=sb_gt[0][:], in1=sb_ut[0][:],
                op=ALU.mult),
                waits=wl)
            gt_free[0] = gu_last
        gs["gu_last"] = gu_last

    def emit_ffnB(t, gs):
        nonlocal br_free
        x1_dones = gs["x1_dones"]
        zb_ = t % 2
        sbz = sb_zz[zb_]
        gu_last = gs["gu_last"]
        st4 = 0
        z2_fin = [None] * DC
        for dc in range(DC):
            s = dc % 2
            d_mm = 0
            for hf in range(2):
                wdr = w.dma(f"dE{hf}", sb_wd[hf][:],
                            wbf_d[:, (dc * HM + hf * 22) * 128:(dc * HM + hf * 22 + 22) * 128],
                            waits=[("pe", wd_free[hf])])
                for kh22 in range(22):
                    kh = hf * 22 + kh22
                    wl = []
                    if kh22 == 0:
                        wl = [(f"dE{hf}", wdr), ("pool", gu_last)]
                        if kh == 0:
                            wl += bank_free[6]
                    d_mm = w.pe(lambda e, dc=dc, kh=kh, hf=hf, kh22=kh22: e.matmul(
                        bank(6), sb_wd[hf][:, kh22 * 128:(kh22 + 1) * 128],
                        sb_gu[:, kh * T2:(kh + 1) * T2],
                        start=(kh == 0), stop=(kh == HM - 1)),
                        waits=wl)
                wd_free[hf] = d_mm
            s2 = dc % 2
            ev6 = w.act(lambda e, dc=dc, s2=s2: e.activation(
                out=sb_xo[s2][:], in_=bank(6), func=AF.Identity,
                scale=sb_sd[:, dc:dc + 1]),
                waits=[("pe", d_mm), ("dS", ld["sd"] if t == 0 and dc == 0 else 0)] +
                      xo_free[s2])
            bank_free[6] = [("act", ev6)]
            z2 = w.pool(lambda e, dc=dc, s2=s2, sbz=sbz: e.tensor_tensor(
                out=sbz[:, dc * T2:(dc + 1) * T2], in0=sb_xo[s2][:],
                in1=sb_x1bf[:, dc * T2:(dc + 1) * T2], op=ALU.add),
                waits=[("act", ev6), ("dve", gs["x1_last"])])
            xo_free[s2] = [("pool", z2)]
            z2_fin[dc] = ("pool", z2)
            gs["x1bf_consumed"] = z2
            vzb = w.act(lambda e, dc=dc, sbz=sbz: e.copy(
                out=sb_zb[0][:], in_=sbz[:, dc * T2:(dc + 1) * T2]),
                waits=[("pool", z2), ("pe", zb_free[0])])
            vz2b = w.act(lambda e, dc=dc, sbz=sbz: e.square(
                out=sb_z2b[0][:], in_=sbz[:, dc * T2:(dc + 1) * T2]),
                waits=[("act", vzb), ("pe", zb_free[1])])
            wl = [("act", vz2b)]
            if dc == 0:
                wl += bank_free[4] + bank_free[5]
            stz = w.pe(lambda e, dc=dc: e.matmul(
                bank(4, 1), sb_onescol[:], sb_zb[0][:],
                start=(dc == 0), stop=(dc == DC - 1), skip_group_check=True),
                waits=wl)
            st4 = w.pe(lambda e, dc=dc: e.matmul(
                bank(5, 1), sb_onescol[:], sb_z2b[0][:],
                start=(dc == 0), stop=(dc == DC - 1), skip_group_check=True),
                waits=[])
            zb_free[0] = stz
            zb_free[1] = st4
        vm2 = w.dve(lambda e: e.tensor_scalar(
            out=rr(0, 0), in0=bank(4, 1), scalar1=1.0 / D, scalar2=None, op0=ALU.mult),
            waits=[("pe", st4), ("pe", gs["rows_pe"])])
        vq2 = w.dve(lambda e: e.tensor_scalar(
            out=rr(0, 1), in0=bank(5, 1), scalar1=1.0 / D, scalar2=None, op0=ALU.mult),
            waits=[("dve", vm2)])
        vt2 = w.dve(lambda e: e.tensor_tensor(out=rr(0, 2), in0=rr(0, 0), in1=rr(0, 0), op=ALU.mult),
                    waits=[("dve", vq2)])
        vv2 = w.dve(lambda e: e.tensor_tensor(out=rr(32, 0), in0=rr(0, 1), in1=rr(0, 2), op=ALU.subtract),
                    waits=[("dve", vt2)])
        vve2 = w.dve(lambda e: e.tensor_scalar(out=rr(32, 1), in0=rr(32, 0), scalar1=EPS,
                                               scalar2=None, op0=ALU.add),
                     waits=[("dve", vv2)])
        vsq2 = w.act(lambda e: e.activation(out=rr(32, 2), in_=rr(32, 1), func=AF.Sqrt),
                     waits=[("dve", vve2)])
        vr2 = w.dve(lambda e: e.reciprocal(out=rr(0, 1), in_=rr(32, 2)), waits=[("act", vsq2)])
        vmr2 = w.dve(lambda e: e.tensor_tensor(out=rr(0, 2), in0=rr(0, 0), in1=rr(0, 1), op=ALU.mult),
                     waits=[("dve", vr2)])
        bank_free[4] = [("dve", vm2)]
        bank_free[5] = [("dve", vq2)]
        mmr3 = w.pe(lambda e: e.matmul(bank(6), sb_onesrow[:], rr(0, 1), start=True, stop=True),
                    waits=bank_free[6] + [("dve", vr2)])
        mmm3 = w.pe(lambda e: e.matmul(bank(7), sb_onesrow[:], rr(0, 2), start=True, stop=True),
                    waits=bank_free[7] + [("dve", vmr2)])
        evr3 = w.act(lambda e: e.copy(out=sb_br[:, 0:T2], in_=bank(6)),
                     waits=[("pe", mmr3), ("dve", x1_dones[0]), ("dve", x1_dones[1])])
        evm3 = w.act(lambda e: e.copy(out=sb_br[:, T2:2 * T2], in_=bank(7)),
                     waits=[("pe", mmm3)])
        bank_free[6] = [("act", evr3)]
        bank_free[7] = [("act", evm3)]
        o_last = [0, 0]
        c0 = t * T2
        for dc in range(DC):
            s = dc % 2
            e1 = w.dve(lambda e, dc=dc, s=s, sbz=sbz: e.tensor_tensor(
                out=sb_xo[s][:], in0=sbz[:, dc * T2:(dc + 1) * T2], in1=sb_br[:, 0:T2],
                op=ALU.mult),
                waits=[("act", evr3)] + xo_free[s])
            e2 = w.dve(lambda e, dc=dc, s=s: e.tensor_tensor(
                out=sb_xo[2][:], in0=sb_xo[s][:], in1=sb_br[:, T2:2 * T2],
                op=ALU.subtract),
                waits=[("act", evm3), ("dve", e1)] + xo_free[2])
            od = w.dma(f"dO{s}", out_T[dc * 128:(dc + 1) * 128, c0:c0 + T2], sb_xo[2][:],
                       waits=[("dve", e2)])
            xo_free[s] = [("dve", e2)]
            xo_free[2] = [(f"dO{s}", od)]
            o_last[s] = e2
        bank_free[6] = [("dve", max(o_last))]
        bank_free[7] = [("dve", max(o_last))]
        zz_free[zb_] = [("dve", max(o_last)),
                        ("act", w.cnt["act"]), ("pool", w.cnt["pool"])]
        nonlocal_rows = [("pe", mmm3)]
        return nonlocal_rows

    for t in range(nt2):
        gs = emit_scanA(t)
        rep_mm_last = gs["rep_mm_last"]
        if t > 0:
            rows_free = emit_ffnB(t - 1, H[t - 1])
            x1bf_free = [("pool", H[t - 1].get("x1bf_consumed", 0))]
            gu_free = [("pe", w.cnt["pe"])]
        emit_statsrows(t, gs)
        emit_ffnA(t, gs)
        H[t] = gs
    rows_free = emit_ffnB(nt2 - 1, H[nt2 - 1])

    r2_stack.close()

    w.barrier("sync", dma_all())

    with nc.Block() as block:
        @block.sync
        def _(e):
            for f in w.ops["sync"]:
                f(e)

        @block.tensor
        def _(e):
            for f in w.ops["tensor"]:
                f(e)

        @block.scalar
        def _(e):
            for f in w.ops["scalar"]:
                f(e)

        @block.vector
        def _(e):
            for f in w.ops["vector"]:
                f(e)

        @block.gpsimd
        def _(e):
            for f in w.ops["gpsimd"]:
                f(e)

    stack.close()
    return nc


def _host_inputs(L0_attn_x, L0_attn_W_in, L0_attn_W_dt, L0_attn_b_dt, L0_attn_A_log,
                 L0_attn_D, L0_ffn_wq_gate, L0_ffn_s_gate, L0_ffn_wq_up, L0_ffn_s_up,
                 L0_ffn_wq_down, L0_ffn_s_down, l_core):
    bf16 = ml_dtypes.bfloat16
    x = np.asarray(L0_attn_x, dtype=np.float32)
    w_inT = np.ascontiguousarray(np.asarray(L0_attn_W_in, np.float32).T)
    w_dtT = np.ascontiguousarray(np.asarray(L0_attn_W_dt, np.float32).T)
    b_dt_t = np.ascontiguousarray(np.asarray(L0_attn_b_dt, np.float32).reshape(DC, 128).T)
    a_t = np.ascontiguousarray(
        np.asarray(L0_attn_A_log, np.float32).reshape(DC, 128, NST)
        .transpose(1, 0, 2).reshape(128, DC * NST))
    d_t = np.ascontiguousarray(np.asarray(L0_attn_D, np.float32).reshape(DC, 128).T)
    s_g = np.ascontiguousarray(np.asarray(L0_ffn_s_gate, np.float32).reshape(HM, 128).T)
    s_u = np.ascontiguousarray(np.asarray(L0_ffn_s_up, np.float32).reshape(HM, 128).T)
    s_d = np.ascontiguousarray(np.asarray(L0_ffn_s_down, np.float32).reshape(DC, 128).T)

    wq_gate = np.asarray(L0_ffn_wq_gate, np.int32).reshape(HM, 128, DC, 128)
    wq_g = np.ascontiguousarray(wq_gate.transpose(3, 0, 2, 1).reshape(128, HM * DC * 128))
    wq_up = np.asarray(L0_ffn_wq_up, np.int32).reshape(HM, 128, DC, 128)
    wq_u = np.ascontiguousarray(wq_up.transpose(3, 0, 2, 1).reshape(128, HM * DC * 128))
    wq_down = np.asarray(L0_ffn_wq_down, np.int32).reshape(DC, 128, HM, 128)
    wq_d = np.ascontiguousarray(wq_down.transpose(3, 0, 2, 1).reshape(128, DC * HM * 128))

    sel = np.zeros((32, 2 * NST * 128), dtype=bf16)
    for r in range(32):
        sel[r, r * 128:(r + 1) * 128] = bf16(1.0)
    onescol = np.ones((128, 1), dtype=bf16)
    onesrow = np.ones((1, 128), dtype=np.float32)

    in_maps = []
    for c in range(NCORES):
        b, q = divmod(c, 4)
        xs = np.ascontiguousarray(x[b, q * l_core:(q + 1) * l_core, :].T)
        msk = np.zeros((128, NCORES), dtype=np.float32)
        for j in range(4 * b, c):
            msk[:, j] = 1.0
        imsk = np.ascontiguousarray(1.0 - msk)
        in_maps.append({
            "x_T": xs, "w_inT": w_inT, "w_dtT": w_dtT, "b_dt_t": b_dt_t,
            "a_t": a_t, "d_t": d_t, "s_g": s_g, "s_u": s_u, "s_d": s_d,
            "wq_g": wq_g, "wq_u": wq_u, "wq_d": wq_d, "msk": msk, "imsk": imsk,
            "sel": sel, "onescol": onescol, "onesrow": onesrow,
        })
    return in_maps


def kernel(**inputs):
    l_core = 4096
    in_maps = _host_inputs(l_core=l_core, **inputs)
    nc = build_nc(l_core)
    res = run_bass_kernel_spmd(nc, in_maps, core_ids=list(range(NCORES)))
    out = np.empty((2, 16384, D), dtype=np.float32)
    for c in range(NCORES):
        b, q = divmod(c, 4)
        out[b, q * l_core:(q + 1) * l_core, :] = res.results[c]["out_T"].T
    return out
